# version 01
# baseline (speedup 1.0000x reference)
import numpy as np
from contextlib import ExitStack
import concourse.bass as bass
import concourse.mybir as mybir
from concourse.bass_utils import run_bass_kernel_spmd

F32 = mybir.dt.float32
BF16 = mybir.dt.bfloat16
F16 = mybir.dt.float16
AF = mybir.ActivationFunctionType
ALU = mybir.AluOpType
AX = mybir.AxisListType

NC = 8
B = 16
BL = 2
T = 2048
NT = 16
D = 1024
KT = 8
H = 8
E = 5
DH = 64
IC = 256
NICH = T // IC
SC_F = DH ** -0.5
SC_A = D ** -0.5

STOP_AFTER = 3

_CACHE = {}


class Buf:
    __slots__ = ("w", "r")

    def __init__(self, edges=()):
        self.w = list(edges)
        self.r = []


class Sch:
    def __init__(self):
        self.streams = {k: [] for k in ("T", "V", "S", "G", "Y")}
        self.cnt = {k: 0 for k in ("t", "v", "s", "g", "dw", "dy")}
        self.esem = {"T": "t", "V": "v", "S": "s", "G": "g", "Y": None}

    def snap(self):
        return [(k, v) for k, v in self.cnt.items() if v > 0]

    def _waits(self, reads, writes):
        waits = {}
        for b in reads:
            for (s, v) in b.w:
                waits[s] = max(waits.get(s, 0), v)
        for b in writes:
            for (s, v) in b.w:
                waits[s] = max(waits.get(s, 0), v)
            for (s, v) in b.r:
                waits[s] = max(waits.get(s, 0), v)
        return sorted(waits.items())

    def op(self, eng, fn, reads=(), writes=(), dma=None):
        waits = self._waits(reads, writes)
        if dma is not None:
            self.cnt[dma] += 16
            edge = (dma, self.cnt[dma], 16)
        else:
            s = self.esem[eng]
            self.cnt[s] += 1
            edge = (s, self.cnt[s], 1)
        self.streams[eng].append((fn, waits, edge))
        for b in reads:
            b.r.append(edge[:2])
        for b in writes:
            b.w = [edge[:2]]
            b.r = []
        return edge

    def group(self, eng, fns, reads=(), writes=()):
        waits = self._waits(reads, writes)
        s = self.esem[eng]
        self.cnt[s] += 1
        edge = (s, self.cnt[s], 1)
        for i, fn in enumerate(fns):
            self.streams[eng].append(
                (fn, waits if i == 0 else [], edge if i == len(fns) - 1 else None))
        for b in reads:
            b.r.append(edge[:2])
        for b in writes:
            b.w = [edge[:2]]
            b.r = []
        return edge


def build_nc():
    nc = bass.Bass("TRN2", target_bir_lowering=False)

    x_in = nc.dram_tensor("x", [BL, T, D], F32, kind="ExternalInput").ap()
    y_out = nc.dram_tensor("y", [BL, T, D], F32, kind="ExternalOutput").ap()
    aWq = nc.dram_tensor("awq", [D, D], F32, kind="ExternalInput").ap()
    aWk = nc.dram_tensor("awk", [D, D], F32, kind="ExternalInput").ap()
    aWv = nc.dram_tensor("awv", [D, D], F32, kind="ExternalInput").ap()
    aWr = nc.dram_tensor("awr", [D, D], F32, kind="ExternalInput").ap()
    fWq = nc.dram_tensor("fwq", [D, H * E * DH], F32, kind="ExternalInput").ap()
    fWkv = nc.dram_tensor("fwkv", [D, 2 * H * E * DH], F32, kind="ExternalInput").ap()
    fWs = nc.dram_tensor("fws", [D, H * E], F32, kind="ExternalInput").ap()
    fWd = nc.dram_tensor("fwd", [D, H * E], F32, kind="ExternalInput").ap()
    fWo = nc.dram_tensor("fwo", [DH, E * D], F32, kind="ExternalInput").ap()
    qspill = nc.dram_tensor("qspill", [T, D], F16, kind="Internal").ap()

    x_t = x_in.rearrange("b (n p) d -> p (b n) d", p=128)
    y_t = y_out.rearrange("b (n p) d -> p (b n) d", p=128)
    qsp_t = qspill.rearrange("(n p) d -> p n d", p=128)

    ARENA_BYTES = 105472

    with ExitStack() as _es:
        en = _es.enter_context
        xsb = en(nc.sbuf_tensor("xsb", [128, BL * NT, D], F16))
        xnT = en(nc.sbuf_tensor("xnT", [128, KT, T], BF16))
        woT = en(nc.sbuf_tensor("woT", [128, 3, D], BF16))
        wsd = en(nc.sbuf_tensor("wsd", [128, KT, 2 * H * E], BF16))
        identb = en(nc.sbuf_tensor("identb", [128, 128], BF16))
        identf = en(nc.sbuf_tensor("identf", [1, 1], F32))
        onesb = en(nc.sbuf_tensor("onesb", [128, 1], BF16))
        ones1 = en(nc.sbuf_tensor("ones1", [1, 128], F32))
        epsv = en(nc.sbuf_tensor("epsv", [128, 1], F32))
        stats = en(nc.sbuf_tensor("stats", [128, NT, 8], F32))
        arena_t = en(nc.sbuf_tensor("arena", [128, ARENA_BYTES], mybir.dt.uint8))
        PA = en(nc.psum_tensor("PA", [128, 1024], F32))
        PB = en(nc.psum_tensor("PB", [128, 1024], F32))
        PP = en(nc.psum_tensor("PP", [128, 1024], F32))
        PT0 = en(nc.psum_tensor("PT0", [128, 1024], BF16))
        PT1 = en(nc.psum_tensor("PT1", [128, 1024], BF16))
        sem_t = en(nc.semaphore("t"))
        sem_v = en(nc.semaphore("v"))
        sem_s = en(nc.semaphore("s"))
        sem_g = en(nc.semaphore("g"))
        sem_dw = en(nc.semaphore("dw"))
        sem_dy = en(nc.semaphore("dy"))
        block = en(nc.Block())
        S = Sch()
        sems = {"t": sem_t, "v": sem_v, "s": sem_s, "g": sem_g,
                "dw": sem_dw, "dy": sem_dy}
        arena = arena_t.ap()

        def carve(off, shape, dtype):
            nel = 1
            for d_ in shape:
                nel *= d_
            nbytes = nel * mybir.dt.size(dtype)
            assert off % 64 == 0 and off + nbytes <= ARENA_BYTES, (off, nbytes)
            sl = arena[:, off:off + nbytes].bitcast(dtype)
            if len(shape) > 1:
                names = " ".join(f"a{i}" for i in range(len(shape)))
                kw = {f"a{i}": shape[i] for i in range(len(shape))}
                sl = sl.rearrange(f"p ({names}) -> p {names}", **kw)
            return sl

        PTT = (PT0, PT1)
        PX2 = (PA, PB)

        xb = [Buf() for _ in range(BL * NT)]
        xnTb = [Buf() for _ in range(NT)]
        woTb, wsdb = Buf(), Buf()
        constb = Buf()
        statsb = [[Buf() for _ in range(8)] for _ in range(NT)]
        pa = [Buf(), Buf()]
        pb = [Buf(), Buf()]
        pp = [Buf(), Buf()]
        pt = [Buf(), Buf()]
        qspb = [Buf() for _ in range(NT)]

        S.op("G", lambda g: g.memset(onesb[:], 1.0), writes=[constb])
        S.op("G", lambda g: g.memset(ones1[:], 1.0), writes=[constb])
        S.op("G", lambda g: g.memset(epsv[:], 1e-5), writes=[constb])
        S.op("G", lambda g: g.memset(identf[:], 1.0), writes=[constb])
        S.op("G", lambda g: g.memset(identb[:], 0.0), writes=[constb])
        S.op("G", lambda g: g.affine_select(
            out=identb[:], in_=identb[:], compare_op=ALU.not_equal, fill=1.0,
            base=0, pattern=[[-1, 128]], channel_multiplier=1), writes=[constb])

        for i in range(4):
            S.op("G", lambda g, i=i: g.dma_start(
                out=xsb[:, i * 8:(i + 1) * 8, :], in_=x_t[:, i * 8:(i + 1) * 8, :]),
                writes=[xb[j] for j in range(i * 8, (i + 1) * 8)], dma="dw")
        S.op("G", lambda g: g.dma_start(
            out=wsd[:, :, 0:H * E],
            in_=fWs.rearrange("(kt p) m -> p kt m", p=128)), writes=[wsdb], dma="dw")
        S.op("G", lambda g: g.dma_start(
            out=wsd[:, :, H * E:2 * H * E],
            in_=fWd.rearrange("(kt p) m -> p kt m", p=128)), writes=[wsdb], dma="dw")
        for e_ in range(E):
            S.op("G", lambda g, e_=e_: g.dma_start(
                out=woT[(e_ % 2) * 64:(e_ % 2) * 64 + 64, e_ // 2, :],
                in_=fWo[:, e_ * D:(e_ + 1) * D]), writes=[woTb], dma="dw")

        def emit_ln(b):
            xn = carve(0, (2, D), BF16)
            sq = carve(4096, (2, D), BF16)
            snap = S.snap()
            xnb = [Buf(snap), Buf(snap)]
            sqb = [Buf(snap), Buf(snap)]

            for n in range(NT):
                tt = b * NT + n
                S.op("V", lambda v, tt=tt, n=n: v.reduce_sum(
                    stats[:, n, 0:1], xsb[:, tt, :], axis=AX.X),
                    reads=[xb[tt]], writes=[statsb[n][0]])
                S.op("S", lambda s, tt=tt, n=n: s.activation(
                    out=sq[:, n % 2, :], in_=xsb[:, tt, :], func=AF.Square,
                    accum_out=stats[:, n, 1:2]),
                    reads=[xb[tt]], writes=[sqb[n % 2], statsb[n][1]])
            for n in range(NT):
                S.op("V", lambda v, n=n: v.tensor_scalar_mul(
                    stats[:, n, 2:3], stats[:, n, 0:1], 1.0 / D),
                    reads=[statsb[n][0]], writes=[statsb[n][2]])
            for n in range(NT):
                S.op("V", lambda v, n=n: v.tensor_mul(
                    stats[:, n, 3:4], stats[:, n, 2:3], stats[:, n, 2:3]),
                    reads=[statsb[n][2]], writes=[statsb[n][3]])
            for n in range(NT):
                S.op("V", lambda v, n=n: v.scalar_tensor_tensor(
                    out=stats[:, n, 4:5], in0=stats[:, n, 1:2], scalar=1.0 / D,
                    in1=stats[:, n, 3:4], op0=ALU.mult, op1=ALU.subtract),
                    reads=[statsb[n][1], statsb[n][3]], writes=[statsb[n][4]])
            for n in range(NT):
                S.op("S", lambda s, n=n: s.activation(
                    out=stats[:, n, 5:6], in_=stats[:, n, 4:5], func=AF.Ln,
                    bias=epsv[:, :], scale=1.0),
                    reads=[statsb[n][4], constb], writes=[statsb[n][5]])
            for n in range(NT):
                S.op("S", lambda s, n=n: s.activation(
                    out=stats[:, n, 6:7], in_=stats[:, n, 5:6], func=AF.Exp,
                    scale=-0.5), reads=[statsb[n][5]], writes=[statsb[n][6]])
            for n in range(NT):
                S.op("V", lambda v, n=n: v.tensor_mul(
                    stats[:, n, 7:8], stats[:, n, 2:3], stats[:, n, 6:7]),
                    reads=[statsb[n][2], statsb[n][6]], writes=[statsb[n][7]])
            for n in range(NT):
                tt = b * NT + n
                mrsb = stats[:, n, 7:8].broadcast_to([128, D])
                S.op("V", lambda v, tt=tt, n=n, mrsb=mrsb: v.scalar_tensor_tensor(
                    out=xn[:, n % 2, :], in0=xsb[:, tt, :], scalar=stats[:, n, 6:7],
                    in1=mrsb, op0=ALU.mult, op1=ALU.subtract),
                    reads=[xb[tt], statsb[n][6], statsb[n][7]], writes=[xnb[n % 2]])
                for half in range(2):
                    hv = (n * 2 + half) % 2
                    S.group("T", [
                        (lambda te, kt=kt, half=half, n=n, hv=hv: te.transpose(
                            PTT[hv][:, kt * 128:(kt + 1) * 128],
                            xn[:, n % 2, (half * 4 + kt) * 128:(half * 4 + kt + 1) * 128],
                            identb[:]))
                        for kt in range(4)],
                        reads=[xnb[n % 2], constb], writes=[pt[hv]])
                    S.op("S", lambda s, half=half, n=n, hv=hv: s.activation(
                        out=xnT[:, half * 4:(half + 1) * 4, n * 128:(n + 1) * 128],
                        in_=PTT[hv][:, 0:512].rearrange("p (a c) -> p a c", c=128),
                        func=AF.Copy),
                        reads=[pt[hv]], writes=[xnTb[n]])

        def emit_ffn(b, call_idx):
            emit_ln(b)
            ez = carve(8192, (NT, 2 * H * E), BF16)
            wrk = carve(10752, (5, 256), F32)
            ssd = carve(15872, (NT, 2 * H * E), F32)
            med = carve(20992, (NT, 2 * H), F32)
            msk = carve(22016, (NT, 2 * H * E), F32)
            gated = carve(27136, (NT, 2 * H * E), BF16)
            whead = carve(29696, (KT, 3 * E * DH), BF16)
            qkT = carve(45056, (2, 2, T), BF16)
            eT = carve(61440, (NT, 2, IC), BF16)
            vext = carve(77824, (2, 2, NT, 66), BF16)
            gacc = carve(86272, (NT, E * DH), BF16)
            otb = carve(96512, (2, 512), BF16)
            onrm = carve(98560, (2, 2, DH), BF16)
            orcp = carve(99072, (2, 2, 1), F32)
            snap = S.snap()
            ezb, wrkb, ssdb = Buf(snap), Buf(snap), Buf(snap)
            medb_, mskb, gatedb = Buf(snap), Buf(snap), Buf(snap)
            wheadb = Buf(snap)
            qkTb = [Buf(snap), Buf(snap)]
            eTb = [Buf(snap) for _ in range(NT)]
            vextb = [Buf(snap), Buf(snap)]
            gaccb = [[Buf(snap) for _ in range(E)] for _ in range(NT)]
            otbb = [Buf(snap), Buf(snap)]
            onrmb = {(hp, it): Buf(snap) for hp in range(2) for it in range(2)}
            orcpb = {(hp, it): Buf(snap) for hp in range(2) for it in range(2)}

            for n in range(NT):
                ppx = pp[n % 2]
                S.group("T", [
                    (lambda te, kt=kt, n=n: te.matmul(
                        PP[:, (n % 2) * 512:(n % 2) * 512 + 2 * H * E],
                        xnT[:, kt, n * 128:(n + 1) * 128], wsd[:, kt, :],
                        start=(kt == 0), stop=(kt == KT - 1)))
                    for kt in range(KT)],
                    reads=[xnTb[n], wsdb], writes=[ppx])
                S.op("S", lambda s, n=n: s.activation(
                    out=ez[:, n, :], in_=PP[:, (n % 2) * 512:(n % 2) * 512 + 2 * H * E],
                    func=AF.Exp, scale=-1.0), reads=[ppx], writes=[ezb])
            ezf = ez.rearrange("p a c -> p (a c)")
            ssdf = ssd.rearrange("p a c -> p (a c)")
            wrkf = wrk.rearrange("p a c -> p (a c)")
            S.op("V", lambda v: v.tensor_scalar_add(wrkf, ezf, 1.0),
                 reads=[ezb], writes=[wrkb])
            S.op("V", lambda v: v.reciprocal(ssdf, wrkf),
                 reads=[wrkb], writes=[ssdb])
            sv = ssd.rearrange("p t (gh e) -> p t gh e", e=E)
            slc = [sv[:, :, :, i] for i in range(E)]
            a_ = [wrk[:, i, :].rearrange("p (t gh) -> p t gh", t=NT) for i in range(5)]
            m5 = [Buf(snap) for _ in range(5)]
            medv = med
            S.op("V", lambda v: v.tensor_tensor(a_[0], slc[0], slc[1], ALU.min),
                 reads=[ssdb], writes=[m5[0]])
            S.op("V", lambda v: v.tensor_tensor(a_[1], slc[0], slc[1], ALU.max),
                 reads=[ssdb], writes=[m5[1]])
            S.op("V", lambda v: v.tensor_tensor(a_[2], slc[2], slc[3], ALU.min),
                 reads=[ssdb], writes=[m5[2]])
            S.op("V", lambda v: v.tensor_tensor(a_[3], slc[2], slc[3], ALU.max),
                 reads=[ssdb], writes=[m5[3]])
            S.op("V", lambda v: v.tensor_tensor(a_[4], a_[0], a_[2], ALU.max),
                 reads=[m5[0], m5[2]], writes=[m5[4]])
            S.op("V", lambda v: v.tensor_tensor(a_[0], a_[1], a_[3], ALU.min),
                 reads=[m5[1], m5[3]], writes=[m5[0]])
            S.op("V", lambda v: v.tensor_tensor(a_[1], a_[4], a_[0], ALU.min),
                 reads=[m5[4], m5[0]], writes=[m5[1]])
            S.op("V", lambda v: v.tensor_tensor(a_[2], a_[4], a_[0], ALU.max),
                 reads=[m5[4], m5[0]], writes=[m5[2]])
            S.op("V", lambda v: v.tensor_tensor(a_[3], slc[4], a_[2], ALU.min),
                 reads=[ssdb, m5[2]], writes=[m5[3]])
            S.op("V", lambda v: v.tensor_tensor(medv, a_[1], a_[3], ALU.max),
                 reads=[m5[1], m5[3]], writes=[medb_])
            medbx = med.unsqueeze(3).broadcast_to([128, NT, 2 * H, E])
            mskv = msk.rearrange("p t (gh e) -> p t gh e", e=E)
            S.op("V", lambda v: v.tensor_tensor(mskv, sv, medbx, ALU.is_ge),
                 reads=[ssdb, medb_], writes=[mskb])
            S.op("V", lambda v: v.tensor_tensor(
                gated.rearrange("p a c -> p (a c)"), ssdf,
                msk.rearrange("p a c -> p (a c)"), ALU.mult),
                reads=[ssdb, mskb], writes=[gatedb])
            S.op("V", lambda v: v.memset(gacc.rearrange("p a c -> p (a c)"), 0.0),
                 writes=[gaccb[n][e] for n in range(NT) for e in range(E)])
            S.op("V", lambda v: v.memset(vext[:, :, :, :, 64:66], 0.0),
                 writes=[vextb[0], vextb[1]])
            S.op("V", lambda v: v.memset(vext[:, :, :, :, 64:65], 1.0),
                 writes=[vextb[0], vextb[1]])

            qg = carve(10752, (3 * E * DH,), BF16)
            qka = carve(12672, (NT, 256), BF16)
            snap2 = S.snap()
            qgb = Buf(snap2)
            qkab = [Buf(snap2) for _ in range(NT)]

            def dma_whead(h):
                c0 = h * E * DH
                S.op("G", lambda g, c0=c0: g.dma_start(
                    out=whead[:, :, 0:320],
                    in_=fWq[:, c0:c0 + 320].rearrange("(kt p) m -> p kt m", p=128)),
                    writes=[wheadb], dma="dw")
                S.op("G", lambda g, c0=c0: g.dma_start(
                    out=whead[:, :, 320:640],
                    in_=fWkv[:, c0:c0 + 320].rearrange("(kt p) m -> p kt m", p=128)),
                    writes=[wheadb], dma="dw")
                S.op("G", lambda g, c0=c0: g.dma_start(
                    out=whead[:, :, 640:960],
                    in_=fWkv[:, 2560 + c0:2560 + c0 + 320].rearrange(
                        "(kt p) m -> p kt m", p=128)),
                    writes=[wheadb], dma="dw")

            def emit_prep_tiles(h, tiles):
                p_ = h // 2
                pbuf = p_ % 2
                hp = h % 2
                for n in tiles:
                    for c, (lo, hi) in enumerate(((0, 512), (512, 960))):
                        ppx = pp[c]
                        S.group("T", [
                            (lambda te, kt=kt, n=n, lo=lo, hi=hi, c=c: te.matmul(
                                PP[:, c * 512:c * 512 + (hi - lo)],
                                xnT[:, kt, n * 128:(n + 1) * 128],
                                whead[:, kt, lo:hi],
                                start=(kt == 0), stop=(kt == KT - 1)))
                            for kt in range(KT)],
                            reads=[xnTb[n], wheadb], writes=[ppx])
                        for s0, s1, gsel in ((0, 320, 1), (320, 640, 0), (640, 960, 0)):
                            a0_, b0_ = max(lo, s0), min(hi, s1)
                            if a0_ >= b0_:
                                continue
                            ne = (b0_ - a0_) // DH
                            e0 = (a0_ - s0) // DH
                            gbase = (H * E if gsel else 0) + h * E + e0
                            gapb = gated[:, n, gbase:gbase + ne].unsqueeze(2) \
                                .broadcast_to([128, ne, DH])
                            pin = PP[:, c * 512 + (a0_ - lo):c * 512 + (b0_ - lo)] \
                                .rearrange("p (e d) -> p e d", d=DH)
                            qout = qg[:, a0_:b0_].rearrange("p (e d) -> p e d", d=DH)
                            S.op("V", lambda v, pin=pin, gapb=gapb, qout=qout:
                                 v.tensor_tensor(qout, pin, gapb, ALU.mult),
                                 reads=[ppx, gatedb], writes=[qgb])
                    S.op("V", lambda v, n=n, hp=hp: v.tensor_reduce(
                        qka[:, n, hp * 64:hp * 64 + 64],
                        qg[:, 0:320].rearrange("p (e d) -> p d e", d=DH),
                        axis=AX.X, op=ALU.add),
                        reads=[qgb], writes=[qkab[n]])
                    S.op("V", lambda v, n=n, hp=hp: v.tensor_reduce(
                        qka[:, n, 128 + hp * 64:128 + hp * 64 + 64],
                        qg[:, 320:640].rearrange("p (e d) -> p d e", d=DH),
                        axis=AX.X, op=ALU.add),
                        reads=[qgb], writes=[qkab[n]])
                    S.op("V", lambda v, n=n, hp=hp, pbuf=pbuf: v.tensor_reduce(
                        vext[:, pbuf, hp, n, 0:64],
                        qg[:, 640:960].rearrange("p (e d) -> p d e", d=DH),
                        axis=AX.X, op=ALU.add),
                        reads=[qgb], writes=[vextb[pbuf]])
                    if hp == 1:
                        for qk in range(2):
                            hv = (n + qk) % 2
                            S.op("T", lambda te, qk=qk, n=n, hv=hv: te.transpose(
                                PTT[hv][:, 0:128], qka[:, n, qk * 128:(qk + 1) * 128],
                                identb[:]),
                                reads=[qkab[n], constb], writes=[pt[hv]])
                            S.op("S", lambda s, qk=qk, n=n, pbuf=pbuf, hv=hv:
                                 s.activation(
                                     out=qkT[:, pbuf, qk, n * 128:(n + 1) * 128],
                                     in_=PTT[hv][:, 0:128], func=AF.Copy),
                                 reads=[pt[hv]], writes=[qkTb[pbuf]])

            def emit_o(p_, ic, j):
                pbuf = p_ % 2
                for hp in range(2):
                    S.op("T", lambda te, j=j, hp=hp, ic=ic, pbuf=pbuf: te.matmul(
                        PB[0:65, hp * 512 + (ic % 2) * 256:hp * 512 + (ic % 2) * 256 + 256],
                        vext[:, pbuf, hp, j, 0:65], eT[:, j, hp, :],
                        start=(j == 0), stop=(j == NT - 1), skip_group_check=True),
                        reads=[vextb[pbuf], eTb[j]], writes=[pb[hp]])

            def emit_attn_chunk(p_, ic):
                pbuf = p_ % 2
                i0 = ic * IC
                for j in range(NT):
                    half = j % 2
                    pax = pa[half]
                    for hp in range(2):
                        S.op("T", lambda te, j=j, hp=hp, half=half, i0=i0, pbuf=pbuf:
                             te.matmul(
                                 PA[:, half * 512 + hp * IC:half * 512 + (hp + 1) * IC],
                                 qkT[64 * hp:64 * (hp + 1), pbuf, 1, j * 128:(j + 1) * 128],
                                 qkT[64 * hp:64 * (hp + 1), pbuf, 0, i0:i0 + IC],
                                 start=True, stop=True, tile_position=(64 * hp, 0)),
                             reads=[qkTb[pbuf]], writes=[pax])
                    S.op("S", lambda s, j=j, half=half: s.activation(
                        out=eT[:, j, :, :].rearrange("p a c -> p (a c)"),
                        in_=PA[:, half * 512:half * 512 + 512], func=AF.Exp,
                        scale=SC_F), reads=[pax], writes=[eTb[j]])
                    if j >= 1:
                        emit_o(p_, ic, j - 1)
                emit_o(p_, ic, NT - 1)
                slot = ic % 2
                S.op("S", lambda s, ic=ic, slot=slot: s.activation(
                    out=otb[0:65, slot, :].rearrange("p (a c) -> p a c", a=2),
                    in_=PB[0:65, :].rearrange("p (hp q) -> p hp q", hp=2)
                        [:, :, (ic % 2) * 256:(ic % 2) * 256 + 256],
                    func=AF.Copy),
                    reads=[pb[0], pb[1]], writes=[otbb[slot]])
                for hp in range(2):
                    for it in range(IC // 128):
                        n = ic * 2 + it
                        hv = (hp * 2 + it) % 2
                        S.op("T", lambda te, hp=hp, it=it, slot=slot, hv=hv:
                             te.transpose(
                                 PTT[hv][:, 0:65],
                                 otb[0:65, slot, hp * IC + it * 128:hp * IC + it * 128 + 128],
                                 identb[0:65, 0:65]),
                             reads=[otbb[slot], constb], writes=[pt[hv]])
                        S.op("V", lambda v, hp=hp, it=it, hv=hv: v.reciprocal(
                            orcp[:, hp, it, :], PTT[hv][:, 64:65]),
                            reads=[pt[hv]], writes=[orcpb[(hp, it)]])
                        S.op("V", lambda v, hp=hp, it=it, hv=hv: v.tensor_scalar_mul(
                            onrm[:, hp, it, :], PTT[hv][:, 0:64],
                            orcp[:, hp, it, 0:1]),
                            reads=[pt[hv], orcpb[(hp, it)]], writes=[onrmb[(hp, it)]])
                        h = p_ * 2 + hp
                        for e in range(E):
                            S.op("V", lambda v, n=n, e=e, h=h, hp=hp, it=it:
                                 v.scalar_tensor_tensor(
                                     out=gacc[:, n, e * DH:(e + 1) * DH],
                                     in0=onrm[:, hp, it, :],
                                     scalar=msk[:, n, H * E + h * E + e:H * E + h * E + e + 1],
                                     in1=gacc[:, n, e * DH:(e + 1) * DH],
                                     op0=ALU.mult, op1=ALU.add),
                                 reads=[onrmb[(hp, it)], mskb, gaccb[n][e]],
                                 writes=[gaccb[n][e]])

            dma_whead(0)
            emit_prep_tiles(0, range(NT))
            dma_whead(1)
            emit_prep_tiles(1, range(NT))
            for p_ in range(4):
                for ic in range(NICH):
                    emit_attn_chunk(p_, ic)
                    if p_ < 3:
                        h2 = 2 * (p_ + 1)
                        if ic == 0:
                            dma_whead(h2)
                        if ic < 4:
                            emit_prep_tiles(h2, range(ic * 4, ic * 4 + 4))
                        if ic == 3:
                            dma_whead(h2 + 1)
                        if ic >= 4:
                            emit_prep_tiles(h2 + 1, range((ic - 4) * 4, (ic - 4) * 4 + 4))

            out_sb = carve(8192, (2, D), F32)
            gt = carve(16448, (2, 3, 128), BF16)
            snap3 = S.snap()
            outb = [Buf(snap3), Buf(snap3)]
            gtb = [Buf(snap3), Buf(snap3)]
            last = (call_idx == 2)
            for n in range(NT):
                slot = n % 2
                for kt in range(3):
                    w_ = 128 if kt < 2 else 64
                    hv = kt % 2
                    S.op("T", lambda te, n=n, kt=kt, w_=w_, hv=hv: te.transpose(
                        PTT[hv][0:w_, 0:128], gacc[:, n, kt * 128:kt * 128 + w_],
                        identb[:]),
                        reads=[gaccb[n][ee] for ee in range(E)] + [constb],
                        writes=[pt[hv]])
                    S.op("S", lambda s, n=n, kt=kt, w_=w_, slot=slot, hv=hv:
                         s.activation(out=gt[0:w_, slot, kt, :],
                                      in_=PTT[hv][0:w_, 0:128], func=AF.Copy),
                         reads=[pt[hv]], writes=[gtb[slot]])
                pax = (pa, pb)[n % 2]
                PX = PX2[n % 2]
                for c in range(2):
                    S.group("T", [
                        (lambda te, kt=kt, c=c, slot=slot, PX=PX: te.matmul(
                            PX[:, c * 512:(c + 1) * 512],
                            gt[0:(128 if kt < 2 else 64), slot, kt, :],
                            woT[0:(128 if kt < 2 else 64), kt, c * 512:(c + 1) * 512],
                            start=(kt == 0), stop=(kt == 2),
                            tile_position=((0, 0) if kt == 2 else None)))
                        for kt in range(3)],
                        reads=[gtb[slot], woTb], writes=[pax[c]])
                tt = b * NT + n
                if last:
                    S.op("V", lambda v, tt=tt, slot=slot, PX=PX: v.scalar_tensor_tensor(
                        out=out_sb[:, slot, :], in0=PX[:, :], scalar=0.5,
                        in1=xsb[:, tt, :], op0=ALU.mult, op1=ALU.add),
                        reads=[pax[0], pax[1], xb[tt]], writes=[outb[slot]])
                    S.op("Y", lambda y, tt=tt, slot=slot: y.dma_start(
                        out=y_t[:, tt, :], in_=out_sb[:, slot, :]),
                        reads=[outb[slot]], dma="dy")
                else:
                    S.op("V", lambda v, tt=tt, PX=PX: v.scalar_tensor_tensor(
                        out=xsb[:, tt, :], in0=PX[:, :], scalar=0.5,
                        in1=xsb[:, tt, :], op0=ALU.mult, op1=ALU.add),
                        reads=[pax[0], pax[1], xb[tt]], writes=[xb[tt]])

        def emit_attn_sub(b):
            emit_ln(b)
            aw = carve(8192, (2, KT, D), BF16)
            uT = carve(40960, (KT, T), BF16)
            qa = carve(73728, (2, D), BF16)
            psb = carve(77824, (2, D), F32)
            pexp = carve(86016, (2, D), BF16)
            q16 = carve(90112, (2, D), F16)
            gqb_s = carve(94208, (D,), F32)
            gq_s = carve(98304, (D,), F32)
            gkT_s = carve(102400, (KT,), F32)
            rcp = carve(102464, (2, 1), F32)
            snap = S.snap()
            awb = [Buf(snap), Buf(snap)]
            uTb = Buf(snap)
            qab = [Buf(snap), Buf(snap)]
            psbb = [Buf(snap), Buf(snap)]
            pexpb = [Buf(snap), Buf(snap)]
            q16b = [Buf(snap), Buf(snap)]
            gqbb = Buf(snap)
            gkTb = Buf(snap)
            gqsb = Buf(snap)
            rcpb = [Buf(snap), Buf(snap)]

            def dma_aw(w, slot):
                S.op("G", lambda g, w=w, slot=slot: g.dma_start(
                    out=aw[:, slot, :, :],
                    in_=w.rearrange("(kt p) m -> p kt m", p=128)),
                    writes=[awb[slot]], dma="dw")

            dma_aw(aWq, 0)
            dma_aw(aWk, 1)

            for n in range(NT):
                pax = (pa, pb)[n % 2]
                PX = PX2[n % 2]
                slot = n % 2
                for c in range(2):
                    S.group("T", [
                        (lambda te, kt=kt, n=n, c=c, PX=PX: te.matmul(
                            PX[:, c * 512:(c + 1) * 512],
                            xnT[:, kt, n * 128:(n + 1) * 128],
                            aw[:, 0, kt, c * 512:(c + 1) * 512],
                            start=(kt == 0), stop=(kt == KT - 1)))
                        for kt in range(KT)],
                        reads=[xnTb[n], awb[0]], writes=[pax[c]])
                S.op("S", lambda s, n=n, slot=slot, PX=PX: s.activation(
                    out=pexp[:, slot, :], in_=PX[:, :], func=AF.Exp, scale=SC_A,
                    accum_out=stats[:, n, 0:1]),
                    reads=[pax[0], pax[1]], writes=[pexpb[slot], statsb[n][0]])
                S.op("S", lambda s, n=n, slot=slot, PX=PX: s.activation(
                    out=q16[:, slot, :], in_=PX[:, :], func=AF.Copy),
                    reads=[pax[0], pax[1]], writes=[q16b[slot]])
                S.op("Y", lambda y, n=n, slot=slot: y.dma_start(
                    out=qsp_t[:, n, :], in_=q16[:, slot, :]),
                    reads=[q16b[slot]], writes=[qspb[n]], dma="dy")
                S.op("V", lambda v, n=n, slot=slot: v.reciprocal(
                    rcp[:, slot, :], stats[:, n, 0:1]),
                    reads=[statsb[n][0]], writes=[rcpb[slot]])
                S.op("V", lambda v, n=n, slot=slot, PX=PX: v.scalar_tensor_tensor(
                    out=qa[:, slot, :], in0=pexp[:, slot, :], scalar=rcp[:, slot, 0:1],
                    in1=PX[:, :], op0=ALU.mult, op1=ALU.mult),
                    reads=[pexpb[slot], rcpb[slot], pax[0], pax[1]],
                    writes=[qab[slot]])
                for c in range(2):
                    S.op("T", lambda te, n=n, c=c, slot=slot: te.matmul(
                        PP[0:1, c * 512:(c + 1) * 512], onesb[:, :],
                        qa[:, slot, c * 512:(c + 1) * 512],
                        start=(n == 0), stop=(n == NT - 1), skip_group_check=True),
                        reads=[qab[slot], constb], writes=[pp[c]])
            S.op("S", lambda s: s.activation(out=gq_s[0:1, :], in_=PP[0:1, :],
                                             func=AF.Copy),
                 reads=[pp[0], pp[1]], writes=[gqsb])
            for c in range(2):
                S.op("T", lambda te, c=c: te.matmul(
                    PA[:, c * 512:(c + 1) * 512], ones1[:, :],
                    gq_s[0:1, c * 512:(c + 1) * 512], start=True, stop=True),
                    reads=[gqsb, constb], writes=[pa[c]])
            S.op("V", lambda v: v.tensor_copy(gqb_s[:, :], PA[:, :]),
                 reads=[pa[0], pa[1]], writes=[gqbb])
            dma_aw(aWv, 0)

            for n in range(NT):
                pax = (pa, pb)[n % 2]
                PX = PX2[n % 2]
                slot = n % 2
                for c in range(2):
                    S.group("T", [
                        (lambda te, kt=kt, n=n, c=c, PX=PX: te.matmul(
                            PX[:, c * 512:(c + 1) * 512],
                            xnT[:, kt, n * 128:(n + 1) * 128],
                            aw[:, 1, kt, c * 512:(c + 1) * 512],
                            start=(kt == 0), stop=(kt == KT - 1)))
                        for kt in range(KT)],
                        reads=[xnTb[n], awb[1]], writes=[pax[c]])
                S.op("V", lambda v, slot=slot, PX=PX: v.tensor_tensor(
                    psb[:, slot, :], PX[:, :], gqb_s[:, :], ALU.mult),
                    reads=[pax[0], pax[1], gqbb], writes=[psbb[slot]])
                S.op("S", lambda s, n=n, slot=slot: s.activation(
                    out=pexp[:, slot, :], in_=psb[:, slot, :], func=AF.Exp,
                    scale=SC_A, accum_out=stats[:, n, 1:2]),
                    reads=[psbb[slot]], writes=[pexpb[slot], statsb[n][1]])
                S.op("V", lambda v, n=n, slot=slot: v.reciprocal(
                    rcp[:, slot, :], stats[:, n, 1:2]),
                    reads=[statsb[n][1]], writes=[rcpb[slot]])
                S.op("V", lambda v, slot=slot: v.scalar_tensor_tensor(
                    out=qa[:, slot, :], in0=pexp[:, slot, :], scalar=rcp[:, slot, 0:1],
                    in1=psb[:, slot, :], op0=ALU.mult, op1=ALU.mult),
                    reads=[pexpb[slot], rcpb[slot], psbb[slot]], writes=[qab[slot]])
                for c in range(2):
                    S.op("T", lambda te, n=n, c=c, slot=slot: te.matmul(
                        PP[0:1, c * 512:(c + 1) * 512], onesb[:, :],
                        qa[:, slot, c * 512:(c + 1) * 512],
                        start=(n == 0), stop=(n == NT - 1), skip_group_check=True),
                        reads=[qab[slot], constb], writes=[pp[c]])
            S.op("S", lambda s: s.activation(out=gq_s[0:1, :], in_=PP[0:1, :],
                                             func=AF.Copy),
                 reads=[pp[0], pp[1]], writes=[gqsb])
            for kt in range(KT):
                S.op("T", lambda te, kt=kt: te.transpose(
                    PA[:, kt:kt + 1], gq_s[0:1, kt * 128:(kt + 1) * 128],
                    identf[:, :]), reads=[gqsb, constb], writes=[pa[0]])
            S.op("V", lambda v: v.tensor_copy(gkT_s[:, :], PA[:, 0:KT]),
                 reads=[pa[0]], writes=[gkTb])
            dma_aw(aWr, 1)

            for ft in range(KT):
                for ch in range(4):
                    idx = ft * 4 + ch
                    bank = ((pa, pb)[(idx // 2) % 2])[idx % 2]
                    PX = PX2[(idx // 2) % 2]
                    S.group("T", [
                        (lambda te, kt=kt, ft=ft, ch=ch, idx=idx, PX=PX: te.matmul(
                            PX[:, (idx % 2) * 512:(idx % 2) * 512 + 512],
                            aw[:, 0, kt, ft * 128:(ft + 1) * 128],
                            xnT[:, kt, ch * 512:(ch + 1) * 512],
                            start=(kt == 0), stop=(kt == KT - 1)))
                        for kt in range(KT)],
                        reads=[awb[0]] + [xnTb[4 * ch + q] for q in range(4)],
                        writes=[bank])
                    S.op("V", lambda v, ft=ft, ch=ch, idx=idx, PX=PX: v.tensor_scalar_mul(
                        uT[:, ft, ch * 512:(ch + 1) * 512],
                        PX[:, (idx % 2) * 512:(idx % 2) * 512 + 512],
                        gkT_s[:, ft:ft + 1]),
                        reads=[bank, gkTb], writes=[uTb])

            for n in range(NT):
                slot = n % 2
                S.op("Y", lambda y, n=n, slot=slot: y.dma_start(
                    out=q16[:, slot, :], in_=qsp_t[:, n, :]),
                    reads=[qspb[n]], writes=[q16b[slot]], dma="dy")
                pax = (pa, pb)[n % 2]
                PX = PX2[n % 2]
                for c in range(2):
                    S.group("T", [
                        (lambda te, kt=kt, n=n, c=c, PX=PX: te.matmul(
                            PX[:, c * 512:(c + 1) * 512],
                            uT[:, kt, n * 128:(n + 1) * 128],
                            aw[:, 1, kt, c * 512:(c + 1) * 512],
                            start=(kt == 0), stop=(kt == KT - 1)))
                        for kt in range(KT)],
                        reads=[uTb, awb[1]], writes=[pax[c]])
                tt = b * NT + n
                S.op("V", lambda v, slot=slot, PX=PX: v.scalar_tensor_tensor(
                    out=psb[:, slot, :], in0=PX[:, :], scalar=1.0,
                    in1=q16[:, slot, :], op0=ALU.mult, op1=ALU.add),
                    reads=[pax[0], pax[1], q16b[slot]], writes=[psbb[slot]])
                S.op("V", lambda v, tt=tt, slot=slot: v.tensor_tensor(
                    xsb[:, tt, :], psb[:, slot, :], xsb[:, tt, :], ALU.add),
                    reads=[psbb[slot], xb[tt]], writes=[xb[tt]])

        for call in range(min(3, STOP_AFTER)):
            for b in range(BL):
                if call == 1:
                    emit_attn_sub(b)
                else:
                    emit_ffn(b, call)

        if STOP_AFTER < 3:
            for i in range(4):
                S.op("G", lambda g, i=i: g.dma_start(
                    out=y_t[:, i * 8:(i + 1) * 8, :],
                    in_=xsb[:, i * 8:(i + 1) * 8, :]),
                    reads=[xb[j] for j in range(i * 8, (i + 1) * 8)], dma="dw")

        n_ops = {k: len(v) for k, v in S.streams.items()}
        n_ops["sem"] = dict(S.cnt)

        def replay(eng_key):
            def run(e):
                seen = {}
                for (fn, waits, edge) in S.streams[eng_key]:
                    for (sname, val) in waits:
                        if seen.get(sname, 0) < val:
                            e.wait_ge(sems[sname], val)
                            seen[sname] = val
                    ins = fn(e)
                    if edge is not None:
                        ins.then_inc(sems[edge[0]], edge[2])
                if eng_key == "Y" and S.cnt["dy"] > 0:
                    e.wait_ge(sems["dy"], S.cnt["dy"])
                if eng_key == "G" and STOP_AFTER < 3:
                    e.wait_ge(sems["dw"], S.cnt["dw"])
            return run

        with nc.allow_low_precision(reason="bf16/f16 stores of fp32-computed values"):
            block.tensor(replay("T"))
            block.vector(replay("V"))
            block.scalar(replay("S"))
            block.gpsimd(replay("G"))
            block.sync(replay("Y"))

    nc._n_ops = n_ops
    return nc


def kernel(L0_attn_x, L0_attn_Wq, L0_attn_Wk, L0_attn_Wv, L0_attn_Wr,
           L0_ffn_Wq, L0_ffn_Wkv, L0_ffn_Ws, L0_ffn_Wd, L0_ffn_Wo):
    if "nc" not in _CACHE:
        _CACHE["nc"] = build_nc()
    nc = _CACHE["nc"]
    x = np.ascontiguousarray(L0_attn_x, dtype=np.float32)
    shards = x.reshape(NC, BL, T, D)
    w = {
        "awq": np.ascontiguousarray(L0_attn_Wq, dtype=np.float32),
        "awk": np.ascontiguousarray(L0_attn_Wk, dtype=np.float32),
        "awv": np.ascontiguousarray(L0_attn_Wv, dtype=np.float32),
        "awr": np.ascontiguousarray(L0_attn_Wr, dtype=np.float32),
        "fwq": np.ascontiguousarray(L0_ffn_Wq, dtype=np.float32),
        "fwkv": np.ascontiguousarray(L0_ffn_Wkv, dtype=np.float32),
        "fws": np.ascontiguousarray(L0_ffn_Ws, dtype=np.float32),
        "fwd": np.ascontiguousarray(L0_ffn_Wd, dtype=np.float32),
        "fwo": np.ascontiguousarray(L0_ffn_Wo, dtype=np.float32),
    }
    in_maps = [dict(w, x=np.ascontiguousarray(shards[i])) for i in range(NC)]
    res = run_bass_kernel_spmd(nc, in_maps, core_ids=list(range(NC)))
    out = np.concatenate([r["y"] for r in res.results], axis=0)
    return out.reshape(B, T, D).astype(np.float32)


# version 03
# speedup vs baseline: 1.0564x; 1.0564x over previous
import numpy as np
from contextlib import ExitStack
import concourse.bass as bass
import concourse.mybir as mybir
from concourse.bass_utils import run_bass_kernel_spmd

F32 = mybir.dt.float32
BF16 = mybir.dt.bfloat16
F16 = mybir.dt.float16
AF = mybir.ActivationFunctionType
ALU = mybir.AluOpType
AX = mybir.AxisListType

NC = 8
B = 16
BL = 2
T = 2048
NT = 16
D = 1024
KT = 8
H = 8
E = 5
DH = 64
IC = 256
NICH = T // IC
SC_F = DH ** -0.5
SC_A = D ** -0.5

STOP_AFTER = 3

_CACHE = {}


class Buf:
    __slots__ = ("w", "r")

    def __init__(self, edges=()):
        self.w = list(edges)
        self.r = []


class Sch:
    def __init__(self):
        self.streams = {k: [] for k in ("T", "V", "S", "G", "Y")}
        self.cnt = {k: 0 for k in ("t", "v", "s", "g", "dw", "dy")}
        self.esem = {"T": "t", "V": "v", "S": "s", "G": "g", "Y": None}

    def snap(self):
        return [(k, v) for k, v in self.cnt.items() if v > 0]

    def _waits(self, reads, writes):
        waits = {}
        for b in reads:
            for (s, v) in b.w:
                waits[s] = max(waits.get(s, 0), v)
        for b in writes:
            for (s, v) in b.w:
                waits[s] = max(waits.get(s, 0), v)
            for (s, v) in b.r:
                waits[s] = max(waits.get(s, 0), v)
        return sorted(waits.items())

    def op(self, eng, fn, reads=(), writes=(), dma=None):
        waits = self._waits(reads, writes)
        if dma is not None:
            self.cnt[dma] += 16
            edge = (dma, self.cnt[dma], 16)
        else:
            s = self.esem[eng]
            self.cnt[s] += 1
            edge = (s, self.cnt[s], 1)
        self.streams[eng].append((fn, waits, edge))
        for b in reads:
            b.r.append(edge[:2])
        for b in writes:
            b.w = [edge[:2]]
            b.r = []
        return edge

    def group(self, eng, fns, reads=(), writes=()):
        waits = self._waits(reads, writes)
        s = self.esem[eng]
        self.cnt[s] += 1
        edge = (s, self.cnt[s], 1)
        for i, fn in enumerate(fns):
            self.streams[eng].append(
                (fn, waits if i == 0 else [], edge if i == len(fns) - 1 else None))
        for b in reads:
            b.r.append(edge[:2])
        for b in writes:
            b.w = [edge[:2]]
            b.r = []
        return edge


def build_nc():
    nc = bass.Bass("TRN2", target_bir_lowering=False)

    x_in = nc.dram_tensor("x", [BL, T, D], F32, kind="ExternalInput").ap()
    y_out = nc.dram_tensor("y", [BL, T, D], F32, kind="ExternalOutput").ap()
    aWq = nc.dram_tensor("awq", [D, D], F32, kind="ExternalInput").ap()
    aWk = nc.dram_tensor("awk", [D, D], F32, kind="ExternalInput").ap()
    aWv = nc.dram_tensor("awv", [D, D], F32, kind="ExternalInput").ap()
    aWr = nc.dram_tensor("awr", [D, D], F32, kind="ExternalInput").ap()
    fWq = nc.dram_tensor("fwq", [D, H * E * DH], F32, kind="ExternalInput").ap()
    fWkv = nc.dram_tensor("fwkv", [D, 2 * H * E * DH], F32, kind="ExternalInput").ap()
    fWs = nc.dram_tensor("fws", [D, H * E], F32, kind="ExternalInput").ap()
    fWd = nc.dram_tensor("fwd", [D, H * E], F32, kind="ExternalInput").ap()
    fWo = nc.dram_tensor("fwo", [DH, E * D], F32, kind="ExternalInput").ap()
    qspill = nc.dram_tensor("qspill", [T, D], F16, kind="Internal").ap()

    x_t = x_in.rearrange("b (n p) d -> p (b n) d", p=128)
    y_t = y_out.rearrange("b (n p) d -> p (b n) d", p=128)
    qsp_t = qspill.rearrange("(n p) d -> p n d", p=128)

    ARENA_BYTES = 105472

    with ExitStack() as _es:
        en = _es.enter_context
        xsb = en(nc.sbuf_tensor("xsb", [128, BL * NT, D], F16))
        xnT = en(nc.sbuf_tensor("xnT", [128, KT, T], BF16))
        woT = en(nc.sbuf_tensor("woT", [128, 3, D], BF16))
        wsd = en(nc.sbuf_tensor("wsd", [128, KT, 2 * H * E], BF16))
        identb = en(nc.sbuf_tensor("identb", [128, 128], BF16))
        identf = en(nc.sbuf_tensor("identf", [1, 1], F32))
        onesb = en(nc.sbuf_tensor("onesb", [128, 1], BF16))
        ones1 = en(nc.sbuf_tensor("ones1", [1, 128], F32))
        epsv = en(nc.sbuf_tensor("epsv", [128, 1], F32))
        stats = en(nc.sbuf_tensor("stats", [128, NT, 8], F32))
        arena_t = en(nc.sbuf_tensor("arena", [128, ARENA_BYTES], mybir.dt.uint8))
        PA = en(nc.psum_tensor("PA", [128, 1024], F32))
        PB = en(nc.psum_tensor("PB", [128, 1024], F32))
        PP = en(nc.psum_tensor("PP", [128, 1024], F32))
        PT0 = en(nc.psum_tensor("PT0", [128, 1024], BF16))
        PT1 = en(nc.psum_tensor("PT1", [128, 1024], BF16))
        sem_t = en(nc.semaphore("t"))
        sem_v = en(nc.semaphore("v"))
        sem_s = en(nc.semaphore("s"))
        sem_g = en(nc.semaphore("g"))
        sem_dw = en(nc.semaphore("dw"))
        sem_dy = en(nc.semaphore("dy"))
        block = en(nc.Block())
        S = Sch()
        sems = {"t": sem_t, "v": sem_v, "s": sem_s, "g": sem_g,
                "dw": sem_dw, "dy": sem_dy}
        arena = arena_t.ap()

        def carve(off, shape, dtype):
            nel = 1
            for d_ in shape:
                nel *= d_
            nbytes = nel * mybir.dt.size(dtype)
            assert off % 64 == 0 and off + nbytes <= ARENA_BYTES, (off, nbytes)
            sl = arena[:, off:off + nbytes].bitcast(dtype)
            if len(shape) > 1:
                names = " ".join(f"a{i}" for i in range(len(shape)))
                kw = {f"a{i}": shape[i] for i in range(len(shape))}
                sl = sl.rearrange(f"p ({names}) -> p {names}", **kw)
            return sl

        PTT = (PT0, PT1)
        PX2 = (PA, PB)

        xb = [Buf() for _ in range(BL * NT)]
        xnTb = [Buf() for _ in range(NT)]
        woTb, wsdb = Buf(), Buf()
        constb = Buf()
        statsb = [[Buf() for _ in range(8)] for _ in range(NT)]
        pa = [Buf(), Buf()]
        pb = [Buf(), Buf()]
        pp = [Buf(), Buf()]
        pt = [Buf(), Buf()]
        qspb = [Buf() for _ in range(NT)]

        S.op("G", lambda g: g.memset(onesb[:], 1.0), writes=[constb])
        S.op("G", lambda g: g.memset(ones1[:], 1.0), writes=[constb])
        S.op("G", lambda g: g.memset(epsv[:], 1e-5), writes=[constb])
        S.op("G", lambda g: g.memset(identf[:], 1.0), writes=[constb])
        S.op("G", lambda g: g.memset(identb[:], 0.0), writes=[constb])
        S.op("G", lambda g: g.affine_select(
            out=identb[:], in_=identb[:], compare_op=ALU.not_equal, fill=1.0,
            base=0, pattern=[[-1, 128]], channel_multiplier=1), writes=[constb])

        for i in range(4):
            S.op("G", lambda g, i=i: g.dma_start(
                out=xsb[:, i * 8:(i + 1) * 8, :], in_=x_t[:, i * 8:(i + 1) * 8, :]),
                writes=[xb[j] for j in range(i * 8, (i + 1) * 8)], dma="dw")
        S.op("G", lambda g: g.dma_start(
            out=wsd[:, :, 0:H * E],
            in_=fWs.rearrange("(kt p) m -> p kt m", p=128)), writes=[wsdb], dma="dw")
        S.op("G", lambda g: g.dma_start(
            out=wsd[:, :, H * E:2 * H * E],
            in_=fWd.rearrange("(kt p) m -> p kt m", p=128)), writes=[wsdb], dma="dw")
        for e_ in range(E):
            S.op("G", lambda g, e_=e_: g.dma_start(
                out=woT[(e_ % 2) * 64:(e_ % 2) * 64 + 64, e_ // 2, :],
                in_=fWo[:, e_ * D:(e_ + 1) * D]), writes=[woTb], dma="dw")

        def emit_ln(b):
            xn = carve(0, (2, D), BF16)
            sq = carve(4096, (2, D), BF16)
            snap = S.snap()
            xnb = [Buf(snap), Buf(snap)]
            sqb = [Buf(snap), Buf(snap)]

            for n in range(NT):
                tt = b * NT + n
                S.op("V", lambda v, tt=tt, n=n: v.reduce_sum(
                    stats[:, n, 0:1], xsb[:, tt, :], axis=AX.X),
                    reads=[xb[tt]], writes=[statsb[n][0]])
                S.op("S", lambda s, tt=tt, n=n: s.activation(
                    out=sq[:, n % 2, :], in_=xsb[:, tt, :], func=AF.Square,
                    accum_out=stats[:, n, 1:2]),
                    reads=[xb[tt]], writes=[sqb[n % 2], statsb[n][1]])
            for n in range(NT):
                S.op("V", lambda v, n=n: v.tensor_scalar_mul(
                    stats[:, n, 2:3], stats[:, n, 0:1], 1.0 / D),
                    reads=[statsb[n][0]], writes=[statsb[n][2]])
            for n in range(NT):
                S.op("V", lambda v, n=n: v.tensor_mul(
                    stats[:, n, 3:4], stats[:, n, 2:3], stats[:, n, 2:3]),
                    reads=[statsb[n][2]], writes=[statsb[n][3]])
            for n in range(NT):
                S.op("V", lambda v, n=n: v.scalar_tensor_tensor(
                    out=stats[:, n, 4:5], in0=stats[:, n, 1:2], scalar=1.0 / D,
                    in1=stats[:, n, 3:4], op0=ALU.mult, op1=ALU.subtract),
                    reads=[statsb[n][1], statsb[n][3]], writes=[statsb[n][4]])
            for n in range(NT):
                S.op("S", lambda s, n=n: s.activation(
                    out=stats[:, n, 5:6], in_=stats[:, n, 4:5], func=AF.Ln,
                    bias=epsv[:, :], scale=1.0),
                    reads=[statsb[n][4], constb], writes=[statsb[n][5]])
            for n in range(NT):
                S.op("S", lambda s, n=n: s.activation(
                    out=stats[:, n, 6:7], in_=stats[:, n, 5:6], func=AF.Exp,
                    scale=-0.5), reads=[statsb[n][5]], writes=[statsb[n][6]])
            for n in range(NT):
                S.op("V", lambda v, n=n: v.tensor_mul(
                    stats[:, n, 7:8], stats[:, n, 2:3], stats[:, n, 6:7]),
                    reads=[statsb[n][2], statsb[n][6]], writes=[statsb[n][7]])
            for n in range(NT):
                tt = b * NT + n
                mrsb = stats[:, n, 7:8].broadcast_to([128, D])
                S.op("V", lambda v, tt=tt, n=n, mrsb=mrsb: v.scalar_tensor_tensor(
                    out=xn[:, n % 2, :], in0=xsb[:, tt, :], scalar=stats[:, n, 6:7],
                    in1=mrsb, op0=ALU.mult, op1=ALU.subtract),
                    reads=[xb[tt], statsb[n][6], statsb[n][7]], writes=[xnb[n % 2]])
                for half in range(2):
                    hv = (n * 2 + half) % 2
                    S.group("T", [
                        (lambda te, kt=kt, half=half, n=n, hv=hv: te.transpose(
                            PTT[hv][:, kt * 128:(kt + 1) * 128],
                            xn[:, n % 2, (half * 4 + kt) * 128:(half * 4 + kt + 1) * 128],
                            identb[:]))
                        for kt in range(4)],
                        reads=[xnb[n % 2], constb], writes=[pt[hv]])
                    S.op("S", lambda s, half=half, n=n, hv=hv: s.activation(
                        out=xnT[:, half * 4:(half + 1) * 4, n * 128:(n + 1) * 128],
                        in_=PTT[hv][:, 0:512].rearrange("p (a c) -> p a c", c=128),
                        func=AF.Copy),
                        reads=[pt[hv]], writes=[xnTb[n]])

        def emit_ffn(b, call_idx):
            emit_ln(b)
            ez = carve(8192, (NT, 2 * H * E), BF16)
            wrk = carve(10752, (5, 256), F32)
            ssd = carve(15872, (NT, 2 * H * E), F32)
            med = carve(20992, (NT, 2 * H), F32)
            msk = carve(22016, (NT, 2 * H * E), F32)
            gated = carve(27136, (NT, 2 * H * E), BF16)
            whead = carve(29696, (KT, 3 * E * DH), BF16)
            qkT = carve(45056, (2, 2, T), BF16)
            eT = carve(61440, (NT, 2, IC), BF16)
            vext = carve(77824, (2, 2, NT, 66), BF16)
            gacc = carve(86272, (NT, E * DH), BF16)
            otb = carve(96512, (2, 512), BF16)
            onrm = carve(98560, (2, 2, DH), BF16)
            orcp = carve(99072, (2, 2, 1), F32)
            snap = S.snap()
            ezb, wrkb, ssdb = Buf(snap), Buf(snap), Buf(snap)
            medb_, mskb, gatedb = Buf(snap), Buf(snap), Buf(snap)
            wheadb = Buf(snap)
            qkTb = [Buf(snap), Buf(snap)]
            eTb = [Buf(snap) for _ in range(NT)]
            vextb = [Buf(snap), Buf(snap)]
            gaccb = [[Buf(snap) for _ in range(E)] for _ in range(NT)]
            otbb = [Buf(snap), Buf(snap)]
            onrmb = {(hp, it): Buf(snap) for hp in range(2) for it in range(2)}
            orcpb = {(hp, it): Buf(snap) for hp in range(2) for it in range(2)}

            def dma_whead(h):
                c0 = h * E * DH
                S.op("G", lambda g, c0=c0: g.dma_start(
                    out=whead[:, :, 0:320],
                    in_=fWq[:, c0:c0 + 320].rearrange("(kt p) m -> p kt m", p=128)),
                    writes=[wheadb], dma="dw")
                S.op("G", lambda g, c0=c0: g.dma_start(
                    out=whead[:, :, 320:640],
                    in_=fWkv[:, c0:c0 + 320].rearrange("(kt p) m -> p kt m", p=128)),
                    writes=[wheadb], dma="dw")
                S.op("G", lambda g, c0=c0: g.dma_start(
                    out=whead[:, :, 640:960],
                    in_=fWkv[:, 2560 + c0:2560 + c0 + 320].rearrange(
                        "(kt p) m -> p kt m", p=128)),
                    writes=[wheadb], dma="dw")

            dma_whead(0)
            for n in range(NT):
                ppx = pp[n % 2]
                S.group("T", [
                    (lambda te, kt=kt, n=n: te.matmul(
                        PP[:, (n % 2) * 512:(n % 2) * 512 + 2 * H * E],
                        xnT[:, kt, n * 128:(n + 1) * 128], wsd[:, kt, :],
                        start=(kt == 0), stop=(kt == KT - 1)))
                    for kt in range(KT)],
                    reads=[xnTb[n], wsdb], writes=[ppx])
                S.op("S", lambda s, n=n: s.activation(
                    out=ez[:, n, :], in_=PP[:, (n % 2) * 512:(n % 2) * 512 + 2 * H * E],
                    func=AF.Exp, scale=-1.0), reads=[ppx], writes=[ezb])
            ezf = ez.rearrange("p a c -> p (a c)")
            ssdf = ssd.rearrange("p a c -> p (a c)")
            wrkf = wrk.rearrange("p a c -> p (a c)")
            S.op("V", lambda v: v.tensor_scalar_add(wrkf, ezf, 1.0),
                 reads=[ezb], writes=[wrkb])
            S.op("V", lambda v: v.reciprocal(ssdf, wrkf),
                 reads=[wrkb], writes=[ssdb])
            sv = ssd.rearrange("p t (gh e) -> p t gh e", e=E)
            slc = [sv[:, :, :, i] for i in range(E)]
            a_ = [wrk[:, i, :].rearrange("p (t gh) -> p t gh", t=NT) for i in range(5)]
            m5 = [Buf(snap) for _ in range(5)]
            medv = med
            S.op("V", lambda v: v.tensor_tensor(a_[0], slc[0], slc[1], ALU.min),
                 reads=[ssdb], writes=[m5[0]])
            S.op("V", lambda v: v.tensor_tensor(a_[1], slc[0], slc[1], ALU.max),
                 reads=[ssdb], writes=[m5[1]])
            S.op("V", lambda v: v.tensor_tensor(a_[2], slc[2], slc[3], ALU.min),
                 reads=[ssdb], writes=[m5[2]])
            S.op("V", lambda v: v.tensor_tensor(a_[3], slc[2], slc[3], ALU.max),
                 reads=[ssdb], writes=[m5[3]])
            S.op("V", lambda v: v.tensor_tensor(a_[4], a_[0], a_[2], ALU.max),
                 reads=[m5[0], m5[2]], writes=[m5[4]])
            S.op("V", lambda v: v.tensor_tensor(a_[0], a_[1], a_[3], ALU.min),
                 reads=[m5[1], m5[3]], writes=[m5[0]])
            S.op("V", lambda v: v.tensor_tensor(a_[1], a_[4], a_[0], ALU.min),
                 reads=[m5[4], m5[0]], writes=[m5[1]])
            S.op("V", lambda v: v.tensor_tensor(a_[2], a_[4], a_[0], ALU.max),
                 reads=[m5[4], m5[0]], writes=[m5[2]])
            S.op("V", lambda v: v.tensor_tensor(a_[3], slc[4], a_[2], ALU.min),
                 reads=[ssdb, m5[2]], writes=[m5[3]])
            S.op("V", lambda v: v.tensor_tensor(medv, a_[1], a_[3], ALU.max),
                 reads=[m5[1], m5[3]], writes=[medb_])
            medbx = med.unsqueeze(3).broadcast_to([128, NT, 2 * H, E])
            mskv = msk.rearrange("p t (gh e) -> p t gh e", e=E)
            S.op("V", lambda v: v.tensor_tensor(mskv, sv, medbx, ALU.is_ge),
                 reads=[ssdb, medb_], writes=[mskb])
            S.op("V", lambda v: v.tensor_tensor(
                gated.rearrange("p a c -> p (a c)"), ssdf,
                msk.rearrange("p a c -> p (a c)"), ALU.mult),
                reads=[ssdb, mskb], writes=[gatedb])
            S.op("V", lambda v: v.memset(gacc.rearrange("p a c -> p (a c)"), 0.0),
                 writes=[gaccb[n][e] for n in range(NT) for e in range(E)])
            S.op("V", lambda v: v.memset(vext[:, :, :, :, 64:66], 0.0),
                 writes=[vextb[0], vextb[1]])
            S.op("V", lambda v: v.memset(vext[:, :, :, :, 64:65], 1.0),
                 writes=[vextb[0], vextb[1]])

            qg = carve(10752, (2, 3 * E * DH), BF16)
            qka = carve(0, (NT, 256), BF16)
            snap2 = S.snap()
            qgb = [Buf(snap2), Buf(snap2)]
            qkab = [Buf(snap2) for _ in range(NT)]

            prep_pend = []

            def prep_mults(h, n):
                sl = n % 2
                for c, (lo, hi) in enumerate(((0, 512), (512, 960))):
                    ppx = pp[c]
                    S.group("T", [
                        (lambda te, kt=kt, n=n, lo=lo, hi=hi, c=c: te.matmul(
                            PP[:, c * 512:c * 512 + (hi - lo)],
                            xnT[:, kt, n * 128:(n + 1) * 128],
                            whead[:, kt, lo:hi],
                            start=(kt == 0), stop=(kt == KT - 1)))
                        for kt in range(KT)],
                        reads=[xnTb[n], wheadb], writes=[ppx])
                    for s0, s1, gsel in ((0, 320, 1), (320, 640, 0), (640, 960, 0)):
                        a0_, b0_ = max(lo, s0), min(hi, s1)
                        if a0_ >= b0_:
                            continue
                        ne = (b0_ - a0_) // DH
                        e0 = (a0_ - s0) // DH
                        gbase = (H * E if gsel else 0) + h * E + e0
                        gapb = gated[:, n, gbase:gbase + ne].unsqueeze(2) \
                            .broadcast_to([128, ne, DH])
                        pin = PP[:, c * 512 + (a0_ - lo):c * 512 + (b0_ - lo)] \
                            .rearrange("p (e d) -> p e d", d=DH)
                        qout = qg[:, sl, a0_:b0_].rearrange("p (e d) -> p e d", d=DH)
                        S.op("V", lambda v, pin=pin, gapb=gapb, qout=qout:
                             v.tensor_tensor(qout, pin, gapb, ALU.mult),
                             reads=[ppx, gatedb], writes=[qgb[sl]])

            def prep_tail(h, n):
                sl = n % 2
                p_ = h // 2
                pbuf = p_ % 2
                hp = h % 2
                S.op("V", lambda v, n=n, hp=hp, sl=sl: v.tensor_reduce(
                    qka[:, n, hp * 64:hp * 64 + 64],
                    qg[:, sl, 0:320].rearrange("p (e d) -> p d e", d=DH),
                    axis=AX.X, op=ALU.add),
                    reads=[qgb[sl]], writes=[qkab[n]])
                S.op("V", lambda v, n=n, hp=hp, sl=sl: v.tensor_reduce(
                    qka[:, n, 128 + hp * 64:128 + hp * 64 + 64],
                    qg[:, sl, 320:640].rearrange("p (e d) -> p d e", d=DH),
                    axis=AX.X, op=ALU.add),
                    reads=[qgb[sl]], writes=[qkab[n]])
                S.op("V", lambda v, n=n, hp=hp, pbuf=pbuf, sl=sl: v.tensor_reduce(
                    vext[:, pbuf, hp, n, 0:64],
                    qg[:, sl, 640:960].rearrange("p (e d) -> p d e", d=DH),
                    axis=AX.X, op=ALU.add),
                    reads=[qgb[sl]], writes=[vextb[pbuf]])
                if hp == 1:
                    for qk in range(2):
                        hv = (n + qk) % 2
                        S.op("T", lambda te, qk=qk, n=n, hv=hv: te.transpose(
                            PTT[hv][:, 0:128], qka[:, n, qk * 128:(qk + 1) * 128],
                            identb[:]),
                            reads=[qkab[n], constb], writes=[pt[hv]])
                        S.op("S", lambda s, qk=qk, n=n, pbuf=pbuf, hv=hv:
                             s.activation(
                                 out=qkT[:, pbuf, qk, n * 128:(n + 1) * 128],
                                 in_=PTT[hv][:, 0:128], func=AF.Copy),
                             reads=[pt[hv]], writes=[qkTb[pbuf]])

            def emit_prep_tiles(h, tiles):
                for n in tiles:
                    prep_mults(h, n)
                    if prep_pend:
                        prep_tail(*prep_pend.pop())
                    prep_pend.append((h, n))

            def emit_o(p_, ic, j):
                pbuf = p_ % 2
                for hp in range(2):
                    S.op("T", lambda te, j=j, hp=hp, ic=ic, pbuf=pbuf: te.matmul(
                        PB[0:65, hp * 512 + (ic % 2) * 256:hp * 512 + (ic % 2) * 256 + 256],
                        vext[:, pbuf, hp, j, 0:65], eT[:, j, hp, :],
                        start=(j == 0), stop=(j == NT - 1), skip_group_check=True),
                        reads=[vextb[pbuf], eTb[j]], writes=[pb[hp]])

            def emit_attn_chunk(p_, ic):
                pbuf = p_ % 2
                i0 = ic * IC
                for j in range(NT):
                    half = j % 2
                    pax = pa[half]
                    for hp in range(2):
                        S.op("T", lambda te, j=j, hp=hp, half=half, i0=i0, pbuf=pbuf:
                             te.matmul(
                                 PA[:, half * 512 + hp * IC:half * 512 + (hp + 1) * IC],
                                 qkT[64 * hp:64 * (hp + 1), pbuf, 1, j * 128:(j + 1) * 128],
                                 qkT[64 * hp:64 * (hp + 1), pbuf, 0, i0:i0 + IC],
                                 start=True, stop=True, tile_position=(64 * hp, 0)),
                             reads=[qkTb[pbuf]], writes=[pax])
                    S.op("S", lambda s, j=j, half=half: s.activation(
                        out=eT[:, j, :, :].rearrange("p a c -> p (a c)"),
                        in_=PA[:, half * 512:half * 512 + 512], func=AF.Exp,
                        scale=SC_F), reads=[pax], writes=[eTb[j]])
                    if j >= 1:
                        emit_o(p_, ic, j - 1)
                emit_o(p_, ic, NT - 1)
                emit_evac(p_, ic)
            def emit_evac(p_, ic):
                slot = ic % 2
                S.op("S", lambda s, ic=ic, slot=slot: s.activation(
                    out=otb[0:65, slot, :].rearrange("p (a c) -> p a c", a=2),
                    in_=PB[0:65, :].rearrange("p (hp q) -> p hp q", hp=2)
                        [:, :, (ic % 2) * 256:(ic % 2) * 256 + 256],
                    func=AF.Copy),
                    reads=[pb[0], pb[1]], writes=[otbb[slot]])
                for hp in range(2):
                    for it in range(IC // 128):
                        n = ic * 2 + it
                        hv = (hp * 2 + it) % 2
                        S.op("T", lambda te, hp=hp, it=it, slot=slot, hv=hv:
                             te.transpose(
                                 PTT[hv][:, 0:65],
                                 otb[0:65, slot, hp * IC + it * 128:hp * IC + it * 128 + 128],
                                 identb[0:65, 0:65]),
                             reads=[otbb[slot], constb], writes=[pt[hv]])
                        S.op("V", lambda v, hp=hp, it=it, hv=hv: v.reciprocal(
                            orcp[:, hp, it, :], PTT[hv][:, 64:65]),
                            reads=[pt[hv]], writes=[orcpb[(hp, it)]])
                        S.op("V", lambda v, hp=hp, it=it, hv=hv: v.tensor_scalar_mul(
                            onrm[:, hp, it, :], PTT[hv][:, 0:64],
                            orcp[:, hp, it, 0:1]),
                            reads=[pt[hv], orcpb[(hp, it)]], writes=[onrmb[(hp, it)]])
                        h = p_ * 2 + hp
                        for e in range(E):
                            S.op("V", lambda v, n=n, e=e, h=h, hp=hp, it=it:
                                 v.scalar_tensor_tensor(
                                     out=gacc[:, n, e * DH:(e + 1) * DH],
                                     in0=onrm[:, hp, it, :],
                                     scalar=msk[:, n, H * E + h * E + e:H * E + h * E + e + 1],
                                     in1=gacc[:, n, e * DH:(e + 1) * DH],
                                     op0=ALU.mult, op1=ALU.add),
                                 reads=[onrmb[(hp, it)], mskb, gaccb[n][e]],
                                 writes=[gaccb[n][e]])

            emit_prep_tiles(0, range(NT))
            dma_whead(1)
            emit_prep_tiles(1, range(NT))
            for p_ in range(4):
                while prep_pend:
                    prep_tail(*prep_pend.pop())
                for ic in range(NICH):
                    emit_attn_chunk(p_, ic)
                    if p_ < 3:
                        h2 = 2 * (p_ + 1)
                        if ic == 0:
                            dma_whead(h2)
                        if ic < 4:
                            emit_prep_tiles(h2, range(ic * 4, ic * 4 + 4))
                        if ic == 3:
                            dma_whead(h2 + 1)
                        if ic >= 4:
                            emit_prep_tiles(h2 + 1, range((ic - 4) * 4, (ic - 4) * 4 + 4))
            out_sb = carve(8192, (2, D), F32)
            gt = carve(16448, (2, 3, 128), BF16)
            snap3 = S.snap()
            outb = [Buf(snap3), Buf(snap3)]
            gtb = [Buf(snap3), Buf(snap3)]
            last = (call_idx == 2)
            for n in range(NT):
                slot = n % 2
                for kt in range(3):
                    w_ = 128 if kt < 2 else 64
                    hv = kt % 2
                    S.op("T", lambda te, n=n, kt=kt, w_=w_, hv=hv: te.transpose(
                        PTT[hv][0:w_, 0:128], gacc[:, n, kt * 128:kt * 128 + w_],
                        identb[:]),
                        reads=[gaccb[n][ee] for ee in range(E)] + [constb],
                        writes=[pt[hv]])
                    S.op("S", lambda s, n=n, kt=kt, w_=w_, slot=slot, hv=hv:
                         s.activation(out=gt[0:w_, slot, kt, :],
                                      in_=PTT[hv][0:w_, 0:128], func=AF.Copy),
                         reads=[pt[hv]], writes=[gtb[slot]])
                pax = (pa, pb)[n % 2]
                PX = PX2[n % 2]
                for c in range(2):
                    S.group("T", [
                        (lambda te, kt=kt, c=c, slot=slot, PX=PX: te.matmul(
                            PX[:, c * 512:(c + 1) * 512],
                            gt[0:(128 if kt < 2 else 64), slot, kt, :],
                            woT[0:(128 if kt < 2 else 64), kt, c * 512:(c + 1) * 512],
                            start=(kt == 0), stop=(kt == 2),
                            tile_position=((0, 0) if kt == 2 else None)))
                        for kt in range(3)],
                        reads=[gtb[slot], woTb], writes=[pax[c]])
                tt = b * NT + n
                if last:
                    S.op("V", lambda v, tt=tt, slot=slot, PX=PX: v.scalar_tensor_tensor(
                        out=out_sb[:, slot, :], in0=PX[:, :], scalar=0.5,
                        in1=xsb[:, tt, :], op0=ALU.mult, op1=ALU.add),
                        reads=[pax[0], pax[1], xb[tt]], writes=[outb[slot]])
                    S.op("Y", lambda y, tt=tt, slot=slot: y.dma_start(
                        out=y_t[:, tt, :], in_=out_sb[:, slot, :]),
                        reads=[outb[slot]], dma="dy")
                else:
                    S.op("V", lambda v, tt=tt, PX=PX: v.scalar_tensor_tensor(
                        out=xsb[:, tt, :], in0=PX[:, :], scalar=0.5,
                        in1=xsb[:, tt, :], op0=ALU.mult, op1=ALU.add),
                        reads=[pax[0], pax[1], xb[tt]], writes=[xb[tt]])

        def emit_attn_sub(b):
            emit_ln(b)
            aw = carve(8192, (2, KT, D), BF16)
            uT = carve(40960, (KT, T), BF16)
            qa = carve(73728, (2, D), BF16)
            psb = carve(77824, (2, D), F32)
            pexp = carve(86016, (2, D), BF16)
            q16 = carve(90112, (2, D), F16)
            gqb_s = carve(94208, (D,), F32)
            gq_s = carve(98304, (D,), F32)
            gkT_s = carve(102400, (KT,), F32)
            rcp = carve(102464, (2, 1), F32)
            snap = S.snap()
            awb = [Buf(snap), Buf(snap)]
            uTb = Buf(snap)
            qab = [Buf(snap), Buf(snap)]
            psbb = [Buf(snap), Buf(snap)]
            pexpb = [Buf(snap), Buf(snap)]
            q16b = [Buf(snap), Buf(snap)]
            gqbb = Buf(snap)
            gkTb = Buf(snap)
            gqsb = Buf(snap)
            rcpb = [Buf(snap), Buf(snap)]

            def dma_aw(w, slot):
                S.op("G", lambda g, w=w, slot=slot: g.dma_start(
                    out=aw[:, slot, :, :],
                    in_=w.rearrange("(kt p) m -> p kt m", p=128)),
                    writes=[awb[slot]], dma="dw")

            dma_aw(aWq, 0)
            dma_aw(aWk, 1)

            for n in range(NT):
                pax = (pa, pb)[n % 2]
                PX = PX2[n % 2]
                slot = n % 2
                for c in range(2):
                    S.group("T", [
                        (lambda te, kt=kt, n=n, c=c, PX=PX: te.matmul(
                            PX[:, c * 512:(c + 1) * 512],
                            xnT[:, kt, n * 128:(n + 1) * 128],
                            aw[:, 0, kt, c * 512:(c + 1) * 512],
                            start=(kt == 0), stop=(kt == KT - 1)))
                        for kt in range(KT)],
                        reads=[xnTb[n], awb[0]], writes=[pax[c]])
                S.op("S", lambda s, n=n, slot=slot, PX=PX: s.activation(
                    out=pexp[:, slot, :], in_=PX[:, :], func=AF.Exp, scale=SC_A,
                    accum_out=stats[:, n, 0:1]),
                    reads=[pax[0], pax[1]], writes=[pexpb[slot], statsb[n][0]])
                S.op("S", lambda s, n=n, slot=slot, PX=PX: s.activation(
                    out=q16[:, slot, :], in_=PX[:, :], func=AF.Copy),
                    reads=[pax[0], pax[1]], writes=[q16b[slot]])
                S.op("Y", lambda y, n=n, slot=slot: y.dma_start(
                    out=qsp_t[:, n, :], in_=q16[:, slot, :]),
                    reads=[q16b[slot]], writes=[qspb[n]], dma="dy")
                S.op("V", lambda v, n=n, slot=slot: v.reciprocal(
                    rcp[:, slot, :], stats[:, n, 0:1]),
                    reads=[statsb[n][0]], writes=[rcpb[slot]])
                S.op("V", lambda v, n=n, slot=slot, PX=PX: v.scalar_tensor_tensor(
                    out=qa[:, slot, :], in0=pexp[:, slot, :], scalar=rcp[:, slot, 0:1],
                    in1=PX[:, :], op0=ALU.mult, op1=ALU.mult),
                    reads=[pexpb[slot], rcpb[slot], pax[0], pax[1]],
                    writes=[qab[slot]])
                for m_ in ([n - 1] if n > 0 else []) + ([n] if n == NT - 1 else []):
                    msl = m_ % 2
                    for c in range(2):
                        S.op("T", lambda te, m_=m_, c=c, msl=msl: te.matmul(
                            PP[0:1, c * 512:(c + 1) * 512], onesb[:, :],
                            qa[:, msl, c * 512:(c + 1) * 512],
                            start=(m_ == 0), stop=(m_ == NT - 1),
                            skip_group_check=True),
                            reads=[qab[msl], constb], writes=[pp[c]])
            S.op("S", lambda s: s.activation(out=gq_s[0:1, :], in_=PP[0:1, :],
                                             func=AF.Copy),
                 reads=[pp[0], pp[1]], writes=[gqsb])
            for c in range(2):
                S.op("T", lambda te, c=c: te.matmul(
                    PA[:, c * 512:(c + 1) * 512], ones1[:, :],
                    gq_s[0:1, c * 512:(c + 1) * 512], start=True, stop=True),
                    reads=[gqsb, constb], writes=[pa[c]])
            S.op("V", lambda v: v.tensor_copy(gqb_s[:, :], PA[:, :]),
                 reads=[pa[0], pa[1]], writes=[gqbb])
            dma_aw(aWv, 0)

            for n in range(NT):
                pax = (pa, pb)[n % 2]
                PX = PX2[n % 2]
                slot = n % 2
                for c in range(2):
                    S.group("T", [
                        (lambda te, kt=kt, n=n, c=c, PX=PX: te.matmul(
                            PX[:, c * 512:(c + 1) * 512],
                            xnT[:, kt, n * 128:(n + 1) * 128],
                            aw[:, 1, kt, c * 512:(c + 1) * 512],
                            start=(kt == 0), stop=(kt == KT - 1)))
                        for kt in range(KT)],
                        reads=[xnTb[n], awb[1]], writes=[pax[c]])
                S.op("V", lambda v, slot=slot, PX=PX: v.tensor_tensor(
                    psb[:, slot, :], PX[:, :], gqb_s[:, :], ALU.mult),
                    reads=[pax[0], pax[1], gqbb], writes=[psbb[slot]])
                S.op("S", lambda s, n=n, slot=slot: s.activation(
                    out=pexp[:, slot, :], in_=psb[:, slot, :], func=AF.Exp,
                    scale=SC_A, accum_out=stats[:, n, 1:2]),
                    reads=[psbb[slot]], writes=[pexpb[slot], statsb[n][1]])
                S.op("V", lambda v, n=n, slot=slot: v.reciprocal(
                    rcp[:, slot, :], stats[:, n, 1:2]),
                    reads=[statsb[n][1]], writes=[rcpb[slot]])
                S.op("V", lambda v, slot=slot: v.scalar_tensor_tensor(
                    out=qa[:, slot, :], in0=pexp[:, slot, :], scalar=rcp[:, slot, 0:1],
                    in1=psb[:, slot, :], op0=ALU.mult, op1=ALU.mult),
                    reads=[pexpb[slot], rcpb[slot], psbb[slot]], writes=[qab[slot]])
                for m_ in ([n - 1] if n > 0 else []) + ([n] if n == NT - 1 else []):
                    msl = m_ % 2
                    for c in range(2):
                        S.op("T", lambda te, m_=m_, c=c, msl=msl: te.matmul(
                            PP[0:1, c * 512:(c + 1) * 512], onesb[:, :],
                            qa[:, msl, c * 512:(c + 1) * 512],
                            start=(m_ == 0), stop=(m_ == NT - 1),
                            skip_group_check=True),
                            reads=[qab[msl], constb], writes=[pp[c]])
            S.op("S", lambda s: s.activation(out=gq_s[0:1, :], in_=PP[0:1, :],
                                             func=AF.Copy),
                 reads=[pp[0], pp[1]], writes=[gqsb])
            for kt in range(KT):
                S.op("T", lambda te, kt=kt: te.transpose(
                    PA[:, kt:kt + 1], gq_s[0:1, kt * 128:(kt + 1) * 128],
                    identf[:, :]), reads=[gqsb, constb], writes=[pa[0]])
            S.op("V", lambda v: v.tensor_copy(gkT_s[:, :], PA[:, 0:KT]),
                 reads=[pa[0]], writes=[gkTb])
            dma_aw(aWr, 1)

            for ft in range(KT):
                for ch in range(4):
                    idx = ft * 4 + ch
                    bank = ((pa, pb)[(idx // 2) % 2])[idx % 2]
                    PX = PX2[(idx // 2) % 2]
                    S.group("T", [
                        (lambda te, kt=kt, ft=ft, ch=ch, idx=idx, PX=PX: te.matmul(
                            PX[:, (idx % 2) * 512:(idx % 2) * 512 + 512],
                            aw[:, 0, kt, ft * 128:(ft + 1) * 128],
                            xnT[:, kt, ch * 512:(ch + 1) * 512],
                            start=(kt == 0), stop=(kt == KT - 1)))
                        for kt in range(KT)],
                        reads=[awb[0]] + [xnTb[4 * ch + q] for q in range(4)],
                        writes=[bank])
                    S.op("V", lambda v, ft=ft, ch=ch, idx=idx, PX=PX: v.tensor_scalar_mul(
                        uT[:, ft, ch * 512:(ch + 1) * 512],
                        PX[:, (idx % 2) * 512:(idx % 2) * 512 + 512],
                        gkT_s[:, ft:ft + 1]),
                        reads=[bank, gkTb], writes=[uTb])

            for n in range(NT):
                slot = n % 2
                S.op("Y", lambda y, n=n, slot=slot: y.dma_start(
                    out=q16[:, slot, :], in_=qsp_t[:, n, :]),
                    reads=[qspb[n]], writes=[q16b[slot]], dma="dy")
                pax = (pa, pb)[n % 2]
                PX = PX2[n % 2]
                for c in range(2):
                    S.group("T", [
                        (lambda te, kt=kt, n=n, c=c, PX=PX: te.matmul(
                            PX[:, c * 512:(c + 1) * 512],
                            uT[:, kt, n * 128:(n + 1) * 128],
                            aw[:, 1, kt, c * 512:(c + 1) * 512],
                            start=(kt == 0), stop=(kt == KT - 1)))
                        for kt in range(KT)],
                        reads=[uTb, awb[1]], writes=[pax[c]])
                tt = b * NT + n
                S.op("V", lambda v, slot=slot, PX=PX: v.scalar_tensor_tensor(
                    out=psb[:, slot, :], in0=PX[:, :], scalar=1.0,
                    in1=q16[:, slot, :], op0=ALU.mult, op1=ALU.add),
                    reads=[pax[0], pax[1], q16b[slot]], writes=[psbb[slot]])
                S.op("V", lambda v, tt=tt, slot=slot: v.tensor_tensor(
                    xsb[:, tt, :], psb[:, slot, :], xsb[:, tt, :], ALU.add),
                    reads=[psbb[slot], xb[tt]], writes=[xb[tt]])

        for call in range(min(3, STOP_AFTER)):
            for b in range(BL):
                if call == 1:
                    emit_attn_sub(b)
                else:
                    emit_ffn(b, call)

        if STOP_AFTER < 3:
            for i in range(4):
                S.op("G", lambda g, i=i: g.dma_start(
                    out=y_t[:, i * 8:(i + 1) * 8, :],
                    in_=xsb[:, i * 8:(i + 1) * 8, :]),
                    reads=[xb[j] for j in range(i * 8, (i + 1) * 8)], dma="dw")

        n_ops = {k: len(v) for k, v in S.streams.items()}
        n_ops["sem"] = dict(S.cnt)

        def replay(eng_key):
            def run(e):
                seen = {}
                for (fn, waits, edge) in S.streams[eng_key]:
                    for (sname, val) in waits:
                        if seen.get(sname, 0) < val:
                            e.wait_ge(sems[sname], val)
                            seen[sname] = val
                    ins = fn(e)
                    if edge is not None:
                        ins.then_inc(sems[edge[0]], edge[2])
                if eng_key == "Y" and S.cnt["dy"] > 0:
                    e.wait_ge(sems["dy"], S.cnt["dy"])
                if eng_key == "G" and STOP_AFTER < 3:
                    e.wait_ge(sems["dw"], S.cnt["dw"])
            return run

        with nc.allow_low_precision(reason="bf16/f16 stores of fp32-computed values"):
            block.tensor(replay("T"))
            block.vector(replay("V"))
            block.scalar(replay("S"))
            block.gpsimd(replay("G"))
            block.sync(replay("Y"))

    nc._n_ops = n_ops
    return nc


def kernel(L0_attn_x, L0_attn_Wq, L0_attn_Wk, L0_attn_Wv, L0_attn_Wr,
           L0_ffn_Wq, L0_ffn_Wkv, L0_ffn_Ws, L0_ffn_Wd, L0_ffn_Wo):
    if "nc" not in _CACHE:
        _CACHE["nc"] = build_nc()
    nc = _CACHE["nc"]
    x = np.ascontiguousarray(L0_attn_x, dtype=np.float32)
    shards = x.reshape(NC, BL, T, D)
    w = {
        "awq": np.ascontiguousarray(L0_attn_Wq, dtype=np.float32),
        "awk": np.ascontiguousarray(L0_attn_Wk, dtype=np.float32),
        "awv": np.ascontiguousarray(L0_attn_Wv, dtype=np.float32),
        "awr": np.ascontiguousarray(L0_attn_Wr, dtype=np.float32),
        "fwq": np.ascontiguousarray(L0_ffn_Wq, dtype=np.float32),
        "fwkv": np.ascontiguousarray(L0_ffn_Wkv, dtype=np.float32),
        "fws": np.ascontiguousarray(L0_ffn_Ws, dtype=np.float32),
        "fwd": np.ascontiguousarray(L0_ffn_Wd, dtype=np.float32),
        "fwo": np.ascontiguousarray(L0_ffn_Wo, dtype=np.float32),
    }
    in_maps = [dict(w, x=np.ascontiguousarray(shards[i])) for i in range(NC)]
    res = run_bass_kernel_spmd(nc, in_maps, core_ids=list(range(NC)))
    out = np.concatenate([r["y"] for r in res.results], axis=0)
    return out.reshape(B, T, D).astype(np.float32)


# version 04
# speedup vs baseline: 1.0598x; 1.0032x over previous
import numpy as np
from contextlib import ExitStack
import concourse.bass as bass
import concourse.mybir as mybir
from concourse.bass_utils import run_bass_kernel_spmd

F32 = mybir.dt.float32
BF16 = mybir.dt.bfloat16
F16 = mybir.dt.float16
AF = mybir.ActivationFunctionType
ALU = mybir.AluOpType
AX = mybir.AxisListType

NC = 8
B = 16
BL = 2
T = 2048
NT = 16
D = 1024
KT = 8
H = 8
E = 5
DH = 64
IC = 256
NICH = T // IC
SC_F = DH ** -0.5
SC_A = D ** -0.5

STOP_AFTER = 3

_CACHE = {}


class Buf:
    __slots__ = ("w", "r")

    def __init__(self, edges=()):
        self.w = list(edges)
        self.r = []


class Sch:
    def __init__(self):
        self.streams = {k: [] for k in ("T", "V", "S", "G", "Y")}
        self.cnt = {k: 0 for k in ("t", "v", "s", "g", "dw", "dy")}
        self.esem = {"T": "t", "V": "v", "S": "s", "G": "g", "Y": None}

    def snap(self):
        return [(k, v) for k, v in self.cnt.items() if v > 0]

    def _waits(self, reads, writes):
        waits = {}
        for b in reads:
            for (s, v) in b.w:
                waits[s] = max(waits.get(s, 0), v)
        for b in writes:
            for (s, v) in b.w:
                waits[s] = max(waits.get(s, 0), v)
            for (s, v) in b.r:
                waits[s] = max(waits.get(s, 0), v)
        return sorted(waits.items())

    def op(self, eng, fn, reads=(), writes=(), dma=None):
        waits = self._waits(reads, writes)
        if dma is not None:
            self.cnt[dma] += 16
            edge = (dma, self.cnt[dma], 16)
        else:
            s = self.esem[eng]
            self.cnt[s] += 1
            edge = (s, self.cnt[s], 1)
        self.streams[eng].append((fn, waits, edge))
        for b in reads:
            b.r.append(edge[:2])
        for b in writes:
            b.w = [edge[:2]]
            b.r = []
        return edge

    def group(self, eng, fns, reads=(), writes=()):
        waits = self._waits(reads, writes)
        s = self.esem[eng]
        self.cnt[s] += 1
        edge = (s, self.cnt[s], 1)
        for i, fn in enumerate(fns):
            self.streams[eng].append(
                (fn, waits if i == 0 else [], edge if i == len(fns) - 1 else None))
        for b in reads:
            b.r.append(edge[:2])
        for b in writes:
            b.w = [edge[:2]]
            b.r = []
        return edge


def build_nc():
    nc = bass.Bass("TRN2", target_bir_lowering=False)

    x_in = nc.dram_tensor("x", [BL, T, D], F32, kind="ExternalInput").ap()
    y_out = nc.dram_tensor("y", [BL, T, D], F32, kind="ExternalOutput").ap()
    aWq = nc.dram_tensor("awq", [D, D], F32, kind="ExternalInput").ap()
    aWk = nc.dram_tensor("awk", [D, D], F32, kind="ExternalInput").ap()
    aWv = nc.dram_tensor("awv", [D, D], F32, kind="ExternalInput").ap()
    aWr = nc.dram_tensor("awr", [D, D], F32, kind="ExternalInput").ap()
    fWq = nc.dram_tensor("fwq", [D, H * E * DH], F32, kind="ExternalInput").ap()
    fWkv = nc.dram_tensor("fwkv", [D, 2 * H * E * DH], F32, kind="ExternalInput").ap()
    fWs = nc.dram_tensor("fws", [D, H * E], F32, kind="ExternalInput").ap()
    fWd = nc.dram_tensor("fwd", [D, H * E], F32, kind="ExternalInput").ap()
    fWo = nc.dram_tensor("fwo", [DH, E * D], F32, kind="ExternalInput").ap()
    qspill = nc.dram_tensor("qspill", [T, D], F16, kind="Internal").ap()

    x_t = x_in.rearrange("b (n p) d -> p (b n) d", p=128)
    y_t = y_out.rearrange("b (n p) d -> p (b n) d", p=128)
    qsp_t = qspill.rearrange("(n p) d -> p n d", p=128)

    ARENA_BYTES = 105472

    with ExitStack() as _es:
        en = _es.enter_context
        xsb = en(nc.sbuf_tensor("xsb", [128, BL * NT, D], F16))
        xnT = en(nc.sbuf_tensor("xnT", [128, KT, T], BF16))
        woT = en(nc.sbuf_tensor("woT", [128, 3, D], BF16))
        wsd = en(nc.sbuf_tensor("wsd", [128, KT, 2 * H * E], BF16))
        identb = en(nc.sbuf_tensor("identb", [128, 128], BF16))
        identf = en(nc.sbuf_tensor("identf", [1, 1], F32))
        onesb = en(nc.sbuf_tensor("onesb", [128, 1], BF16))
        ones1 = en(nc.sbuf_tensor("ones1", [1, 128], F32))
        epsv = en(nc.sbuf_tensor("epsv", [128, 1], F32))
        stats = en(nc.sbuf_tensor("stats", [128, NT, 8], F32))
        arena_t = en(nc.sbuf_tensor("arena", [128, ARENA_BYTES], mybir.dt.uint8))
        PA = en(nc.psum_tensor("PA", [128, 1024], F32))
        PB = en(nc.psum_tensor("PB", [128, 1024], F32))
        PP = en(nc.psum_tensor("PP", [128, 1024], F32))
        PT0 = en(nc.psum_tensor("PT0", [128, 1024], BF16))
        PT1 = en(nc.psum_tensor("PT1", [128, 1024], BF16))
        sem_t = en(nc.semaphore("t"))
        sem_v = en(nc.semaphore("v"))
        sem_s = en(nc.semaphore("s"))
        sem_g = en(nc.semaphore("g"))
        sem_dw = en(nc.semaphore("dw"))
        sem_dy = en(nc.semaphore("dy"))
        block = en(nc.Block())
        S = Sch()
        sems = {"t": sem_t, "v": sem_v, "s": sem_s, "g": sem_g,
                "dw": sem_dw, "dy": sem_dy}
        arena = arena_t.ap()

        def carve(off, shape, dtype):
            nel = 1
            for d_ in shape:
                nel *= d_
            nbytes = nel * mybir.dt.size(dtype)
            assert off % 64 == 0 and off + nbytes <= ARENA_BYTES, (off, nbytes)
            sl = arena[:, off:off + nbytes].bitcast(dtype)
            if len(shape) > 1:
                names = " ".join(f"a{i}" for i in range(len(shape)))
                kw = {f"a{i}": shape[i] for i in range(len(shape))}
                sl = sl.rearrange(f"p ({names}) -> p {names}", **kw)
            return sl

        PTT = (PT0, PT1)
        PX2 = (PA, PB)

        xb = [Buf() for _ in range(BL * NT)]
        xnTb = [Buf() for _ in range(NT)]
        woTb, wsdb = Buf(), Buf()
        constb = Buf()
        statsb = [[Buf() for _ in range(8)] for _ in range(NT)]
        pa = [Buf(), Buf()]
        pb = [Buf(), Buf()]
        pp = [Buf(), Buf()]
        pt = [Buf(), Buf()]
        qspb = [Buf() for _ in range(NT)]

        S.op("G", lambda g: g.memset(onesb[:], 1.0), writes=[constb])
        S.op("G", lambda g: g.memset(ones1[:], 1.0), writes=[constb])
        S.op("G", lambda g: g.memset(epsv[:], 1e-5), writes=[constb])
        S.op("G", lambda g: g.memset(identf[:], 1.0), writes=[constb])
        S.op("G", lambda g: g.memset(identb[:], 0.0), writes=[constb])
        S.op("G", lambda g: g.affine_select(
            out=identb[:], in_=identb[:], compare_op=ALU.not_equal, fill=1.0,
            base=0, pattern=[[-1, 128]], channel_multiplier=1), writes=[constb])

        for i in range(4):
            S.op("G", lambda g, i=i: g.dma_start(
                out=xsb[:, i * 8:(i + 1) * 8, :], in_=x_t[:, i * 8:(i + 1) * 8, :]),
                writes=[xb[j] for j in range(i * 8, (i + 1) * 8)], dma="dw")
        S.op("G", lambda g: g.dma_start(
            out=wsd[:, :, 0:H * E],
            in_=fWs.rearrange("(kt p) m -> p kt m", p=128)), writes=[wsdb], dma="dw")
        S.op("G", lambda g: g.dma_start(
            out=wsd[:, :, H * E:2 * H * E],
            in_=fWd.rearrange("(kt p) m -> p kt m", p=128)), writes=[wsdb], dma="dw")
        for e_ in range(E):
            S.op("G", lambda g, e_=e_: g.dma_start(
                out=woT[(e_ % 2) * 64:(e_ % 2) * 64 + 64, e_ // 2, :],
                in_=fWo[:, e_ * D:(e_ + 1) * D]), writes=[woTb], dma="dw")

        def emit_ln_stats(b):
            xn = carve(0, (2, D), BF16)
            sq = carve(4096, (2, D), BF16)
            snap = S.snap()
            st = {"xn": xn, "xnb": [Buf(snap), Buf(snap)],
                  "sqb": [Buf(snap), Buf(snap)], "b": b}
            for n in range(NT):
                tt = b * NT + n
                S.op("V", lambda v, tt=tt, n=n: v.reduce_sum(
                    stats[:, n, 0:1], xsb[:, tt, :], axis=AX.X),
                    reads=[xb[tt]], writes=[statsb[n][0]])
                S.op("S", lambda s, tt=tt, n=n, sq=sq: s.activation(
                    out=sq[:, n % 2, :], in_=xsb[:, tt, :], func=AF.Square,
                    accum_out=stats[:, n, 1:2]),
                    reads=[xb[tt]], writes=[st["sqb"][n % 2], statsb[n][1]])
            for n in range(NT):
                S.op("V", lambda v, n=n: v.tensor_scalar_mul(
                    stats[:, n, 2:3], stats[:, n, 0:1], 1.0 / D),
                    reads=[statsb[n][0]], writes=[statsb[n][2]])
            for n in range(NT):
                S.op("V", lambda v, n=n: v.tensor_mul(
                    stats[:, n, 3:4], stats[:, n, 2:3], stats[:, n, 2:3]),
                    reads=[statsb[n][2]], writes=[statsb[n][3]])
            for n in range(NT):
                S.op("V", lambda v, n=n: v.scalar_tensor_tensor(
                    out=stats[:, n, 4:5], in0=stats[:, n, 1:2], scalar=1.0 / D,
                    in1=stats[:, n, 3:4], op0=ALU.mult, op1=ALU.subtract),
                    reads=[statsb[n][1], statsb[n][3]], writes=[statsb[n][4]])
            for n in range(NT):
                S.op("S", lambda s, n=n: s.activation(
                    out=stats[:, n, 5:6], in_=stats[:, n, 4:5], func=AF.Ln,
                    bias=epsv[:, :], scale=1.0),
                    reads=[statsb[n][4], constb], writes=[statsb[n][5]])
            for n in range(NT):
                S.op("S", lambda s, n=n: s.activation(
                    out=stats[:, n, 6:7], in_=stats[:, n, 5:6], func=AF.Exp,
                    scale=-0.5), reads=[statsb[n][5]], writes=[statsb[n][6]])
            for n in range(NT):
                S.op("V", lambda v, n=n: v.tensor_mul(
                    stats[:, n, 7:8], stats[:, n, 2:3], stats[:, n, 6:7]),
                    reads=[statsb[n][2], statsb[n][6]], writes=[statsb[n][7]])
            return st

        def emit_ln_apply(st):
            b = st["b"]
            xn = st["xn"]
            xnb = st["xnb"]
            for n in range(NT):
                tt = b * NT + n
                mrsb = stats[:, n, 7:8].broadcast_to([128, D])
                S.op("V", lambda v, tt=tt, n=n, mrsb=mrsb, xn=xn: v.scalar_tensor_tensor(
                    out=xn[:, n % 2, :], in0=xsb[:, tt, :], scalar=stats[:, n, 6:7],
                    in1=mrsb, op0=ALU.mult, op1=ALU.subtract),
                    reads=[xb[tt], statsb[n][6], statsb[n][7]], writes=[xnb[n % 2]])
                for half in range(2):
                    hv = (n * 2 + half) % 2
                    S.group("T", [
                        (lambda te, kt=kt, half=half, n=n, hv=hv, xn=xn: te.transpose(
                            PTT[hv][:, kt * 128:(kt + 1) * 128],
                            xn[:, n % 2, (half * 4 + kt) * 128:(half * 4 + kt + 1) * 128],
                            identb[:]))
                        for kt in range(4)],
                        reads=[xnb[n % 2], constb], writes=[pt[hv]])
                    S.op("S", lambda s, half=half, n=n, hv=hv: s.activation(
                        out=xnT[:, half * 4:(half + 1) * 4, n * 128:(n + 1) * 128],
                        in_=PTT[hv][:, 0:512].rearrange("p (a c) -> p a c", c=128),
                        func=AF.Copy),
                        reads=[pt[hv]], writes=[xnTb[n]])

        def emit_ln(b):
            emit_ln_apply(emit_ln_stats(b))

        def emit_ffn(b, call_idx, ln_st0=None, ln_next=None):
            snap0 = S.snap()
            if ln_st0 is None:
                emit_ln(b)
            ez = carve(8192, (NT, 2 * H * E), BF16)
            wrk = carve(10752, (5, 256), F32)
            ssd = carve(15872, (NT, 2 * H * E), F32)
            med = carve(20992, (NT, 2 * H), F32)
            msk = carve(22016, (NT, 2 * H * E), F32)
            gated = carve(27136, (NT, 2 * H * E), BF16)
            whead = carve(29696, (KT, 3 * E * DH), BF16)
            qkT = carve(45056, (2, 2, T), BF16)
            eT = carve(61440, (NT, 2, IC), BF16)
            vext = carve(77824, (2, 2, NT, 66), BF16)
            gacc = carve(86272, (NT, E * DH), BF16)
            otb = carve(96512, (2, 512), BF16)
            onrm = carve(98560, (2, 2, DH), BF16)
            orcp = carve(99072, (2, 2, 1), F32)
            snap = S.snap()
            ezb, wrkb, ssdb = Buf(snap), Buf(snap), Buf(snap)
            medb_, mskb, gatedb = Buf(snap), Buf(snap), Buf(snap)
            wheadb = Buf(snap0)
            qkTb = [Buf(snap), Buf(snap)]
            eTb = [Buf(snap) for _ in range(NT)]
            vextb = [Buf(snap), Buf(snap)]
            gaccb = [[Buf(snap) for _ in range(E)] for _ in range(NT)]
            otbb = [Buf(snap), Buf(snap)]
            onrmb = {(hp, it): Buf(snap) for hp in range(2) for it in range(2)}
            orcpb = {(hp, it): Buf(snap) for hp in range(2) for it in range(2)}

            def dma_whead(h):
                c0 = h * E * DH
                S.op("G", lambda g, c0=c0: g.dma_start(
                    out=whead[:, :, 0:320],
                    in_=fWq[:, c0:c0 + 320].rearrange("(kt p) m -> p kt m", p=128)),
                    writes=[wheadb], dma="dw")
                S.op("G", lambda g, c0=c0: g.dma_start(
                    out=whead[:, :, 320:640],
                    in_=fWkv[:, c0:c0 + 320].rearrange("(kt p) m -> p kt m", p=128)),
                    writes=[wheadb], dma="dw")
                S.op("G", lambda g, c0=c0: g.dma_start(
                    out=whead[:, :, 640:960],
                    in_=fWkv[:, 2560 + c0:2560 + c0 + 320].rearrange(
                        "(kt p) m -> p kt m", p=128)),
                    writes=[wheadb], dma="dw")

            dma_whead(0)
            for n in range(NT):
                ppx = pp[n % 2]
                S.group("T", [
                    (lambda te, kt=kt, n=n: te.matmul(
                        PP[:, (n % 2) * 512:(n % 2) * 512 + 2 * H * E],
                        xnT[:, kt, n * 128:(n + 1) * 128], wsd[:, kt, :],
                        start=(kt == 0), stop=(kt == KT - 1)))
                    for kt in range(KT)],
                    reads=[xnTb[n], wsdb], writes=[ppx])
                S.op("S", lambda s, n=n: s.activation(
                    out=ez[:, n, :], in_=PP[:, (n % 2) * 512:(n % 2) * 512 + 2 * H * E],
                    func=AF.Exp, scale=-1.0), reads=[ppx], writes=[ezb])
            ezf = ez.rearrange("p a c -> p (a c)")
            ssdf = ssd.rearrange("p a c -> p (a c)")
            wrkf = wrk.rearrange("p a c -> p (a c)")
            S.op("V", lambda v: v.tensor_scalar_add(wrkf, ezf, 1.0),
                 reads=[ezb], writes=[wrkb])
            S.op("V", lambda v: v.reciprocal(ssdf, wrkf),
                 reads=[wrkb], writes=[ssdb])
            sv = ssd.rearrange("p t (gh e) -> p t gh e", e=E)
            slc = [sv[:, :, :, i] for i in range(E)]
            a_ = [wrk[:, i, :].rearrange("p (t gh) -> p t gh", t=NT) for i in range(5)]
            m5 = [Buf(snap) for _ in range(5)]
            medv = med
            S.op("V", lambda v: v.tensor_tensor(a_[0], slc[0], slc[1], ALU.min),
                 reads=[ssdb], writes=[m5[0]])
            S.op("V", lambda v: v.tensor_tensor(a_[1], slc[0], slc[1], ALU.max),
                 reads=[ssdb], writes=[m5[1]])
            S.op("V", lambda v: v.tensor_tensor(a_[2], slc[2], slc[3], ALU.min),
                 reads=[ssdb], writes=[m5[2]])
            S.op("V", lambda v: v.tensor_tensor(a_[3], slc[2], slc[3], ALU.max),
                 reads=[ssdb], writes=[m5[3]])
            S.op("V", lambda v: v.tensor_tensor(a_[4], a_[0], a_[2], ALU.max),
                 reads=[m5[0], m5[2]], writes=[m5[4]])
            S.op("V", lambda v: v.tensor_tensor(a_[0], a_[1], a_[3], ALU.min),
                 reads=[m5[1], m5[3]], writes=[m5[0]])
            S.op("V", lambda v: v.tensor_tensor(a_[1], a_[4], a_[0], ALU.min),
                 reads=[m5[4], m5[0]], writes=[m5[1]])
            S.op("V", lambda v: v.tensor_tensor(a_[2], a_[4], a_[0], ALU.max),
                 reads=[m5[4], m5[0]], writes=[m5[2]])
            S.op("V", lambda v: v.tensor_tensor(a_[3], slc[4], a_[2], ALU.min),
                 reads=[ssdb, m5[2]], writes=[m5[3]])
            S.op("V", lambda v: v.tensor_tensor(medv, a_[1], a_[3], ALU.max),
                 reads=[m5[1], m5[3]], writes=[medb_])
            medbx = med.unsqueeze(3).broadcast_to([128, NT, 2 * H, E])
            mskv = msk.rearrange("p t (gh e) -> p t gh e", e=E)
            S.op("V", lambda v: v.tensor_tensor(mskv, sv, medbx, ALU.is_ge),
                 reads=[ssdb, medb_], writes=[mskb])
            S.op("V", lambda v: v.tensor_tensor(
                gated.rearrange("p a c -> p (a c)"), ssdf,
                msk.rearrange("p a c -> p (a c)"), ALU.mult),
                reads=[ssdb, mskb], writes=[gatedb])
            S.op("V", lambda v: v.memset(gacc.rearrange("p a c -> p (a c)"), 0.0),
                 writes=[gaccb[n][e] for n in range(NT) for e in range(E)])
            S.op("V", lambda v: v.memset(vext[:, :, :, :, 64:66], 0.0),
                 writes=[vextb[0], vextb[1]])
            S.op("V", lambda v: v.memset(vext[:, :, :, :, 64:65], 1.0),
                 writes=[vextb[0], vextb[1]])

            qg = carve(10752, (3, 3 * E * DH), BF16)
            qka = carve(0, (NT, 256), BF16)
            snap2 = S.snap()
            qgb = [Buf(snap2), Buf(snap2), Buf(snap2)]
            qkab = [Buf(snap2) for _ in range(NT)]

            prep_pend = []

            def prep_mults(h, n):
                sl = n % 3
                for c, (lo, hi) in enumerate(((0, 512), (512, 960))):
                    ppx = pp[c]
                    S.group("T", [
                        (lambda te, kt=kt, n=n, lo=lo, hi=hi, c=c: te.matmul(
                            PP[:, c * 512:c * 512 + (hi - lo)],
                            xnT[:, kt, n * 128:(n + 1) * 128],
                            whead[:, kt, lo:hi],
                            start=(kt == 0), stop=(kt == KT - 1)))
                        for kt in range(KT)],
                        reads=[xnTb[n], wheadb], writes=[ppx])
                    for s0, s1, gsel in ((0, 320, 1), (320, 640, 0), (640, 960, 0)):
                        a0_, b0_ = max(lo, s0), min(hi, s1)
                        if a0_ >= b0_:
                            continue
                        ne = (b0_ - a0_) // DH
                        e0 = (a0_ - s0) // DH
                        gbase = (H * E if gsel else 0) + h * E + e0
                        gapb = gated[:, n, gbase:gbase + ne].unsqueeze(2) \
                            .broadcast_to([128, ne, DH])
                        pin = PP[:, c * 512 + (a0_ - lo):c * 512 + (b0_ - lo)] \
                            .rearrange("p (e d) -> p e d", d=DH)
                        qout = qg[:, sl, a0_:b0_].rearrange("p (e d) -> p e d", d=DH)
                        S.op("V", lambda v, pin=pin, gapb=gapb, qout=qout:
                             v.tensor_tensor(qout, pin, gapb, ALU.mult),
                             reads=[ppx, gatedb], writes=[qgb[sl]])

            def prep_tail(h, n):
                sl = n % 3
                p_ = h // 2
                pbuf = p_ % 2
                hp = h % 2
                S.op("V", lambda v, n=n, hp=hp, sl=sl: v.tensor_reduce(
                    qka[:, n, hp * 64:hp * 64 + 64],
                    qg[:, sl, 0:320].rearrange("p (e d) -> p d e", d=DH),
                    axis=AX.X, op=ALU.add),
                    reads=[qgb[sl]], writes=[qkab[n]])
                S.op("V", lambda v, n=n, hp=hp, sl=sl: v.tensor_reduce(
                    qka[:, n, 128 + hp * 64:128 + hp * 64 + 64],
                    qg[:, sl, 320:640].rearrange("p (e d) -> p d e", d=DH),
                    axis=AX.X, op=ALU.add),
                    reads=[qgb[sl]], writes=[qkab[n]])
                S.op("V", lambda v, n=n, hp=hp, pbuf=pbuf, sl=sl: v.tensor_reduce(
                    vext[:, pbuf, hp, n, 0:64],
                    qg[:, sl, 640:960].rearrange("p (e d) -> p d e", d=DH),
                    axis=AX.X, op=ALU.add),
                    reads=[qgb[sl]], writes=[vextb[pbuf]])
                if hp == 1:
                    for qk in range(2):
                        hv = (n + qk) % 2
                        S.op("T", lambda te, qk=qk, n=n, hv=hv: te.transpose(
                            PTT[hv][:, 0:128], qka[:, n, qk * 128:(qk + 1) * 128],
                            identb[:]),
                            reads=[qkab[n], constb], writes=[pt[hv]])
                        S.op("S", lambda s, qk=qk, n=n, pbuf=pbuf, hv=hv:
                             s.activation(
                                 out=qkT[:, pbuf, qk, n * 128:(n + 1) * 128],
                                 in_=PTT[hv][:, 0:128], func=AF.Copy),
                             reads=[pt[hv]], writes=[qkTb[pbuf]])

            def emit_prep_tiles(h, tiles):
                if prep_pend and prep_pend[0][0] != h:
                    while prep_pend:
                        prep_tail(*prep_pend.pop(0))
                for n in tiles:
                    prep_mults(h, n)
                    prep_pend.append((h, n))
                    if len(prep_pend) > 2:
                        prep_tail(*prep_pend.pop(0))

            def emit_o(p_, ic, j):
                pbuf = p_ % 2
                for hp in range(2):
                    S.op("T", lambda te, j=j, hp=hp, ic=ic, pbuf=pbuf: te.matmul(
                        PB[0:65, hp * 512 + (ic % 2) * 256:hp * 512 + (ic % 2) * 256 + 256],
                        vext[:, pbuf, hp, j, 0:65], eT[:, j, hp, :],
                        start=(j == 0), stop=(j == NT - 1), skip_group_check=True),
                        reads=[vextb[pbuf], eTb[j]], writes=[pb[hp]])

            def emit_attn_chunk(p_, ic):
                pbuf = p_ % 2
                i0 = ic * IC
                for j in range(NT):
                    half = j % 2
                    pax = pa[half]
                    for hp in range(2):
                        S.op("T", lambda te, j=j, hp=hp, half=half, i0=i0, pbuf=pbuf:
                             te.matmul(
                                 PA[:, half * 512 + hp * IC:half * 512 + (hp + 1) * IC],
                                 qkT[64 * hp:64 * (hp + 1), pbuf, 1, j * 128:(j + 1) * 128],
                                 qkT[64 * hp:64 * (hp + 1), pbuf, 0, i0:i0 + IC],
                                 start=True, stop=True, tile_position=(64 * hp, 0)),
                             reads=[qkTb[pbuf]], writes=[pax])
                    S.op("S", lambda s, j=j, half=half: s.activation(
                        out=eT[:, j, :, :].rearrange("p a c -> p (a c)"),
                        in_=PA[:, half * 512:half * 512 + 512], func=AF.Exp,
                        scale=SC_F), reads=[pax], writes=[eTb[j]])
                    if j == 2 and ic > 0:
                        emit_evac(p_, ic - 1)
                    if j >= 1:
                        emit_o(p_, ic, j - 1)
                emit_o(p_, ic, NT - 1)
                if ic == NICH - 1:
                    emit_evac(p_, ic)
            def emit_evac(p_, ic):
                slot = ic % 2
                S.op("S", lambda s, ic=ic, slot=slot: s.activation(
                    out=otb[0:65, slot, :].rearrange("p (a c) -> p a c", a=2),
                    in_=PB[0:65, :].rearrange("p (hp q) -> p hp q", hp=2)
                        [:, :, (ic % 2) * 256:(ic % 2) * 256 + 256],
                    func=AF.Copy),
                    reads=[pb[0], pb[1]], writes=[otbb[slot]])
                for hp in range(2):
                    for it in range(IC // 128):
                        n = ic * 2 + it
                        hv = (hp * 2 + it) % 2
                        S.op("T", lambda te, hp=hp, it=it, slot=slot, hv=hv:
                             te.transpose(
                                 PTT[hv][:, 0:65],
                                 otb[0:65, slot, hp * IC + it * 128:hp * IC + it * 128 + 128],
                                 identb[0:65, 0:65]),
                             reads=[otbb[slot], constb], writes=[pt[hv]])
                        S.op("V", lambda v, hp=hp, it=it, hv=hv: v.reciprocal(
                            orcp[:, hp, it, :], PTT[hv][:, 64:65]),
                            reads=[pt[hv]], writes=[orcpb[(hp, it)]])
                        S.op("V", lambda v, hp=hp, it=it, hv=hv: v.tensor_scalar_mul(
                            onrm[:, hp, it, :], PTT[hv][:, 0:64],
                            orcp[:, hp, it, 0:1]),
                            reads=[pt[hv], orcpb[(hp, it)]], writes=[onrmb[(hp, it)]])
                        h = p_ * 2 + hp
                        for e in range(E):
                            S.op("V", lambda v, n=n, e=e, h=h, hp=hp, it=it:
                                 v.scalar_tensor_tensor(
                                     out=gacc[:, n, e * DH:(e + 1) * DH],
                                     in0=onrm[:, hp, it, :],
                                     scalar=msk[:, n, H * E + h * E + e:H * E + h * E + e + 1],
                                     in1=gacc[:, n, e * DH:(e + 1) * DH],
                                     op0=ALU.mult, op1=ALU.add),
                                 reads=[onrmb[(hp, it)], mskb, gaccb[n][e]],
                                 writes=[gaccb[n][e]])

            emit_prep_tiles(0, range(NT))
            dma_whead(1)
            emit_prep_tiles(1, range(NT))
            for p_ in range(4):
                while prep_pend:
                    prep_tail(*prep_pend.pop(0))
                for ic in range(NICH):
                    emit_attn_chunk(p_, ic)
                    if p_ < 3:
                        h2 = 2 * (p_ + 1)
                        if ic == 0:
                            dma_whead(h2)
                        if ic < 4:
                            emit_prep_tiles(h2, range(ic * 4, ic * 4 + 4))
                        if ic == 3:
                            dma_whead(h2 + 1)
                        if ic >= 4:
                            emit_prep_tiles(h2 + 1, range((ic - 4) * 4, (ic - 4) * 4 + 4))
            ln_st = None
            if ln_next is not None:
                ln_st = emit_ln_stats(ln_next)

            out_sb = carve(8192, (2, D), F32)
            gt = carve(16448, (2, 3, 128), BF16)
            snap3 = S.snap()
            outb = [Buf(snap3), Buf(snap3)]
            gtb = [Buf(snap3), Buf(snap3)]
            last = (call_idx == 2)
            for n in range(NT):
                slot = n % 2
                for kt in range(3):
                    w_ = 128 if kt < 2 else 64
                    hv = kt % 2
                    S.op("T", lambda te, n=n, kt=kt, w_=w_, hv=hv: te.transpose(
                        PTT[hv][0:w_, 0:128], gacc[:, n, kt * 128:kt * 128 + w_],
                        identb[:]),
                        reads=[gaccb[n][ee] for ee in range(E)] + [constb],
                        writes=[pt[hv]])
                    S.op("S", lambda s, n=n, kt=kt, w_=w_, slot=slot, hv=hv:
                         s.activation(out=gt[0:w_, slot, kt, :],
                                      in_=PTT[hv][0:w_, 0:128], func=AF.Copy),
                         reads=[pt[hv]], writes=[gtb[slot]])
                pax = (pa, pb)[n % 2]
                PX = PX2[n % 2]
                for c in range(2):
                    S.group("T", [
                        (lambda te, kt=kt, c=c, slot=slot, PX=PX: te.matmul(
                            PX[:, c * 512:(c + 1) * 512],
                            gt[0:(128 if kt < 2 else 64), slot, kt, :],
                            woT[0:(128 if kt < 2 else 64), kt, c * 512:(c + 1) * 512],
                            start=(kt == 0), stop=(kt == 2),
                            tile_position=((0, 0) if kt == 2 else None)))
                        for kt in range(3)],
                        reads=[gtb[slot], woTb], writes=[pax[c]])
                tt = b * NT + n
                if last:
                    S.op("V", lambda v, tt=tt, slot=slot, PX=PX: v.scalar_tensor_tensor(
                        out=out_sb[:, slot, :], in0=PX[:, :], scalar=0.5,
                        in1=xsb[:, tt, :], op0=ALU.mult, op1=ALU.add),
                        reads=[pax[0], pax[1], xb[tt]], writes=[outb[slot]])
                    S.op("Y", lambda y, tt=tt, slot=slot: y.dma_start(
                        out=y_t[:, tt, :], in_=out_sb[:, slot, :]),
                        reads=[outb[slot]], dma="dy")
                else:
                    S.op("V", lambda v, tt=tt, PX=PX: v.scalar_tensor_tensor(
                        out=xsb[:, tt, :], in0=PX[:, :], scalar=0.5,
                        in1=xsb[:, tt, :], op0=ALU.mult, op1=ALU.add),
                        reads=[pax[0], pax[1], xb[tt]], writes=[xb[tt]])
            if ln_st is not None:
                emit_ln_apply(ln_st)
            return ln_st

        def emit_attn_sub(b, ln_st0=None, ln_next=None):
            snap0 = S.snap()
            if ln_st0 is None:
                emit_ln(b)
            aw = carve(8192, (2, KT, D), BF16)
            uT = carve(40960, (KT, T), BF16)
            qa = carve(73728, (2, D), BF16)
            psb = carve(77824, (2, D), F32)
            pexp = carve(86016, (2, D), BF16)
            q16 = carve(90112, (2, D), F16)
            gqb_s = carve(94208, (D,), F32)
            gq_s = carve(98304, (D,), F32)
            gkT_s = carve(102400, (KT,), F32)
            rcp = carve(102464, (2, 1), F32)
            snap = S.snap()
            awb = [Buf(snap0), Buf(snap0)]
            uTb = Buf(snap)
            qab = [Buf(snap), Buf(snap)]
            psbb = [Buf(snap), Buf(snap)]
            pexpb = [Buf(snap), Buf(snap)]
            q16b = [Buf(snap), Buf(snap)]
            gqbb = Buf(snap)
            gkTb = Buf(snap)
            gqsb = Buf(snap)
            rcpb = [Buf(snap), Buf(snap)]

            def dma_aw(w, slot):
                S.op("G", lambda g, w=w, slot=slot: g.dma_start(
                    out=aw[:, slot, :, :],
                    in_=w.rearrange("(kt p) m -> p kt m", p=128)),
                    writes=[awb[slot]], dma="dw")

            dma_aw(aWq, 0)
            dma_aw(aWk, 1)

            for n in range(NT):
                pax = (pa, pb)[n % 2]
                PX = PX2[n % 2]
                slot = n % 2
                for c in range(2):
                    S.group("T", [
                        (lambda te, kt=kt, n=n, c=c, PX=PX: te.matmul(
                            PX[:, c * 512:(c + 1) * 512],
                            xnT[:, kt, n * 128:(n + 1) * 128],
                            aw[:, 0, kt, c * 512:(c + 1) * 512],
                            start=(kt == 0), stop=(kt == KT - 1)))
                        for kt in range(KT)],
                        reads=[xnTb[n], awb[0]], writes=[pax[c]])
                S.op("S", lambda s, n=n, slot=slot, PX=PX: s.activation(
                    out=pexp[:, slot, :], in_=PX[:, :], func=AF.Exp, scale=SC_A,
                    accum_out=stats[:, n, 0:1]),
                    reads=[pax[0], pax[1]], writes=[pexpb[slot], statsb[n][0]])
                S.op("S", lambda s, n=n, slot=slot, PX=PX: s.activation(
                    out=q16[:, slot, :], in_=PX[:, :], func=AF.Copy),
                    reads=[pax[0], pax[1]], writes=[q16b[slot]])
                S.op("Y", lambda y, n=n, slot=slot: y.dma_start(
                    out=qsp_t[:, n, :], in_=q16[:, slot, :]),
                    reads=[q16b[slot]], writes=[qspb[n]], dma="dy")
                S.op("V", lambda v, n=n, slot=slot: v.reciprocal(
                    rcp[:, slot, :], stats[:, n, 0:1]),
                    reads=[statsb[n][0]], writes=[rcpb[slot]])
                S.op("V", lambda v, n=n, slot=slot, PX=PX: v.scalar_tensor_tensor(
                    out=qa[:, slot, :], in0=pexp[:, slot, :], scalar=rcp[:, slot, 0:1],
                    in1=PX[:, :], op0=ALU.mult, op1=ALU.mult),
                    reads=[pexpb[slot], rcpb[slot], pax[0], pax[1]],
                    writes=[qab[slot]])
                for m_ in ([n - 1] if n > 0 else []) + ([n] if n == NT - 1 else []):
                    msl = m_ % 2
                    for c in range(2):
                        S.op("T", lambda te, m_=m_, c=c, msl=msl: te.matmul(
                            PP[0:1, c * 512:(c + 1) * 512], onesb[:, :],
                            qa[:, msl, c * 512:(c + 1) * 512],
                            start=(m_ == 0), stop=(m_ == NT - 1),
                            skip_group_check=True),
                            reads=[qab[msl], constb], writes=[pp[c]])
            S.op("S", lambda s: s.activation(out=gq_s[0:1, :], in_=PP[0:1, :],
                                             func=AF.Copy),
                 reads=[pp[0], pp[1]], writes=[gqsb])
            for c in range(2):
                S.op("T", lambda te, c=c: te.matmul(
                    PA[:, c * 512:(c + 1) * 512], ones1[:, :],
                    gq_s[0:1, c * 512:(c + 1) * 512], start=True, stop=True),
                    reads=[gqsb, constb], writes=[pa[c]])
            S.op("V", lambda v: v.tensor_copy(gqb_s[:, :], PA[:, :]),
                 reads=[pa[0], pa[1]], writes=[gqbb])
            dma_aw(aWv, 0)

            for n in range(NT):
                pax = (pa, pb)[n % 2]
                PX = PX2[n % 2]
                slot = n % 2
                for c in range(2):
                    S.group("T", [
                        (lambda te, kt=kt, n=n, c=c, PX=PX: te.matmul(
                            PX[:, c * 512:(c + 1) * 512],
                            xnT[:, kt, n * 128:(n + 1) * 128],
                            aw[:, 1, kt, c * 512:(c + 1) * 512],
                            start=(kt == 0), stop=(kt == KT - 1)))
                        for kt in range(KT)],
                        reads=[xnTb[n], awb[1]], writes=[pax[c]])
                S.op("V", lambda v, slot=slot, PX=PX: v.tensor_tensor(
                    psb[:, slot, :], PX[:, :], gqb_s[:, :], ALU.mult),
                    reads=[pax[0], pax[1], gqbb], writes=[psbb[slot]])
                S.op("S", lambda s, n=n, slot=slot: s.activation(
                    out=pexp[:, slot, :], in_=psb[:, slot, :], func=AF.Exp,
                    scale=SC_A, accum_out=stats[:, n, 1:2]),
                    reads=[psbb[slot]], writes=[pexpb[slot], statsb[n][1]])
                S.op("V", lambda v, n=n, slot=slot: v.reciprocal(
                    rcp[:, slot, :], stats[:, n, 1:2]),
                    reads=[statsb[n][1]], writes=[rcpb[slot]])
                S.op("V", lambda v, slot=slot: v.scalar_tensor_tensor(
                    out=qa[:, slot, :], in0=pexp[:, slot, :], scalar=rcp[:, slot, 0:1],
                    in1=psb[:, slot, :], op0=ALU.mult, op1=ALU.mult),
                    reads=[pexpb[slot], rcpb[slot], psbb[slot]], writes=[qab[slot]])
                for m_ in ([n - 1] if n > 0 else []) + ([n] if n == NT - 1 else []):
                    msl = m_ % 2
                    for c in range(2):
                        S.op("T", lambda te, m_=m_, c=c, msl=msl: te.matmul(
                            PP[0:1, c * 512:(c + 1) * 512], onesb[:, :],
                            qa[:, msl, c * 512:(c + 1) * 512],
                            start=(m_ == 0), stop=(m_ == NT - 1),
                            skip_group_check=True),
                            reads=[qab[msl], constb], writes=[pp[c]])
            S.op("S", lambda s: s.activation(out=gq_s[0:1, :], in_=PP[0:1, :],
                                             func=AF.Copy),
                 reads=[pp[0], pp[1]], writes=[gqsb])
            for kt in range(KT):
                S.op("T", lambda te, kt=kt: te.transpose(
                    PA[:, kt:kt + 1], gq_s[0:1, kt * 128:(kt + 1) * 128],
                    identf[:, :]), reads=[gqsb, constb], writes=[pa[0]])
            S.op("V", lambda v: v.tensor_copy(gkT_s[:, :], PA[:, 0:KT]),
                 reads=[pa[0]], writes=[gkTb])
            dma_aw(aWr, 1)

            for ft in range(KT):
                for ch in range(4):
                    idx = ft * 4 + ch
                    bank = ((pa, pb)[(idx // 2) % 2])[idx % 2]
                    PX = PX2[(idx // 2) % 2]
                    S.group("T", [
                        (lambda te, kt=kt, ft=ft, ch=ch, idx=idx, PX=PX: te.matmul(
                            PX[:, (idx % 2) * 512:(idx % 2) * 512 + 512],
                            aw[:, 0, kt, ft * 128:(ft + 1) * 128],
                            xnT[:, kt, ch * 512:(ch + 1) * 512],
                            start=(kt == 0), stop=(kt == KT - 1)))
                        for kt in range(KT)],
                        reads=[awb[0]] + [xnTb[4 * ch + q] for q in range(4)],
                        writes=[bank])
                    S.op("V", lambda v, ft=ft, ch=ch, idx=idx, PX=PX: v.tensor_scalar_mul(
                        uT[:, ft, ch * 512:(ch + 1) * 512],
                        PX[:, (idx % 2) * 512:(idx % 2) * 512 + 512],
                        gkT_s[:, ft:ft + 1]),
                        reads=[bank, gkTb], writes=[uTb])

            ln_st = None
            if ln_next is not None:
                ln_st = emit_ln_stats(ln_next)

            for n in range(NT):
                slot = n % 2
                S.op("Y", lambda y, n=n, slot=slot: y.dma_start(
                    out=q16[:, slot, :], in_=qsp_t[:, n, :]),
                    reads=[qspb[n]], writes=[q16b[slot]], dma="dy")
                pax = (pa, pb)[n % 2]
                PX = PX2[n % 2]
                for c in range(2):
                    S.group("T", [
                        (lambda te, kt=kt, n=n, c=c, PX=PX: te.matmul(
                            PX[:, c * 512:(c + 1) * 512],
                            uT[:, kt, n * 128:(n + 1) * 128],
                            aw[:, 1, kt, c * 512:(c + 1) * 512],
                            start=(kt == 0), stop=(kt == KT - 1)))
                        for kt in range(KT)],
                        reads=[uTb, awb[1]], writes=[pax[c]])
                tt = b * NT + n
                S.op("V", lambda v, slot=slot, PX=PX: v.scalar_tensor_tensor(
                    out=psb[:, slot, :], in0=PX[:, :], scalar=1.0,
                    in1=q16[:, slot, :], op0=ALU.mult, op1=ALU.add),
                    reads=[pax[0], pax[1], q16b[slot]], writes=[psbb[slot]])
                S.op("V", lambda v, tt=tt, slot=slot: v.tensor_tensor(
                    xsb[:, tt, :], psb[:, slot, :], xsb[:, tt, :], ALU.add),
                    reads=[psbb[slot], xb[tt]], writes=[xb[tt]])
            if ln_st is not None:
                emit_ln_apply(ln_st)
            return ln_st

        phases = [(c, b) for c in range(min(3, STOP_AFTER)) for b in range(BL)]
        carry = None
        for idx, (c, b) in enumerate(phases):
            nxt = phases[idx + 1][1] if idx + 1 < len(phases) else None
            if c == 1:
                carry = emit_attn_sub(b, ln_st0=carry, ln_next=nxt)
            else:
                carry = emit_ffn(b, c, ln_st0=carry, ln_next=nxt)

        if STOP_AFTER < 3:
            for i in range(4):
                S.op("G", lambda g, i=i: g.dma_start(
                    out=y_t[:, i * 8:(i + 1) * 8, :],
                    in_=xsb[:, i * 8:(i + 1) * 8, :]),
                    reads=[xb[j] for j in range(i * 8, (i + 1) * 8)], dma="dw")

        n_ops = {k: len(v) for k, v in S.streams.items()}
        n_ops["sem"] = dict(S.cnt)

        def replay(eng_key):
            def run(e):
                seen = {}
                for (fn, waits, edge) in S.streams[eng_key]:
                    for (sname, val) in waits:
                        if seen.get(sname, 0) < val:
                            e.wait_ge(sems[sname], val)
                            seen[sname] = val
                    ins = fn(e)
                    if edge is not None:
                        ins.then_inc(sems[edge[0]], edge[2])
                if eng_key == "Y" and S.cnt["dy"] > 0:
                    e.wait_ge(sems["dy"], S.cnt["dy"])
                if eng_key == "G" and STOP_AFTER < 3:
                    e.wait_ge(sems["dw"], S.cnt["dw"])
            return run

        with nc.allow_low_precision(reason="bf16/f16 stores of fp32-computed values"):
            block.tensor(replay("T"))
            block.vector(replay("V"))
            block.scalar(replay("S"))
            block.gpsimd(replay("G"))
            block.sync(replay("Y"))

    nc._n_ops = n_ops
    return nc


def kernel(L0_attn_x, L0_attn_Wq, L0_attn_Wk, L0_attn_Wv, L0_attn_Wr,
           L0_ffn_Wq, L0_ffn_Wkv, L0_ffn_Ws, L0_ffn_Wd, L0_ffn_Wo):
    if "nc" not in _CACHE:
        _CACHE["nc"] = build_nc()
    nc = _CACHE["nc"]
    x = np.ascontiguousarray(L0_attn_x, dtype=np.float32)
    shards = x.reshape(NC, BL, T, D)
    w = {
        "awq": np.ascontiguousarray(L0_attn_Wq, dtype=np.float32),
        "awk": np.ascontiguousarray(L0_attn_Wk, dtype=np.float32),
        "awv": np.ascontiguousarray(L0_attn_Wv, dtype=np.float32),
        "awr": np.ascontiguousarray(L0_attn_Wr, dtype=np.float32),
        "fwq": np.ascontiguousarray(L0_ffn_Wq, dtype=np.float32),
        "fwkv": np.ascontiguousarray(L0_ffn_Wkv, dtype=np.float32),
        "fws": np.ascontiguousarray(L0_ffn_Ws, dtype=np.float32),
        "fwd": np.ascontiguousarray(L0_ffn_Wd, dtype=np.float32),
        "fwo": np.ascontiguousarray(L0_ffn_Wo, dtype=np.float32),
    }
    in_maps = [dict(w, x=np.ascontiguousarray(shards[i])) for i in range(NC)]
    res = run_bass_kernel_spmd(nc, in_maps, core_ids=list(range(NC)))
    out = np.concatenate([r["y"] for r in res.results], axis=0)
    return out.reshape(B, T, D).astype(np.float32)


# version 05
# speedup vs baseline: 1.1482x; 1.0834x over previous
import numpy as np
from contextlib import ExitStack
import concourse.bass as bass
import concourse.mybir as mybir
from concourse.bass_utils import run_bass_kernel_spmd

F32 = mybir.dt.float32
BF16 = mybir.dt.bfloat16
F16 = mybir.dt.float16
AF = mybir.ActivationFunctionType
ALU = mybir.AluOpType
AX = mybir.AxisListType

NC = 8
B = 16
BL = 2
T = 2048
NT = 16
D = 1024
KT = 8
H = 8
E = 5
DH = 64
IC = 256
NICH = T // IC
SC_F = DH ** -0.5
SC_A = D ** -0.5

STOP_AFTER = 3

_CACHE = {}


class Buf:
    __slots__ = ("w", "r")

    def __init__(self, edges=()):
        self.w = list(edges)
        self.r = []


class Sch:
    def __init__(self):
        self.streams = {k: [] for k in ("T", "V", "S", "G", "Y")}
        self.cnt = {k: 0 for k in ("t", "v", "s", "g", "dw", "dy")}
        self.esem = {"T": "t", "V": "v", "S": "s", "G": "g", "Y": None}

    def snap(self):
        return [(k, v) for k, v in self.cnt.items() if v > 0]

    def _waits(self, reads, writes):
        waits = {}
        for b in reads:
            for (s, v) in b.w:
                waits[s] = max(waits.get(s, 0), v)
        for b in writes:
            for (s, v) in b.w:
                waits[s] = max(waits.get(s, 0), v)
            for (s, v) in b.r:
                waits[s] = max(waits.get(s, 0), v)
        return sorted(waits.items())

    def op(self, eng, fn, reads=(), writes=(), dma=None):
        waits = self._waits(reads, writes)
        if dma is not None:
            self.cnt[dma] += 16
            edge = (dma, self.cnt[dma], 16)
        else:
            s = self.esem[eng]
            self.cnt[s] += 1
            edge = (s, self.cnt[s], 1)
        self.streams[eng].append((fn, waits, edge))
        for b in reads:
            b.r.append(edge[:2])
        for b in writes:
            b.w = [edge[:2]]
            b.r = []
        return edge

    def group(self, eng, fns, reads=(), writes=()):
        waits = self._waits(reads, writes)
        s = self.esem[eng]
        self.cnt[s] += 1
        edge = (s, self.cnt[s], 1)
        for i, fn in enumerate(fns):
            self.streams[eng].append(
                (fn, waits if i == 0 else [], edge if i == len(fns) - 1 else None))
        for b in reads:
            b.r.append(edge[:2])
        for b in writes:
            b.w = [edge[:2]]
            b.r = []
        return edge


def build_nc():
    nc = bass.Bass("TRN2", target_bir_lowering=False)

    x_in = nc.dram_tensor("x", [BL, T, D], F32, kind="ExternalInput").ap()
    y_out = nc.dram_tensor("y", [BL, T, D], F32, kind="ExternalOutput").ap()
    aWq = nc.dram_tensor("awq", [D, D], F32, kind="ExternalInput").ap()
    aWk = nc.dram_tensor("awk", [D, D], F32, kind="ExternalInput").ap()
    aWv = nc.dram_tensor("awv", [D, D], F32, kind="ExternalInput").ap()
    aWr = nc.dram_tensor("awr", [D, D], F32, kind="ExternalInput").ap()
    fWq = nc.dram_tensor("fwq", [D, H * E * DH], F32, kind="ExternalInput").ap()
    fWkv = nc.dram_tensor("fwkv", [D, 2 * H * E * DH], F32, kind="ExternalInput").ap()
    fWs = nc.dram_tensor("fws", [D, H * E], F32, kind="ExternalInput").ap()
    fWd = nc.dram_tensor("fwd", [D, H * E], F32, kind="ExternalInput").ap()
    fWo = nc.dram_tensor("fwo", [DH, E * D], F32, kind="ExternalInput").ap()
    qspill = nc.dram_tensor("qspill", [T, D], F16, kind="Internal").ap()

    x_t = x_in.rearrange("b (n p) d -> p (b n) d", p=128)
    y_t = y_out.rearrange("b (n p) d -> p (b n) d", p=128)
    qsp_t = qspill.rearrange("(n p) d -> p n d", p=128)

    ARENA_BYTES = 105472

    with ExitStack() as _es:
        en = _es.enter_context
        xsb = en(nc.sbuf_tensor("xsb", [128, BL * NT, D], F16))
        xnT = en(nc.sbuf_tensor("xnT", [128, KT, T], BF16))
        woT = en(nc.sbuf_tensor("woT", [128, 3, D], BF16))
        wsd = en(nc.sbuf_tensor("wsd", [128, KT, 2 * H * E], BF16))
        identb = en(nc.sbuf_tensor("identb", [128, 128], BF16))
        identf = en(nc.sbuf_tensor("identf", [1, 1], F32))
        onesb = en(nc.sbuf_tensor("onesb", [128, 1], BF16))
        ones1 = en(nc.sbuf_tensor("ones1", [1, 128], F32))
        epsv = en(nc.sbuf_tensor("epsv", [128, 1], F32))
        stats = en(nc.sbuf_tensor("stats", [128, NT, 8], F32))
        arena_t = en(nc.sbuf_tensor("arena", [128, ARENA_BYTES], mybir.dt.uint8))
        PA = en(nc.psum_tensor("PA", [128, 1024], F32))
        PB = en(nc.psum_tensor("PB", [128, 1024], F32))
        PP = en(nc.psum_tensor("PP", [128, 1024], F32))
        PT0 = en(nc.psum_tensor("PT0", [128, 1024], BF16))
        PT1 = en(nc.psum_tensor("PT1", [128, 1024], BF16))
        sem_t = en(nc.semaphore("t"))
        sem_v = en(nc.semaphore("v"))
        sem_s = en(nc.semaphore("s"))
        sem_g = en(nc.semaphore("g"))
        sem_dw = en(nc.semaphore("dw"))
        sem_dy = en(nc.semaphore("dy"))
        block = en(nc.Block())
        S = Sch()
        sems = {"t": sem_t, "v": sem_v, "s": sem_s, "g": sem_g,
                "dw": sem_dw, "dy": sem_dy}
        arena = arena_t.ap()

        def carve(off, shape, dtype):
            nel = 1
            for d_ in shape:
                nel *= d_
            nbytes = nel * mybir.dt.size(dtype)
            assert off % 64 == 0 and off + nbytes <= ARENA_BYTES, (off, nbytes)
            sl = arena[:, off:off + nbytes].bitcast(dtype)
            if len(shape) > 1:
                names = " ".join(f"a{i}" for i in range(len(shape)))
                kw = {f"a{i}": shape[i] for i in range(len(shape))}
                sl = sl.rearrange(f"p ({names}) -> p {names}", **kw)
            return sl

        PTT = (PT0, PT1)
        PX2 = (PA, PB)

        xb = [Buf() for _ in range(BL * NT)]
        xnTb = [Buf() for _ in range(NT)]
        woTb, wsdb = Buf(), Buf()
        constb = Buf()
        statsb = [[Buf() for _ in range(8)] for _ in range(NT)]
        pa = [Buf(), Buf()]
        pb = [Buf(), Buf()]
        pp = [Buf(), Buf()]
        pt = [Buf(), Buf()]
        qspb = [Buf() for _ in range(NT)]

        S.op("G", lambda g: g.memset(onesb[:], 1.0), writes=[constb])
        S.op("G", lambda g: g.memset(ones1[:], 1.0), writes=[constb])
        S.op("G", lambda g: g.memset(epsv[:], 1e-5), writes=[constb])
        S.op("G", lambda g: g.memset(identf[:], 1.0), writes=[constb])
        S.op("G", lambda g: g.memset(identb[:], 0.0), writes=[constb])
        S.op("G", lambda g: g.affine_select(
            out=identb[:], in_=identb[:], compare_op=ALU.not_equal, fill=1.0,
            base=0, pattern=[[-1, 128]], channel_multiplier=1), writes=[constb])

        for i in range(4):
            S.op("G", lambda g, i=i: g.dma_start(
                out=xsb[:, i * 8:(i + 1) * 8, :], in_=x_t[:, i * 8:(i + 1) * 8, :]),
                writes=[xb[j] for j in range(i * 8, (i + 1) * 8)], dma="dw")
        S.op("G", lambda g: g.dma_start(
            out=wsd[:, :, 0:H * E],
            in_=fWs.rearrange("(kt p) m -> p kt m", p=128)), writes=[wsdb], dma="dw")
        S.op("G", lambda g: g.dma_start(
            out=wsd[:, :, H * E:2 * H * E],
            in_=fWd.rearrange("(kt p) m -> p kt m", p=128)), writes=[wsdb], dma="dw")
        for e_ in range(E):
            S.op("G", lambda g, e_=e_: g.dma_start(
                out=woT[(e_ % 2) * 64:(e_ % 2) * 64 + 64, e_ // 2, :],
                in_=fWo[:, e_ * D:(e_ + 1) * D]), writes=[woTb], dma="dw")

        def emit_ln_stats(b):
            xn = carve(0, (2, D), BF16)
            sq = carve(4096, (2, D), BF16)
            snap = S.snap()
            st = {"xn": xn, "xnb": [Buf(snap), Buf(snap)],
                  "sqb": [Buf(snap), Buf(snap)], "b": b}
            for n in range(NT):
                tt = b * NT + n
                S.op("V", lambda v, tt=tt, n=n: v.reduce_sum(
                    stats[:, n, 0:1], xsb[:, tt, :], axis=AX.X),
                    reads=[xb[tt]], writes=[statsb[n][0]])
                S.op("S", lambda s, tt=tt, n=n, sq=sq: s.activation(
                    out=sq[:, n % 2, :], in_=xsb[:, tt, :], func=AF.Square,
                    accum_out=stats[:, n, 1:2]),
                    reads=[xb[tt]], writes=[st["sqb"][n % 2], statsb[n][1]])
            for n in range(NT):
                S.op("V", lambda v, n=n: v.tensor_scalar_mul(
                    stats[:, n, 2:3], stats[:, n, 0:1], 1.0 / D),
                    reads=[statsb[n][0]], writes=[statsb[n][2]])
            for n in range(NT):
                S.op("V", lambda v, n=n: v.tensor_mul(
                    stats[:, n, 3:4], stats[:, n, 2:3], stats[:, n, 2:3]),
                    reads=[statsb[n][2]], writes=[statsb[n][3]])
            for n in range(NT):
                S.op("V", lambda v, n=n: v.scalar_tensor_tensor(
                    out=stats[:, n, 4:5], in0=stats[:, n, 1:2], scalar=1.0 / D,
                    in1=stats[:, n, 3:4], op0=ALU.mult, op1=ALU.subtract),
                    reads=[statsb[n][1], statsb[n][3]], writes=[statsb[n][4]])
            for n in range(NT):
                S.op("S", lambda s, n=n: s.activation(
                    out=stats[:, n, 5:6], in_=stats[:, n, 4:5], func=AF.Ln,
                    bias=epsv[:, :], scale=1.0),
                    reads=[statsb[n][4], constb], writes=[statsb[n][5]])
            for n in range(NT):
                S.op("S", lambda s, n=n: s.activation(
                    out=stats[:, n, 6:7], in_=stats[:, n, 5:6], func=AF.Exp,
                    scale=-0.5), reads=[statsb[n][5]], writes=[statsb[n][6]])
            for n in range(NT):
                S.op("V", lambda v, n=n: v.tensor_mul(
                    stats[:, n, 7:8], stats[:, n, 2:3], stats[:, n, 6:7]),
                    reads=[statsb[n][2], statsb[n][6]], writes=[statsb[n][7]])
            return st

        def emit_ln_apply(st):
            b = st["b"]
            xn = st["xn"]
            xnb = st["xnb"]
            for n in range(NT):
                tt = b * NT + n
                mrsb = stats[:, n, 7:8].broadcast_to([128, D])
                S.op("V", lambda v, tt=tt, n=n, mrsb=mrsb, xn=xn: v.scalar_tensor_tensor(
                    out=xn[:, n % 2, :], in0=xsb[:, tt, :], scalar=stats[:, n, 6:7],
                    in1=mrsb, op0=ALU.mult, op1=ALU.subtract),
                    reads=[xb[tt], statsb[n][6], statsb[n][7]], writes=[xnb[n % 2]])
                for half in range(2):
                    hv = (n * 2 + half) % 2
                    S.group("T", [
                        (lambda te, kt=kt, half=half, n=n, hv=hv, xn=xn: te.transpose(
                            PTT[hv][:, kt * 128:(kt + 1) * 128],
                            xn[:, n % 2, (half * 4 + kt) * 128:(half * 4 + kt + 1) * 128],
                            identb[:]))
                        for kt in range(4)],
                        reads=[xnb[n % 2], constb], writes=[pt[hv]])
                    S.op("S", lambda s, half=half, n=n, hv=hv: s.activation(
                        out=xnT[:, half * 4:(half + 1) * 4, n * 128:(n + 1) * 128],
                        in_=PTT[hv][:, 0:512].rearrange("p (a c) -> p a c", c=128),
                        func=AF.Copy),
                        reads=[pt[hv]], writes=[xnTb[n]])

        def emit_ln(b):
            emit_ln_apply(emit_ln_stats(b))

        def emit_ffn(b, call_idx, ln_st0=None, ln_next=None):
            snap0 = S.snap()
            if ln_st0 is None:
                emit_ln(b)
            ez = carve(8192, (NT, 2 * H * E), BF16)
            wrk = carve(10752, (5, 256), F32)
            ssd = carve(15872, (NT, 2 * H * E), F32)
            med = carve(20992, (NT, 2 * H), F32)
            msk = carve(22016, (NT, 2 * H * E), F32)
            gated = carve(27136, (NT, 2 * H * E), BF16)
            whead = carve(29696, (KT, 3 * E * DH), BF16)
            qkT = carve(45056, (2, 2, T), BF16)
            eT = carve(61440, (NT, 512), BF16)
            vext = carve(77824, (2, 2, NT, 66), BF16)
            gacc = carve(86272, (NT, E * DH), BF16)
            otb = carve(96512, (2, 512), BF16)
            onrm = carve(98560, (2, 2, DH), BF16)
            orcp = carve(99072, (2, 2, 1), F32)
            snap = S.snap()
            ezb, wrkb, ssdb = Buf(snap), Buf(snap), Buf(snap)
            medb_, mskb, gatedb = Buf(snap), Buf(snap), Buf(snap)
            wheadb = Buf(snap0)
            qkTb = [Buf(snap), Buf(snap)]
            eTb = [Buf(snap) for _ in range(NT)]
            vextb = [Buf(snap), Buf(snap)]
            gaccb = [[Buf(snap) for _ in range(E)] for _ in range(NT)]
            otbb = [Buf(snap), Buf(snap)]
            onrmb = {(hp, it): Buf(snap) for hp in range(2) for it in range(2)}
            orcpb = {(hp, it): Buf(snap) for hp in range(2) for it in range(2)}

            def dma_whead(h):
                c0 = h * E * DH
                S.op("G", lambda g, c0=c0: g.dma_start(
                    out=whead[:, :, 0:320],
                    in_=fWq[:, c0:c0 + 320].rearrange("(kt p) m -> p kt m", p=128)),
                    writes=[wheadb], dma="dw")
                S.op("G", lambda g, c0=c0: g.dma_start(
                    out=whead[:, :, 320:640],
                    in_=fWkv[:, c0:c0 + 320].rearrange("(kt p) m -> p kt m", p=128)),
                    writes=[wheadb], dma="dw")
                S.op("G", lambda g, c0=c0: g.dma_start(
                    out=whead[:, :, 640:960],
                    in_=fWkv[:, 2560 + c0:2560 + c0 + 320].rearrange(
                        "(kt p) m -> p kt m", p=128)),
                    writes=[wheadb], dma="dw")

            dma_whead(0)
            for n in range(NT):
                ppx = pp[n % 2]
                S.group("T", [
                    (lambda te, kt=kt, n=n: te.matmul(
                        PP[:, (n % 2) * 512:(n % 2) * 512 + 2 * H * E],
                        xnT[:, kt, n * 128:(n + 1) * 128], wsd[:, kt, :],
                        start=(kt == 0), stop=(kt == KT - 1)))
                    for kt in range(KT)],
                    reads=[xnTb[n], wsdb], writes=[ppx])
                S.op("S", lambda s, n=n: s.activation(
                    out=ez[:, n, :], in_=PP[:, (n % 2) * 512:(n % 2) * 512 + 2 * H * E],
                    func=AF.Exp, scale=-1.0), reads=[ppx], writes=[ezb])
            ezf = ez.rearrange("p a c -> p (a c)")
            ssdf = ssd.rearrange("p a c -> p (a c)")
            wrkf = wrk.rearrange("p a c -> p (a c)")
            S.op("V", lambda v: v.tensor_scalar_add(wrkf, ezf, 1.0),
                 reads=[ezb], writes=[wrkb])
            S.op("V", lambda v: v.reciprocal(ssdf, wrkf),
                 reads=[wrkb], writes=[ssdb])
            sv = ssd.rearrange("p t (gh e) -> p t gh e", e=E)
            slc = [sv[:, :, :, i] for i in range(E)]
            a_ = [wrk[:, i, :].rearrange("p (t gh) -> p t gh", t=NT) for i in range(5)]
            m5 = [Buf(snap) for _ in range(5)]
            medv = med
            S.op("V", lambda v: v.tensor_tensor(a_[0], slc[0], slc[1], ALU.min),
                 reads=[ssdb], writes=[m5[0]])
            S.op("V", lambda v: v.tensor_tensor(a_[1], slc[0], slc[1], ALU.max),
                 reads=[ssdb], writes=[m5[1]])
            S.op("V", lambda v: v.tensor_tensor(a_[2], slc[2], slc[3], ALU.min),
                 reads=[ssdb], writes=[m5[2]])
            S.op("V", lambda v: v.tensor_tensor(a_[3], slc[2], slc[3], ALU.max),
                 reads=[ssdb], writes=[m5[3]])
            S.op("V", lambda v: v.tensor_tensor(a_[4], a_[0], a_[2], ALU.max),
                 reads=[m5[0], m5[2]], writes=[m5[4]])
            S.op("V", lambda v: v.tensor_tensor(a_[0], a_[1], a_[3], ALU.min),
                 reads=[m5[1], m5[3]], writes=[m5[0]])
            S.op("V", lambda v: v.tensor_tensor(a_[1], a_[4], a_[0], ALU.min),
                 reads=[m5[4], m5[0]], writes=[m5[1]])
            S.op("V", lambda v: v.tensor_tensor(a_[2], a_[4], a_[0], ALU.max),
                 reads=[m5[4], m5[0]], writes=[m5[2]])
            S.op("V", lambda v: v.tensor_tensor(a_[3], slc[4], a_[2], ALU.min),
                 reads=[ssdb, m5[2]], writes=[m5[3]])
            S.op("V", lambda v: v.tensor_tensor(medv, a_[1], a_[3], ALU.max),
                 reads=[m5[1], m5[3]], writes=[medb_])
            medbx = med.unsqueeze(3).broadcast_to([128, NT, 2 * H, E])
            mskv = msk.rearrange("p t (gh e) -> p t gh e", e=E)
            S.op("V", lambda v: v.tensor_tensor(mskv, sv, medbx, ALU.is_ge),
                 reads=[ssdb, medb_], writes=[mskb])
            S.op("V", lambda v: v.tensor_tensor(
                gated.rearrange("p a c -> p (a c)"), ssdf,
                msk.rearrange("p a c -> p (a c)"), ALU.mult),
                reads=[ssdb, mskb], writes=[gatedb])
            S.op("V", lambda v: v.memset(gacc.rearrange("p a c -> p (a c)"), 0.0),
                 writes=[gaccb[n][e] for n in range(NT) for e in range(E)])
            S.op("V", lambda v: v.memset(vext[:, :, :, :, 64:66], 0.0),
                 writes=[vextb[0], vextb[1]])
            S.op("V", lambda v: v.memset(vext[:, :, :, :, 64:65], 1.0),
                 writes=[vextb[0], vextb[1]])

            qg = carve(10752, (3, 3 * E * DH), BF16)
            qka = carve(0, (NT, 256), BF16)
            snap2 = S.snap()
            qgb = [Buf(snap2), Buf(snap2), Buf(snap2)]
            qkab = [Buf(snap2) for _ in range(NT)]

            prep_pend = []

            def prep_mults(h, n):
                sl = n % 3
                for c, (lo, hi) in enumerate(((0, 512), (512, 960))):
                    ppx = pp[c]
                    S.group("T", [
                        (lambda te, kt=kt, n=n, lo=lo, hi=hi, c=c: te.matmul(
                            PP[:, c * 512:c * 512 + (hi - lo)],
                            xnT[:, kt, n * 128:(n + 1) * 128],
                            whead[:, kt, lo:hi],
                            start=(kt == 0), stop=(kt == KT - 1)))
                        for kt in range(KT)],
                        reads=[xnTb[n], wheadb], writes=[ppx])
                    for s0, s1, gsel in ((0, 320, 1), (320, 640, 0), (640, 960, 0)):
                        a0_, b0_ = max(lo, s0), min(hi, s1)
                        if a0_ >= b0_:
                            continue
                        ne = (b0_ - a0_) // DH
                        e0 = (a0_ - s0) // DH
                        gbase = (H * E if gsel else 0) + h * E + e0
                        gapb = gated[:, n, gbase:gbase + ne].unsqueeze(2) \
                            .broadcast_to([128, ne, DH])
                        pin = PP[:, c * 512 + (a0_ - lo):c * 512 + (b0_ - lo)] \
                            .rearrange("p (e d) -> p e d", d=DH)
                        qout = qg[:, sl, a0_:b0_].rearrange("p (e d) -> p e d", d=DH)
                        S.op("V", lambda v, pin=pin, gapb=gapb, qout=qout:
                             v.tensor_tensor(qout, pin, gapb, ALU.mult),
                             reads=[ppx, gatedb], writes=[qgb[sl]])

            def prep_tail(h, n):
                sl = n % 3
                p_ = h // 2
                pbuf = p_ % 2
                hp = h % 2
                S.op("V", lambda v, n=n, hp=hp, sl=sl: v.tensor_reduce(
                    qka[:, n, hp * 64:hp * 64 + 64],
                    qg[:, sl, 0:320].rearrange("p (e d) -> p d e", d=DH),
                    axis=AX.X, op=ALU.add),
                    reads=[qgb[sl]], writes=[qkab[n]])
                S.op("V", lambda v, n=n, hp=hp, sl=sl: v.tensor_reduce(
                    qka[:, n, 128 + hp * 64:128 + hp * 64 + 64],
                    qg[:, sl, 320:640].rearrange("p (e d) -> p d e", d=DH),
                    axis=AX.X, op=ALU.add),
                    reads=[qgb[sl]], writes=[qkab[n]])
                S.op("V", lambda v, n=n, hp=hp, pbuf=pbuf, sl=sl: v.tensor_reduce(
                    vext[:, pbuf, hp, n, 0:64],
                    qg[:, sl, 640:960].rearrange("p (e d) -> p d e", d=DH),
                    axis=AX.X, op=ALU.add),
                    reads=[qgb[sl]], writes=[vextb[pbuf]])
                if hp == 1:
                    for qk in range(2):
                        hv = (n + qk) % 2
                        S.op("T", lambda te, qk=qk, n=n, hv=hv: te.transpose(
                            PTT[hv][:, 0:128], qka[:, n, qk * 128:(qk + 1) * 128],
                            identb[:]),
                            reads=[qkab[n], constb], writes=[pt[hv]])
                        S.op("S", lambda s, qk=qk, n=n, pbuf=pbuf, hv=hv:
                             s.activation(
                                 out=qkT[:, pbuf, qk, n * 128:(n + 1) * 128],
                                 in_=PTT[hv][:, 0:128], func=AF.Copy),
                             reads=[pt[hv]], writes=[qkTb[pbuf]])

            def emit_prep_tiles(h, tiles):
                if prep_pend and prep_pend[0][0] != h:
                    while prep_pend:
                        prep_tail(*prep_pend.pop(0))
                for n in tiles:
                    prep_mults(h, n)
                    prep_pend.append((h, n))
                    if len(prep_pend) > 2:
                        prep_tail(*prep_pend.pop(0))

            def emit_o(p_, hp, ic, j):
                pbuf = p_ % 2
                S.op("T", lambda te, j=j, hp=hp, ic=ic, pbuf=pbuf: te.matmul(
                    PB[0:65, (ic % 2) * 512:(ic % 2) * 512 + 512],
                    vext[:, pbuf, hp, j, 0:65], eT[:, j, :],
                    start=(j == 0), stop=(j == NT - 1), skip_group_check=True),
                    reads=[vextb[pbuf], eTb[j]], writes=[pb[ic % 2]])

            def emit_attn_chunk(p_, cc):
                hp, ic = cc // 4, cc % 4
                pbuf = p_ % 2
                i0 = ic * 512
                for j in range(NT):
                    half = j % 2
                    pax = pa[half]
                    S.op("T", lambda te, j=j, hp=hp, half=half, i0=i0, pbuf=pbuf:
                         te.matmul(
                             PA[:, half * 512:(half + 1) * 512],
                             qkT[64 * hp:64 * (hp + 1), pbuf, 1, j * 128:(j + 1) * 128],
                             qkT[64 * hp:64 * (hp + 1), pbuf, 0, i0:i0 + 512],
                             start=True, stop=True, tile_position=(64 * hp, 0)),
                         reads=[qkTb[pbuf]], writes=[pax])
                    S.op("S", lambda s, j=j, half=half: s.activation(
                        out=eT[:, j, :],
                        in_=PA[:, half * 512:(half + 1) * 512], func=AF.Exp,
                        scale=SC_F), reads=[pax], writes=[eTb[j]])
                    if j == 2 and cc > 0:
                        emit_evac(p_, cc - 1)
                    if j >= 1:
                        emit_o(p_, hp, ic, j - 1)
                emit_o(p_, hp, ic, NT - 1)
                if cc == 7:
                    emit_evac(p_, cc)

            def emit_evac(p_, cc):
                hp, ic = cc // 4, cc % 4
                slot = cc % 2
                S.op("S", lambda s, ic=ic, slot=slot: s.activation(
                    out=otb[0:65, slot, :],
                    in_=PB[0:65, (ic % 2) * 512:(ic % 2) * 512 + 512],
                    func=AF.Copy),
                    reads=[pb[ic % 2]], writes=[otbb[slot]])
                h = p_ * 2 + hp
                for it in range(4):
                    n = ic * 4 + it
                    hv = it % 2
                    ons = (it // 2, it % 2)
                    S.op("T", lambda te, it=it, slot=slot, hv=hv: te.transpose(
                        PTT[hv][:, 0:65],
                        otb[0:65, slot, it * 128:it * 128 + 128],
                        identb[0:65, 0:65]),
                        reads=[otbb[slot], constb], writes=[pt[hv]])
                    S.op("V", lambda v, ons=ons, hv=hv: v.reciprocal(
                        orcp[:, ons[0], ons[1]:ons[1] + 1], PTT[hv][:, 64:65]),
                        reads=[pt[hv]], writes=[orcpb[ons]])
                    S.op("V", lambda v, ons=ons, hv=hv: v.tensor_scalar_mul(
                        onrm[:, ons[0], ons[1], :], PTT[hv][:, 0:64],
                        orcp[:, ons[0], ons[1]:ons[1] + 1]),
                        reads=[pt[hv], orcpb[ons]], writes=[onrmb[ons]])
                    for e in range(E):
                        S.op("V", lambda v, n=n, e=e, h=h, ons=ons:
                             v.scalar_tensor_tensor(
                                 out=gacc[:, n, e * DH:(e + 1) * DH],
                                 in0=onrm[:, ons[0], ons[1], :],
                                 scalar=msk[:, n, H * E + h * E + e:H * E + h * E + e + 1],
                                 in1=gacc[:, n, e * DH:(e + 1) * DH],
                                 op0=ALU.mult, op1=ALU.add),
                             reads=[onrmb[ons], mskb, gaccb[n][e]],
                             writes=[gaccb[n][e]])

            emit_prep_tiles(0, range(NT))
            dma_whead(1)
            emit_prep_tiles(1, range(NT))
            for p_ in range(4):
                while prep_pend:
                    prep_tail(*prep_pend.pop(0))
                for cc in range(8):
                    emit_attn_chunk(p_, cc)
                    if p_ < 3:
                        h2 = 2 * (p_ + 1)
                        if cc == 0:
                            dma_whead(h2)
                        if cc < 4:
                            emit_prep_tiles(h2, range(cc * 4, cc * 4 + 4))
                        if cc == 3:
                            dma_whead(h2 + 1)
                        if cc >= 4:
                            emit_prep_tiles(h2 + 1, range((cc - 4) * 4, (cc - 4) * 4 + 4))

            ln_st = None
            if ln_next is not None:
                ln_st = emit_ln_stats(ln_next)

            out_sb = carve(8192, (2, D), F32)
            gt = carve(16448, (2, 3, 128), BF16)
            snap3 = S.snap()
            outb = [Buf(snap3), Buf(snap3)]
            gtb = [Buf(snap3), Buf(snap3)]
            last = (call_idx == 2)
            for n in range(NT):
                slot = n % 2
                for kt in range(3):
                    w_ = 128 if kt < 2 else 64
                    hv = kt % 2
                    S.op("T", lambda te, n=n, kt=kt, w_=w_, hv=hv: te.transpose(
                        PTT[hv][0:w_, 0:128], gacc[:, n, kt * 128:kt * 128 + w_],
                        identb[:]),
                        reads=[gaccb[n][ee] for ee in range(E)] + [constb],
                        writes=[pt[hv]])
                    S.op("S", lambda s, n=n, kt=kt, w_=w_, slot=slot, hv=hv:
                         s.activation(out=gt[0:w_, slot, kt, :],
                                      in_=PTT[hv][0:w_, 0:128], func=AF.Copy),
                         reads=[pt[hv]], writes=[gtb[slot]])
                pax = (pa, pb)[n % 2]
                PX = PX2[n % 2]
                for c in range(2):
                    S.group("T", [
                        (lambda te, kt=kt, c=c, slot=slot, PX=PX: te.matmul(
                            PX[:, c * 512:(c + 1) * 512],
                            gt[0:(128 if kt < 2 else 64), slot, kt, :],
                            woT[0:(128 if kt < 2 else 64), kt, c * 512:(c + 1) * 512],
                            start=(kt == 0), stop=(kt == 2),
                            tile_position=((0, 0) if kt == 2 else None)))
                        for kt in range(3)],
                        reads=[gtb[slot], woTb], writes=[pax[c]])
                tt = b * NT + n
                if last:
                    S.op("V", lambda v, tt=tt, slot=slot, PX=PX: v.scalar_tensor_tensor(
                        out=out_sb[:, slot, :], in0=PX[:, :], scalar=0.5,
                        in1=xsb[:, tt, :], op0=ALU.mult, op1=ALU.add),
                        reads=[pax[0], pax[1], xb[tt]], writes=[outb[slot]])
                    S.op("Y", lambda y, tt=tt, slot=slot: y.dma_start(
                        out=y_t[:, tt, :], in_=out_sb[:, slot, :]),
                        reads=[outb[slot]], dma="dy")
                else:
                    S.op("V", lambda v, tt=tt, PX=PX: v.scalar_tensor_tensor(
                        out=xsb[:, tt, :], in0=PX[:, :], scalar=0.5,
                        in1=xsb[:, tt, :], op0=ALU.mult, op1=ALU.add),
                        reads=[pax[0], pax[1], xb[tt]], writes=[xb[tt]])
            if ln_st is not None:
                emit_ln_apply(ln_st)
            return ln_st

        def emit_attn_sub(b, ln_st0=None, ln_next=None):
            snap0 = S.snap()
            if ln_st0 is None:
                emit_ln(b)
            aw = carve(8192, (2, KT, D), BF16)
            uT = carve(40960, (KT, T), BF16)
            qa = carve(73728, (2, D), BF16)
            psb = carve(77824, (2, D), F32)
            pexp = carve(86016, (2, D), BF16)
            q16 = carve(90112, (2, D), F16)
            gqb_s = carve(94208, (D,), F32)
            gq_s = carve(98304, (D,), F32)
            gkT_s = carve(102400, (KT,), F32)
            rcp = carve(102464, (2, 1), F32)
            snap = S.snap()
            awb = [Buf(snap0), Buf(snap0)]
            uTb = Buf(snap)
            qab = [Buf(snap), Buf(snap)]
            psbb = [Buf(snap), Buf(snap)]
            pexpb = [Buf(snap), Buf(snap)]
            q16b = [Buf(snap), Buf(snap)]
            gqbb = Buf(snap)
            gkTb = Buf(snap)
            gqsb = Buf(snap)
            rcpb = [Buf(snap), Buf(snap)]

            def dma_aw(w, slot):
                S.op("G", lambda g, w=w, slot=slot: g.dma_start(
                    out=aw[:, slot, :, :],
                    in_=w.rearrange("(kt p) m -> p kt m", p=128)),
                    writes=[awb[slot]], dma="dw")

            dma_aw(aWq, 0)
            dma_aw(aWk, 1)

            for n in range(NT):
                pax = (pa, pb)[n % 2]
                PX = PX2[n % 2]
                slot = n % 2
                for c in range(2):
                    S.group("T", [
                        (lambda te, kt=kt, n=n, c=c, PX=PX: te.matmul(
                            PX[:, c * 512:(c + 1) * 512],
                            xnT[:, kt, n * 128:(n + 1) * 128],
                            aw[:, 0, kt, c * 512:(c + 1) * 512],
                            start=(kt == 0), stop=(kt == KT - 1)))
                        for kt in range(KT)],
                        reads=[xnTb[n], awb[0]], writes=[pax[c]])
                S.op("S", lambda s, n=n, slot=slot, PX=PX: s.activation(
                    out=pexp[:, slot, :], in_=PX[:, :], func=AF.Exp, scale=SC_A,
                    accum_out=stats[:, n, 0:1]),
                    reads=[pax[0], pax[1]], writes=[pexpb[slot], statsb[n][0]])
                S.op("S", lambda s, n=n, slot=slot, PX=PX: s.activation(
                    out=q16[:, slot, :], in_=PX[:, :], func=AF.Copy),
                    reads=[pax[0], pax[1]], writes=[q16b[slot]])
                S.op("Y", lambda y, n=n, slot=slot: y.dma_start(
                    out=qsp_t[:, n, :], in_=q16[:, slot, :]),
                    reads=[q16b[slot]], writes=[qspb[n]], dma="dy")
                S.op("V", lambda v, n=n, slot=slot: v.reciprocal(
                    rcp[:, slot, :], stats[:, n, 0:1]),
                    reads=[statsb[n][0]], writes=[rcpb[slot]])
                S.op("V", lambda v, n=n, slot=slot, PX=PX: v.scalar_tensor_tensor(
                    out=qa[:, slot, :], in0=pexp[:, slot, :], scalar=rcp[:, slot, 0:1],
                    in1=PX[:, :], op0=ALU.mult, op1=ALU.mult),
                    reads=[pexpb[slot], rcpb[slot], pax[0], pax[1]],
                    writes=[qab[slot]])
                for m_ in ([n - 1] if n > 0 else []) + ([n] if n == NT - 1 else []):
                    msl = m_ % 2
                    for c in range(2):
                        S.op("T", lambda te, m_=m_, c=c, msl=msl: te.matmul(
                            PP[0:1, c * 512:(c + 1) * 512], onesb[:, :],
                            qa[:, msl, c * 512:(c + 1) * 512],
                            start=(m_ == 0), stop=(m_ == NT - 1),
                            skip_group_check=True),
                            reads=[qab[msl], constb], writes=[pp[c]])
            S.op("S", lambda s: s.activation(out=gq_s[0:1, :], in_=PP[0:1, :],
                                             func=AF.Copy),
                 reads=[pp[0], pp[1]], writes=[gqsb])
            for c in range(2):
                S.op("T", lambda te, c=c: te.matmul(
                    PA[:, c * 512:(c + 1) * 512], ones1[:, :],
                    gq_s[0:1, c * 512:(c + 1) * 512], start=True, stop=True),
                    reads=[gqsb, constb], writes=[pa[c]])
            S.op("V", lambda v: v.tensor_copy(gqb_s[:, :], PA[:, :]),
                 reads=[pa[0], pa[1]], writes=[gqbb])
            dma_aw(aWv, 0)

            for n in range(NT):
                pax = (pa, pb)[n % 2]
                PX = PX2[n % 2]
                slot = n % 2
                for c in range(2):
                    S.group("T", [
                        (lambda te, kt=kt, n=n, c=c, PX=PX: te.matmul(
                            PX[:, c * 512:(c + 1) * 512],
                            xnT[:, kt, n * 128:(n + 1) * 128],
                            aw[:, 1, kt, c * 512:(c + 1) * 512],
                            start=(kt == 0), stop=(kt == KT - 1)))
                        for kt in range(KT)],
                        reads=[xnTb[n], awb[1]], writes=[pax[c]])
                S.op("V", lambda v, slot=slot, PX=PX: v.tensor_tensor(
                    psb[:, slot, :], PX[:, :], gqb_s[:, :], ALU.mult),
                    reads=[pax[0], pax[1], gqbb], writes=[psbb[slot]])
                S.op("S", lambda s, n=n, slot=slot: s.activation(
                    out=pexp[:, slot, :], in_=psb[:, slot, :], func=AF.Exp,
                    scale=SC_A, accum_out=stats[:, n, 1:2]),
                    reads=[psbb[slot]], writes=[pexpb[slot], statsb[n][1]])
                S.op("V", lambda v, n=n, slot=slot: v.reciprocal(
                    rcp[:, slot, :], stats[:, n, 1:2]),
                    reads=[statsb[n][1]], writes=[rcpb[slot]])
                S.op("V", lambda v, slot=slot: v.scalar_tensor_tensor(
                    out=qa[:, slot, :], in0=pexp[:, slot, :], scalar=rcp[:, slot, 0:1],
                    in1=psb[:, slot, :], op0=ALU.mult, op1=ALU.mult),
                    reads=[pexpb[slot], rcpb[slot], psbb[slot]], writes=[qab[slot]])
                for m_ in ([n - 1] if n > 0 else []) + ([n] if n == NT - 1 else []):
                    msl = m_ % 2
                    for c in range(2):
                        S.op("T", lambda te, m_=m_, c=c, msl=msl: te.matmul(
                            PP[0:1, c * 512:(c + 1) * 512], onesb[:, :],
                            qa[:, msl, c * 512:(c + 1) * 512],
                            start=(m_ == 0), stop=(m_ == NT - 1),
                            skip_group_check=True),
                            reads=[qab[msl], constb], writes=[pp[c]])
            S.op("S", lambda s: s.activation(out=gq_s[0:1, :], in_=PP[0:1, :],
                                             func=AF.Copy),
                 reads=[pp[0], pp[1]], writes=[gqsb])
            for kt in range(KT):
                S.op("T", lambda te, kt=kt: te.transpose(
                    PA[:, kt:kt + 1], gq_s[0:1, kt * 128:(kt + 1) * 128],
                    identf[:, :]), reads=[gqsb, constb], writes=[pa[0]])
            S.op("V", lambda v: v.tensor_copy(gkT_s[:, :], PA[:, 0:KT]),
                 reads=[pa[0]], writes=[gkTb])
            dma_aw(aWr, 1)

            for ft in range(KT):
                for ch in range(4):
                    idx = ft * 4 + ch
                    bank = ((pa, pb)[(idx // 2) % 2])[idx % 2]
                    PX = PX2[(idx // 2) % 2]
                    S.group("T", [
                        (lambda te, kt=kt, ft=ft, ch=ch, idx=idx, PX=PX: te.matmul(
                            PX[:, (idx % 2) * 512:(idx % 2) * 512 + 512],
                            aw[:, 0, kt, ft * 128:(ft + 1) * 128],
                            xnT[:, kt, ch * 512:(ch + 1) * 512],
                            start=(kt == 0), stop=(kt == KT - 1)))
                        for kt in range(KT)],
                        reads=[awb[0]] + [xnTb[4 * ch + q] for q in range(4)],
                        writes=[bank])
                    S.op("V", lambda v, ft=ft, ch=ch, idx=idx, PX=PX: v.tensor_scalar_mul(
                        uT[:, ft, ch * 512:(ch + 1) * 512],
                        PX[:, (idx % 2) * 512:(idx % 2) * 512 + 512],
                        gkT_s[:, ft:ft + 1]),
                        reads=[bank, gkTb], writes=[uTb])

            ln_st = None
            if ln_next is not None:
                ln_st = emit_ln_stats(ln_next)

            for n in range(NT):
                slot = n % 2
                S.op("Y", lambda y, n=n, slot=slot: y.dma_start(
                    out=q16[:, slot, :], in_=qsp_t[:, n, :]),
                    reads=[qspb[n]], writes=[q16b[slot]], dma="dy")
                pax = (pa, pb)[n % 2]
                PX = PX2[n % 2]
                for c in range(2):
                    S.group("T", [
                        (lambda te, kt=kt, n=n, c=c, PX=PX: te.matmul(
                            PX[:, c * 512:(c + 1) * 512],
                            uT[:, kt, n * 128:(n + 1) * 128],
                            aw[:, 1, kt, c * 512:(c + 1) * 512],
                            start=(kt == 0), stop=(kt == KT - 1)))
                        for kt in range(KT)],
                        reads=[uTb, awb[1]], writes=[pax[c]])
                tt = b * NT + n
                S.op("V", lambda v, slot=slot, PX=PX: v.scalar_tensor_tensor(
                    out=psb[:, slot, :], in0=PX[:, :], scalar=1.0,
                    in1=q16[:, slot, :], op0=ALU.mult, op1=ALU.add),
                    reads=[pax[0], pax[1], q16b[slot]], writes=[psbb[slot]])
                S.op("V", lambda v, tt=tt, slot=slot: v.tensor_tensor(
                    xsb[:, tt, :], psb[:, slot, :], xsb[:, tt, :], ALU.add),
                    reads=[psbb[slot], xb[tt]], writes=[xb[tt]])
            if ln_st is not None:
                emit_ln_apply(ln_st)
            return ln_st

        phases = [(c, b) for c in range(min(3, STOP_AFTER)) for b in range(BL)]
        carry = None
        for idx, (c, b) in enumerate(phases):
            nxt = phases[idx + 1][1] if idx + 1 < len(phases) else None
            if c == 1:
                carry = emit_attn_sub(b, ln_st0=carry, ln_next=nxt)
            else:
                carry = emit_ffn(b, c, ln_st0=carry, ln_next=nxt)

        if STOP_AFTER < 3:
            for i in range(4):
                S.op("G", lambda g, i=i: g.dma_start(
                    out=y_t[:, i * 8:(i + 1) * 8, :],
                    in_=xsb[:, i * 8:(i + 1) * 8, :]),
                    reads=[xb[j] for j in range(i * 8, (i + 1) * 8)], dma="dw")

        n_ops = {k: len(v) for k, v in S.streams.items()}
        n_ops["sem"] = dict(S.cnt)

        def replay(eng_key):
            def run(e):
                seen = {}
                for (fn, waits, edge) in S.streams[eng_key]:
                    for (sname, val) in waits:
                        if seen.get(sname, 0) < val:
                            e.wait_ge(sems[sname], val)
                            seen[sname] = val
                    ins = fn(e)
                    if edge is not None:
                        ins.then_inc(sems[edge[0]], edge[2])
                if eng_key == "Y" and S.cnt["dy"] > 0:
                    e.wait_ge(sems["dy"], S.cnt["dy"])
                if eng_key == "G" and STOP_AFTER < 3:
                    e.wait_ge(sems["dw"], S.cnt["dw"])
            return run

        with nc.allow_low_precision(reason="bf16/f16 stores of fp32-computed values"):
            block.tensor(replay("T"))
            block.vector(replay("V"))
            block.scalar(replay("S"))
            block.gpsimd(replay("G"))
            block.sync(replay("Y"))

    nc._n_ops = n_ops
    return nc


def kernel(L0_attn_x, L0_attn_Wq, L0_attn_Wk, L0_attn_Wv, L0_attn_Wr,
           L0_ffn_Wq, L0_ffn_Wkv, L0_ffn_Ws, L0_ffn_Wd, L0_ffn_Wo):
    if "nc" not in _CACHE:
        _CACHE["nc"] = build_nc()
    nc = _CACHE["nc"]
    x = np.ascontiguousarray(L0_attn_x, dtype=np.float32)
    shards = x.reshape(NC, BL, T, D)
    w = {
        "awq": np.ascontiguousarray(L0_attn_Wq, dtype=np.float32),
        "awk": np.ascontiguousarray(L0_attn_Wk, dtype=np.float32),
        "awv": np.ascontiguousarray(L0_attn_Wv, dtype=np.float32),
        "awr": np.ascontiguousarray(L0_attn_Wr, dtype=np.float32),
        "fwq": np.ascontiguousarray(L0_ffn_Wq, dtype=np.float32),
        "fwkv": np.ascontiguousarray(L0_ffn_Wkv, dtype=np.float32),
        "fws": np.ascontiguousarray(L0_ffn_Ws, dtype=np.float32),
        "fwd": np.ascontiguousarray(L0_ffn_Wd, dtype=np.float32),
        "fwo": np.ascontiguousarray(L0_ffn_Wo, dtype=np.float32),
    }
    in_maps = [dict(w, x=np.ascontiguousarray(shards[i])) for i in range(NC)]
    res = run_bass_kernel_spmd(nc, in_maps, core_ids=list(range(NC)))
    out = np.concatenate([r["y"] for r in res.results], axis=0)
    return out.reshape(B, T, D).astype(np.float32)


# version 06 (final)
# speedup vs baseline: 1.2658x; 1.1025x over previous
import numpy as np
from contextlib import ExitStack
import concourse.bass as bass
import concourse.mybir as mybir
from concourse.bass_utils import run_bass_kernel_spmd

F32 = mybir.dt.float32
BF16 = mybir.dt.bfloat16
F16 = mybir.dt.float16
AF = mybir.ActivationFunctionType
ALU = mybir.AluOpType
AX = mybir.AxisListType

NC = 8
B = 16
BL = 2
T = 2048
NT = 16
D = 1024
KT = 8
H = 8
E = 5
DH = 64
IC = 256
NICH = T // IC
SC_F = DH ** -0.5
SC_A = D ** -0.5

STOP_AFTER = 3

_CACHE = {}


class Buf:
    __slots__ = ("w", "r")

    def __init__(self, edges=()):
        self.w = list(edges)
        self.r = []


class Sch:
    def __init__(self):
        self.streams = {k: [] for k in ("T", "V", "S", "G", "Y")}
        self.cnt = {k: 0 for k in ("t", "v", "s", "g", "dw", "dy")}
        self.esem = {"T": "t", "V": "v", "S": "s", "G": "g", "Y": None}

    def snap(self):
        return [(k, v) for k, v in self.cnt.items() if v > 0]

    def _waits(self, reads, writes):
        waits = {}
        for b in reads:
            for (s, v) in b.w:
                waits[s] = max(waits.get(s, 0), v)
        for b in writes:
            for (s, v) in b.w:
                waits[s] = max(waits.get(s, 0), v)
            for (s, v) in b.r:
                waits[s] = max(waits.get(s, 0), v)
        return sorted(waits.items())

    def op(self, eng, fn, reads=(), writes=(), dma=None):
        waits = self._waits(reads, writes)
        if dma is not None:
            self.cnt[dma] += 16
            edge = (dma, self.cnt[dma], 16)
        else:
            s = self.esem[eng]
            self.cnt[s] += 1
            edge = (s, self.cnt[s], 1)
        self.streams[eng].append((fn, waits, edge))
        for b in reads:
            b.r.append(edge[:2])
        for b in writes:
            b.w = [edge[:2]]
            b.r = []
        return edge

    def group(self, eng, fns, reads=(), writes=()):
        waits = self._waits(reads, writes)
        s = self.esem[eng]
        self.cnt[s] += 1
        edge = (s, self.cnt[s], 1)
        for i, fn in enumerate(fns):
            self.streams[eng].append(
                (fn, waits if i == 0 else [], edge if i == len(fns) - 1 else None))
        for b in reads:
            b.r.append(edge[:2])
        for b in writes:
            b.w = [edge[:2]]
            b.r = []
        return edge


def build_nc():
    nc = bass.Bass("TRN2", target_bir_lowering=False)

    x_in = nc.dram_tensor("x", [BL, T, D], F32, kind="ExternalInput").ap()
    y_out = nc.dram_tensor("y", [BL, T, D], F32, kind="ExternalOutput").ap()
    aWq = nc.dram_tensor("awq", [D, D], F32, kind="ExternalInput").ap()
    aWk = nc.dram_tensor("awk", [D, D], F32, kind="ExternalInput").ap()
    aWv = nc.dram_tensor("awv", [D, D], F32, kind="ExternalInput").ap()
    aWr = nc.dram_tensor("awr", [D, D], F32, kind="ExternalInput").ap()
    fWq = nc.dram_tensor("fwq", [D, H * E * DH], F32, kind="ExternalInput").ap()
    fWkv = nc.dram_tensor("fwkv", [D, 2 * H * E * DH], F32, kind="ExternalInput").ap()
    fWs = nc.dram_tensor("fws", [D, H * E], F32, kind="ExternalInput").ap()
    fWd = nc.dram_tensor("fwd", [D, H * E], F32, kind="ExternalInput").ap()
    fWo = nc.dram_tensor("fwo", [DH, E * D], F32, kind="ExternalInput").ap()
    qspill = nc.dram_tensor("qspill", [T, D], F16, kind="Internal").ap()

    x_t = x_in.rearrange("b (n p) d -> p (b n) d", p=128)
    y_t = y_out.rearrange("b (n p) d -> p (b n) d", p=128)
    qsp_t = qspill.rearrange("(n p) d -> p n d", p=128)

    ARENA_BYTES = 105472

    with ExitStack() as _es:
        en = _es.enter_context
        xsb = en(nc.sbuf_tensor("xsb", [128, BL * NT, D], F16))
        xnT = en(nc.sbuf_tensor("xnT", [128, KT, T], BF16))
        woT = en(nc.sbuf_tensor("woT", [128, 3, D], BF16))
        wsd = en(nc.sbuf_tensor("wsd", [128, KT, 2 * H * E], BF16))
        identb = en(nc.sbuf_tensor("identb", [128, 128], BF16))
        identf = en(nc.sbuf_tensor("identf", [1, 1], F32))
        onesb = en(nc.sbuf_tensor("onesb", [128, 1], BF16))
        ones1 = en(nc.sbuf_tensor("ones1", [1, 128], F32))
        epsv = en(nc.sbuf_tensor("epsv", [128, 1], F32))
        stats = en(nc.sbuf_tensor("stats", [128, NT, 8], F32))
        arena_t = en(nc.sbuf_tensor("arena", [128, ARENA_BYTES], mybir.dt.uint8))
        PA = en(nc.psum_tensor("PA", [128, 1024], F32))
        PB = en(nc.psum_tensor("PB", [128, 1024], F32))
        PP = en(nc.psum_tensor("PP", [128, 1024], F32))
        PT0 = en(nc.psum_tensor("PT0", [128, 1024], BF16))
        PT1 = en(nc.psum_tensor("PT1", [128, 1024], BF16))
        sem_t = en(nc.semaphore("t"))
        sem_v = en(nc.semaphore("v"))
        sem_s = en(nc.semaphore("s"))
        sem_g = en(nc.semaphore("g"))
        sem_dw = en(nc.semaphore("dw"))
        sem_dy = en(nc.semaphore("dy"))
        block = en(nc.Block())
        S = Sch()
        sems = {"t": sem_t, "v": sem_v, "s": sem_s, "g": sem_g,
                "dw": sem_dw, "dy": sem_dy}
        arena = arena_t.ap()

        def carve(off, shape, dtype):
            nel = 1
            for d_ in shape:
                nel *= d_
            nbytes = nel * mybir.dt.size(dtype)
            assert off % 64 == 0 and off + nbytes <= ARENA_BYTES, (off, nbytes)
            sl = arena[:, off:off + nbytes].bitcast(dtype)
            if len(shape) > 1:
                names = " ".join(f"a{i}" for i in range(len(shape)))
                kw = {f"a{i}": shape[i] for i in range(len(shape))}
                sl = sl.rearrange(f"p ({names}) -> p {names}", **kw)
            return sl

        PTT = (PT0, PT1)
        PX2 = (PA, PB)

        xb = [Buf() for _ in range(BL * NT)]
        xnTb = [Buf() for _ in range(NT)]
        woTb, wsdb = Buf(), Buf()
        constb = Buf()
        statsb = [[Buf() for _ in range(8)] for _ in range(NT)]
        pa = [Buf(), Buf()]
        pb = [Buf(), Buf()]
        pp = [Buf(), Buf()]
        pt = [Buf(), Buf()]
        qspb = [Buf() for _ in range(NT)]

        S.op("G", lambda g: g.memset(onesb[:], 1.0), writes=[constb])
        S.op("G", lambda g: g.memset(ones1[:], 1.0), writes=[constb])
        S.op("G", lambda g: g.memset(epsv[:], 1e-5), writes=[constb])
        S.op("G", lambda g: g.memset(identf[:], 1.0), writes=[constb])
        S.op("G", lambda g: g.memset(identb[:], 0.0), writes=[constb])
        S.op("G", lambda g: g.affine_select(
            out=identb[:], in_=identb[:], compare_op=ALU.not_equal, fill=1.0,
            base=0, pattern=[[-1, 128]], channel_multiplier=1), writes=[constb])

        for i in range(4):
            S.op("G", lambda g, i=i: g.dma_start(
                out=xsb[:, i * 8:(i + 1) * 8, :], in_=x_t[:, i * 8:(i + 1) * 8, :]),
                writes=[xb[j] for j in range(i * 8, (i + 1) * 8)], dma="dw")
        S.op("G", lambda g: g.dma_start(
            out=wsd[:, :, 0:H * E],
            in_=fWs.rearrange("(kt p) m -> p kt m", p=128)), writes=[wsdb], dma="dw")
        S.op("G", lambda g: g.dma_start(
            out=wsd[:, :, H * E:2 * H * E],
            in_=fWd.rearrange("(kt p) m -> p kt m", p=128)), writes=[wsdb], dma="dw")
        for e_ in range(E):
            S.op("G", lambda g, e_=e_: g.dma_start(
                out=woT[(e_ % 2) * 64:(e_ % 2) * 64 + 64, e_ // 2, :],
                in_=fWo[:, e_ * D:(e_ + 1) * D]), writes=[woTb], dma="dw")

        def emit_ln_stats(b):
            xn = carve(0, (4, D), BF16)
            sq = carve(99200, (2, D), BF16)
            snap = S.snap()
            st = {"xn": xn, "xnb": [Buf(snap) for _ in range(4)],
                  "sqb": [Buf(snap), Buf(snap)], "b": b}
            for n in range(NT):
                tt = b * NT + n
                S.op("V", lambda v, tt=tt, n=n: v.reduce_sum(
                    stats[:, n, 0:1], xsb[:, tt, :], axis=AX.X),
                    reads=[xb[tt]], writes=[statsb[n][0]])
                S.op("S", lambda s, tt=tt, n=n, sq=sq: s.activation(
                    out=sq[:, n % 2, :], in_=xsb[:, tt, :], func=AF.Square,
                    accum_out=stats[:, n, 1:2]),
                    reads=[xb[tt]], writes=[st["sqb"][n % 2], statsb[n][1]])
            for n in range(NT):
                S.op("V", lambda v, n=n: v.tensor_scalar_mul(
                    stats[:, n, 2:3], stats[:, n, 0:1], 1.0 / D),
                    reads=[statsb[n][0]], writes=[statsb[n][2]])
            for n in range(NT):
                S.op("V", lambda v, n=n: v.tensor_mul(
                    stats[:, n, 3:4], stats[:, n, 2:3], stats[:, n, 2:3]),
                    reads=[statsb[n][2]], writes=[statsb[n][3]])
            for n in range(NT):
                S.op("V", lambda v, n=n: v.scalar_tensor_tensor(
                    out=stats[:, n, 4:5], in0=stats[:, n, 1:2], scalar=1.0 / D,
                    in1=stats[:, n, 3:4], op0=ALU.mult, op1=ALU.subtract),
                    reads=[statsb[n][1], statsb[n][3]], writes=[statsb[n][4]])
            for n in range(NT):
                S.op("S", lambda s, n=n: s.activation(
                    out=stats[:, n, 5:6], in_=stats[:, n, 4:5], func=AF.Ln,
                    bias=epsv[:, :], scale=1.0),
                    reads=[statsb[n][4], constb], writes=[statsb[n][5]])
            for n in range(NT):
                S.op("S", lambda s, n=n: s.activation(
                    out=stats[:, n, 6:7], in_=stats[:, n, 5:6], func=AF.Exp,
                    scale=-0.5), reads=[statsb[n][5]], writes=[statsb[n][6]])
            for n in range(NT):
                S.op("V", lambda v, n=n: v.tensor_mul(
                    stats[:, n, 7:8], stats[:, n, 2:3], stats[:, n, 6:7]),
                    reads=[statsb[n][2], statsb[n][6]], writes=[statsb[n][7]])
            return st

        def emit_ln_apply(st):
            b = st["b"]
            xn = st["xn"]
            xnb = st["xnb"]
            for n in range(NT):
                tt = b * NT + n
                mrsb = stats[:, n, 7:8].broadcast_to([128, D])
                S.op("V", lambda v, tt=tt, n=n, mrsb=mrsb, xn=xn: v.scalar_tensor_tensor(
                    out=xn[:, n % 4, :], in0=xsb[:, tt, :], scalar=stats[:, n, 6:7],
                    in1=mrsb, op0=ALU.mult, op1=ALU.subtract),
                    reads=[xb[tt], statsb[n][6], statsb[n][7]], writes=[xnb[n % 4]])
                if n >= 3:
                    ln_transpose(st, n - 3)
            for n in range(NT - 3, NT):
                ln_transpose(st, n)

        def ln_transpose(st, n):
            xn = st["xn"]
            xnb = st["xnb"]
            for half in range(2):
                hv = (n * 2 + half) % 2
                S.group("T", [
                    (lambda te, kt=kt, half=half, n=n, hv=hv, xn=xn: te.transpose(
                        PTT[hv][:, kt * 128:(kt + 1) * 128],
                        xn[:, n % 4, (half * 4 + kt) * 128:(half * 4 + kt + 1) * 128],
                        identb[:]))
                    for kt in range(4)],
                    reads=[xnb[n % 4], constb], writes=[pt[hv]])
                S.op("S", lambda s, half=half, n=n, hv=hv: s.activation(
                    out=xnT[:, half * 4:(half + 1) * 4, n * 128:(n + 1) * 128],
                    in_=PTT[hv][:, 0:512].rearrange("p (a c) -> p a c", c=128),
                    func=AF.Copy),
                    reads=[pt[hv]], writes=[xnTb[n]])

        def emit_ln(b):
            emit_ln_apply(emit_ln_stats(b))

        def emit_ffn(b, call_idx, ln_st0=None, ln_next=None):
            snap0 = S.snap()
            if ln_st0 is None:
                emit_ln(b)
            ez = carve(8192, (NT, 2 * H * E), BF16)
            wrk = carve(10752, (5, 256), F32)
            ssd = carve(15872, (NT, 2 * H * E), F32)
            med = carve(20992, (NT, 2 * H), F32)
            msk = carve(22016, (NT, 2 * H * E), F32)
            gated = carve(27136, (NT, 2 * H * E), BF16)
            whead = carve(29696, (KT, 3 * E * DH), BF16)
            qkT = carve(45056, (2, 2, T), BF16)
            eT = carve(61440, (NT, 512), BF16)
            vext = carve(77824, (2, 2, NT, 66), BF16)
            gacc = carve(86272, (NT, E * DH), BF16)
            otb = carve(96512, (2, 512), BF16)
            onrm = carve(98560, (2, 2, DH), BF16)
            orcp = carve(99072, (2, 2, 1), F32)
            snap = S.snap()
            ezb, wrkb, ssdb = Buf(snap), Buf(snap), Buf(snap)
            medb_, mskb, gatedb = Buf(snap), Buf(snap), Buf(snap)
            wheadb = Buf(snap0)
            qkTb = [Buf(snap), Buf(snap)]
            eTb = [Buf(snap) for _ in range(NT)]
            vextb = [Buf(snap), Buf(snap)]
            gaccb = [[Buf(snap) for _ in range(E)] for _ in range(NT)]
            otbb = [Buf(snap), Buf(snap)]
            onrmb = {(hp, it): Buf(snap) for hp in range(2) for it in range(2)}
            orcpb = {(hp, it): Buf(snap) for hp in range(2) for it in range(2)}

            def dma_whead(h):
                c0 = h * E * DH
                S.op("G", lambda g, c0=c0: g.dma_start(
                    out=whead[:, :, 0:320],
                    in_=fWq[:, c0:c0 + 320].rearrange("(kt p) m -> p kt m", p=128)),
                    writes=[wheadb], dma="dw")
                S.op("G", lambda g, c0=c0: g.dma_start(
                    out=whead[:, :, 320:640],
                    in_=fWkv[:, c0:c0 + 320].rearrange("(kt p) m -> p kt m", p=128)),
                    writes=[wheadb], dma="dw")
                S.op("G", lambda g, c0=c0: g.dma_start(
                    out=whead[:, :, 640:960],
                    in_=fWkv[:, 2560 + c0:2560 + c0 + 320].rearrange(
                        "(kt p) m -> p kt m", p=128)),
                    writes=[wheadb], dma="dw")

            dma_whead(0)
            for n in range(NT):
                ppx = pp[n % 2]
                S.group("T", [
                    (lambda te, kt=kt, n=n: te.matmul(
                        PP[:, (n % 2) * 512:(n % 2) * 512 + 2 * H * E],
                        xnT[:, kt, n * 128:(n + 1) * 128], wsd[:, kt, :],
                        start=(kt == 0), stop=(kt == KT - 1)))
                    for kt in range(KT)],
                    reads=[xnTb[n], wsdb], writes=[ppx])
                S.op("S", lambda s, n=n: s.activation(
                    out=ez[:, n, :], in_=PP[:, (n % 2) * 512:(n % 2) * 512 + 2 * H * E],
                    func=AF.Exp, scale=-1.0), reads=[ppx], writes=[ezb])
            ezf = ez.rearrange("p a c -> p (a c)")
            ssdf = ssd.rearrange("p a c -> p (a c)")
            wrkf = wrk.rearrange("p a c -> p (a c)")
            S.op("V", lambda v: v.tensor_scalar_add(wrkf, ezf, 1.0),
                 reads=[ezb], writes=[wrkb])
            S.op("V", lambda v: v.reciprocal(ssdf, wrkf),
                 reads=[wrkb], writes=[ssdb])
            sv = ssd.rearrange("p t (gh e) -> p t gh e", e=E)
            slc = [sv[:, :, :, i] for i in range(E)]
            a_ = [wrk[:, i, :].rearrange("p (t gh) -> p t gh", t=NT) for i in range(5)]
            m5 = [Buf(snap) for _ in range(5)]
            medv = med
            S.op("V", lambda v: v.tensor_tensor(a_[0], slc[0], slc[1], ALU.min),
                 reads=[ssdb], writes=[m5[0]])
            S.op("V", lambda v: v.tensor_tensor(a_[1], slc[0], slc[1], ALU.max),
                 reads=[ssdb], writes=[m5[1]])
            S.op("V", lambda v: v.tensor_tensor(a_[2], slc[2], slc[3], ALU.min),
                 reads=[ssdb], writes=[m5[2]])
            S.op("V", lambda v: v.tensor_tensor(a_[3], slc[2], slc[3], ALU.max),
                 reads=[ssdb], writes=[m5[3]])
            S.op("V", lambda v: v.tensor_tensor(a_[4], a_[0], a_[2], ALU.max),
                 reads=[m5[0], m5[2]], writes=[m5[4]])
            S.op("V", lambda v: v.tensor_tensor(a_[0], a_[1], a_[3], ALU.min),
                 reads=[m5[1], m5[3]], writes=[m5[0]])
            S.op("V", lambda v: v.tensor_tensor(a_[1], a_[4], a_[0], ALU.min),
                 reads=[m5[4], m5[0]], writes=[m5[1]])
            S.op("V", lambda v: v.tensor_tensor(a_[2], a_[4], a_[0], ALU.max),
                 reads=[m5[4], m5[0]], writes=[m5[2]])
            S.op("V", lambda v: v.tensor_tensor(a_[3], slc[4], a_[2], ALU.min),
                 reads=[ssdb, m5[2]], writes=[m5[3]])
            S.op("V", lambda v: v.tensor_tensor(medv, a_[1], a_[3], ALU.max),
                 reads=[m5[1], m5[3]], writes=[medb_])
            medbx = med.unsqueeze(3).broadcast_to([128, NT, 2 * H, E])
            mskv = msk.rearrange("p t (gh e) -> p t gh e", e=E)
            S.op("V", lambda v: v.tensor_tensor(mskv, sv, medbx, ALU.is_ge),
                 reads=[ssdb, medb_], writes=[mskb])
            S.op("V", lambda v: v.tensor_tensor(
                gated.rearrange("p a c -> p (a c)"), ssdf,
                msk.rearrange("p a c -> p (a c)"), ALU.mult),
                reads=[ssdb, mskb], writes=[gatedb])
            S.op("V", lambda v: v.memset(gacc.rearrange("p a c -> p (a c)"), 0.0),
                 writes=[gaccb[n][e] for n in range(NT) for e in range(E)])
            S.op("V", lambda v: v.memset(vext[:, :, :, :, 64:66], 0.0),
                 writes=[vextb[0], vextb[1]])
            S.op("V", lambda v: v.memset(vext[:, :, :, :, 64:65], 1.0),
                 writes=[vextb[0], vextb[1]])

            qg = carve(10752, (3, 3 * E * DH), BF16)
            qka = carve(0, (NT, 256), BF16)
            snap2 = S.snap()
            qgb = [Buf(snap2), Buf(snap2), Buf(snap2)]
            qkab = [Buf(snap2) for _ in range(NT)]

            prep_pend = []

            def prep_mults(h, n):
                sl = n % 3
                for c, (lo, hi) in enumerate(((0, 512), (512, 960))):
                    ppx = pp[c]
                    S.group("T", [
                        (lambda te, kt=kt, n=n, lo=lo, hi=hi, c=c: te.matmul(
                            PP[:, c * 512:c * 512 + (hi - lo)],
                            xnT[:, kt, n * 128:(n + 1) * 128],
                            whead[:, kt, lo:hi],
                            start=(kt == 0), stop=(kt == KT - 1)))
                        for kt in range(KT)],
                        reads=[xnTb[n], wheadb], writes=[ppx])
                    for s0, s1, gsel in ((0, 320, 1), (320, 640, 0), (640, 960, 0)):
                        a0_, b0_ = max(lo, s0), min(hi, s1)
                        if a0_ >= b0_:
                            continue
                        ne = (b0_ - a0_) // DH
                        e0 = (a0_ - s0) // DH
                        gbase = (H * E if gsel else 0) + h * E + e0
                        gapb = gated[:, n, gbase:gbase + ne].unsqueeze(2) \
                            .broadcast_to([128, ne, DH])
                        pin = PP[:, c * 512 + (a0_ - lo):c * 512 + (b0_ - lo)] \
                            .rearrange("p (e d) -> p e d", d=DH)
                        qout = qg[:, sl, a0_:b0_].rearrange("p (e d) -> p e d", d=DH)
                        S.op("V", lambda v, pin=pin, gapb=gapb, qout=qout:
                             v.tensor_tensor(qout, pin, gapb, ALU.mult),
                             reads=[ppx, gatedb], writes=[qgb[sl]])

            def prep_tail(h, n):
                sl = n % 3
                p_ = h // 2
                pbuf = p_ % 2
                hp = h % 2
                S.op("V", lambda v, n=n, hp=hp, sl=sl: v.tensor_reduce(
                    qka[:, n, hp * 64:hp * 64 + 64],
                    qg[:, sl, 0:320].rearrange("p (e d) -> p d e", d=DH),
                    axis=AX.X, op=ALU.add),
                    reads=[qgb[sl]], writes=[qkab[n]])
                S.op("V", lambda v, n=n, hp=hp, sl=sl: v.tensor_reduce(
                    qka[:, n, 128 + hp * 64:128 + hp * 64 + 64],
                    qg[:, sl, 320:640].rearrange("p (e d) -> p d e", d=DH),
                    axis=AX.X, op=ALU.add),
                    reads=[qgb[sl]], writes=[qkab[n]])
                S.op("V", lambda v, n=n, hp=hp, pbuf=pbuf, sl=sl: v.tensor_reduce(
                    vext[:, pbuf, hp, n, 0:64],
                    qg[:, sl, 640:960].rearrange("p (e d) -> p d e", d=DH),
                    axis=AX.X, op=ALU.add),
                    reads=[qgb[sl]], writes=[vextb[pbuf]])


            def emit_qk_transposes(p_):
                pbuf = p_ % 2
                for n in range(NT):
                    for qk in range(2):
                        hv = (n * 2 + qk) % 2
                        S.op("T", lambda te, qk=qk, n=n, hv=hv: te.transpose(
                            PTT[hv][:, 0:128], qka[:, n, qk * 128:(qk + 1) * 128],
                            identb[:]),
                            reads=[qkab[n], constb], writes=[pt[hv]])
                        S.op("S", lambda s, qk=qk, n=n, pbuf=pbuf, hv=hv:
                             s.activation(
                                 out=qkT[:, pbuf, qk, n * 128:(n + 1) * 128],
                                 in_=PTT[hv][:, 0:128], func=AF.Copy),
                             reads=[pt[hv]], writes=[qkTb[pbuf]])

            def emit_prep_tiles(h, tiles):
                if prep_pend and prep_pend[0][0] != h:
                    while prep_pend:
                        prep_tail(*prep_pend.pop(0))
                for n in tiles:
                    prep_mults(h, n)
                    prep_pend.append((h, n))
                    if len(prep_pend) > 2:
                        prep_tail(*prep_pend.pop(0))

            def emit_o(p_, hp, ic, j):
                pbuf = p_ % 2
                S.op("T", lambda te, j=j, hp=hp, ic=ic, pbuf=pbuf: te.matmul(
                    PB[0:65, (ic % 2) * 512:(ic % 2) * 512 + 512],
                    vext[:, pbuf, hp, j, 0:65], eT[:, j, :],
                    start=(j == 0), stop=(j == NT - 1), skip_group_check=True),
                    reads=[vextb[pbuf], eTb[j]], writes=[pb[ic % 2]])

            def emit_attn_chunk(p_, cc):
                hp, ic = cc // 4, cc % 4
                pbuf = p_ % 2
                i0 = ic * 512
                for j in range(NT):
                    half = j % 2
                    pax = pa[half]
                    S.op("T", lambda te, j=j, hp=hp, half=half, i0=i0, pbuf=pbuf:
                         te.matmul(
                             PA[:, half * 512:(half + 1) * 512],
                             qkT[64 * hp:64 * (hp + 1), pbuf, 1, j * 128:(j + 1) * 128],
                             qkT[64 * hp:64 * (hp + 1), pbuf, 0, i0:i0 + 512],
                             start=True, stop=True, tile_position=(64 * hp, 0)),
                         reads=[qkTb[pbuf]], writes=[pax])
                    S.op("S", lambda s, j=j, half=half: s.activation(
                        out=eT[:, j, :],
                        in_=PA[:, half * 512:(half + 1) * 512], func=AF.Exp,
                        scale=SC_F), reads=[pax], writes=[eTb[j]])
                    if j == 2 and cc > 0:
                        emit_evac_copy(p_, cc - 1)
                    if j == 6 and cc > 0:
                        emit_evac_rest(p_, cc - 1)
                    if j >= 2:
                        emit_o(p_, hp, ic, j - 2)
                emit_o(p_, hp, ic, NT - 2)
                emit_o(p_, hp, ic, NT - 1)
                if cc == 7:
                    emit_evac_copy(p_, cc)
                    emit_evac_rest(p_, cc)

            def emit_evac_copy(p_, cc):
                hp, ic = cc // 4, cc % 4
                slot = cc % 2
                S.op("S", lambda s, ic=ic, slot=slot: s.activation(
                    out=otb[0:65, slot, :],
                    in_=PB[0:65, (ic % 2) * 512:(ic % 2) * 512 + 512],
                    func=AF.Copy),
                    reads=[pb[ic % 2]], writes=[otbb[slot]])

            def emit_evac_rest(p_, cc):
                hp, ic = cc // 4, cc % 4
                slot = cc % 2
                h = p_ * 2 + hp
                for it in range(4):
                    n = ic * 4 + it
                    hv = it % 2
                    ons = (it // 2, it % 2)
                    S.op("T", lambda te, it=it, slot=slot, hv=hv: te.transpose(
                        PTT[hv][:, 0:65],
                        otb[0:65, slot, it * 128:it * 128 + 128],
                        identb[0:65, 0:65]),
                        reads=[otbb[slot], constb], writes=[pt[hv]])
                    S.op("V", lambda v, ons=ons, hv=hv: v.reciprocal(
                        orcp[:, ons[0], ons[1]:ons[1] + 1], PTT[hv][:, 64:65]),
                        reads=[pt[hv]], writes=[orcpb[ons]])
                    S.op("V", lambda v, ons=ons, hv=hv: v.tensor_scalar_mul(
                        onrm[:, ons[0], ons[1], :], PTT[hv][:, 0:64],
                        orcp[:, ons[0], ons[1]:ons[1] + 1]),
                        reads=[pt[hv], orcpb[ons]], writes=[onrmb[ons]])
                    for e in range(E):
                        S.op("V", lambda v, n=n, e=e, h=h, ons=ons:
                             v.scalar_tensor_tensor(
                                 out=gacc[:, n, e * DH:(e + 1) * DH],
                                 in0=onrm[:, ons[0], ons[1], :],
                                 scalar=msk[:, n, H * E + h * E + e:H * E + h * E + e + 1],
                                 in1=gacc[:, n, e * DH:(e + 1) * DH],
                                 op0=ALU.mult, op1=ALU.add),
                             reads=[onrmb[ons], mskb, gaccb[n][e]],
                             writes=[gaccb[n][e]])

            emit_prep_tiles(0, range(NT))
            dma_whead(1)
            emit_prep_tiles(1, range(NT))
            for p_ in range(4):
                while prep_pend:
                    prep_tail(*prep_pend.pop(0))
                emit_qk_transposes(p_)
                for cc in range(8):
                    emit_attn_chunk(p_, cc)
                    if p_ < 3:
                        h2 = 2 * (p_ + 1)
                        if cc == 0:
                            dma_whead(h2)
                        if cc < 4:
                            emit_prep_tiles(h2, range(cc * 4, cc * 4 + 4))
                        if cc == 3:
                            dma_whead(h2 + 1)
                        if cc >= 4:
                            emit_prep_tiles(h2 + 1, range((cc - 4) * 4, (cc - 4) * 4 + 4))

            ln_st = None
            if ln_next is not None:
                ln_st = emit_ln_stats(ln_next)

            out_sb = carve(8192, (2, D), F32)
            gt = carve(16448, (2, 3, 128), BF16)
            snap3 = S.snap()
            outb = [Buf(snap3), Buf(snap3)]
            gtb = [Buf(snap3), Buf(snap3)]
            last = (call_idx == 2)
            for n in range(NT):
                slot = n % 2
                for kt in range(3):
                    w_ = 128 if kt < 2 else 64
                    hv = kt % 2
                    S.op("T", lambda te, n=n, kt=kt, w_=w_, hv=hv: te.transpose(
                        PTT[hv][0:w_, 0:128], gacc[:, n, kt * 128:kt * 128 + w_],
                        identb[:]),
                        reads=[gaccb[n][ee] for ee in range(E)] + [constb],
                        writes=[pt[hv]])
                    S.op("S", lambda s, n=n, kt=kt, w_=w_, slot=slot, hv=hv:
                         s.activation(out=gt[0:w_, slot, kt, :],
                                      in_=PTT[hv][0:w_, 0:128], func=AF.Copy),
                         reads=[pt[hv]], writes=[gtb[slot]])
                pax = (pa, pb)[n % 2]
                PX = PX2[n % 2]
                for c in range(2):
                    S.group("T", [
                        (lambda te, kt=kt, c=c, slot=slot, PX=PX: te.matmul(
                            PX[:, c * 512:(c + 1) * 512],
                            gt[0:(128 if kt < 2 else 64), slot, kt, :],
                            woT[0:(128 if kt < 2 else 64), kt, c * 512:(c + 1) * 512],
                            start=(kt == 0), stop=(kt == 2),
                            tile_position=((0, 0) if kt == 2 else None)))
                        for kt in range(3)],
                        reads=[gtb[slot], woTb], writes=[pax[c]])
                tt = b * NT + n
                if last:
                    S.op("V", lambda v, tt=tt, slot=slot, PX=PX: v.scalar_tensor_tensor(
                        out=out_sb[:, slot, :], in0=PX[:, :], scalar=0.5,
                        in1=xsb[:, tt, :], op0=ALU.mult, op1=ALU.add),
                        reads=[pax[0], pax[1], xb[tt]], writes=[outb[slot]])
                    S.op("Y", lambda y, tt=tt, slot=slot: y.dma_start(
                        out=y_t[:, tt, :], in_=out_sb[:, slot, :]),
                        reads=[outb[slot]], dma="dy")
                else:
                    S.op("V", lambda v, tt=tt, PX=PX: v.scalar_tensor_tensor(
                        out=xsb[:, tt, :], in0=PX[:, :], scalar=0.5,
                        in1=xsb[:, tt, :], op0=ALU.mult, op1=ALU.add),
                        reads=[pax[0], pax[1], xb[tt]], writes=[xb[tt]])
            if ln_st is not None:
                emit_ln_apply(ln_st)
            return ln_st

        def emit_attn_sub(b, ln_st0=None, ln_next=None):
            snap0 = S.snap()
            if ln_st0 is None:
                emit_ln(b)
            aw = carve(8192, (2, KT, D), BF16)
            uT = carve(40960, (KT, T), BF16)
            qa = carve(73728, (2, D), BF16)
            psb = carve(77824, (2, D), F32)
            pexp = carve(86016, (2, D), BF16)
            q16 = carve(90112, (2, D), F16)
            gqb_s = carve(94208, (D,), F32)
            gq_s = carve(98304, (D,), F32)
            gkT_s = carve(102400, (KT,), F32)
            rcp = carve(102464, (2, 1), F32)
            snap = S.snap()
            awb = [Buf(snap0), Buf(snap0)]
            uTb = Buf(snap)
            qab = [Buf(snap), Buf(snap)]
            psbb = [Buf(snap), Buf(snap)]
            pexpb = [Buf(snap), Buf(snap)]
            q16b = [Buf(snap), Buf(snap)]
            gqbb = Buf(snap)
            gkTb = Buf(snap)
            gqsb = Buf(snap)
            rcpb = [Buf(snap), Buf(snap)]

            def dma_aw(w, slot):
                S.op("G", lambda g, w=w, slot=slot: g.dma_start(
                    out=aw[:, slot, :, :],
                    in_=w.rearrange("(kt p) m -> p kt m", p=128)),
                    writes=[awb[slot]], dma="dw")

            dma_aw(aWq, 0)
            dma_aw(aWk, 1)

            for n in range(NT):
                pax = (pa, pb)[n % 2]
                PX = PX2[n % 2]
                slot = n % 2
                for c in range(2):
                    S.group("T", [
                        (lambda te, kt=kt, n=n, c=c, PX=PX: te.matmul(
                            PX[:, c * 512:(c + 1) * 512],
                            xnT[:, kt, n * 128:(n + 1) * 128],
                            aw[:, 0, kt, c * 512:(c + 1) * 512],
                            start=(kt == 0), stop=(kt == KT - 1)))
                        for kt in range(KT)],
                        reads=[xnTb[n], awb[0]], writes=[pax[c]])
                S.op("S", lambda s, n=n, slot=slot, PX=PX: s.activation(
                    out=pexp[:, slot, :], in_=PX[:, :], func=AF.Exp, scale=SC_A,
                    accum_out=stats[:, n, 0:1]),
                    reads=[pax[0], pax[1]], writes=[pexpb[slot], statsb[n][0]])
                S.op("S", lambda s, n=n, slot=slot, PX=PX: s.activation(
                    out=q16[:, slot, :], in_=PX[:, :], func=AF.Copy),
                    reads=[pax[0], pax[1]], writes=[q16b[slot]])
                S.op("Y", lambda y, n=n, slot=slot: y.dma_start(
                    out=qsp_t[:, n, :], in_=q16[:, slot, :]),
                    reads=[q16b[slot]], writes=[qspb[n]], dma="dy")
                S.op("V", lambda v, n=n, slot=slot: v.reciprocal(
                    rcp[:, slot, :], stats[:, n, 0:1]),
                    reads=[statsb[n][0]], writes=[rcpb[slot]])
                S.op("V", lambda v, n=n, slot=slot, PX=PX: v.scalar_tensor_tensor(
                    out=qa[:, slot, :], in0=pexp[:, slot, :], scalar=rcp[:, slot, 0:1],
                    in1=PX[:, :], op0=ALU.mult, op1=ALU.mult),
                    reads=[pexpb[slot], rcpb[slot], pax[0], pax[1]],
                    writes=[qab[slot]])
                for m_ in ([n - 1] if n > 0 else []) + ([n] if n == NT - 1 else []):
                    msl = m_ % 2
                    for c in range(2):
                        S.op("T", lambda te, m_=m_, c=c, msl=msl: te.matmul(
                            PP[0:1, c * 512:(c + 1) * 512], onesb[:, :],
                            qa[:, msl, c * 512:(c + 1) * 512],
                            start=(m_ == 0), stop=(m_ == NT - 1),
                            skip_group_check=True),
                            reads=[qab[msl], constb], writes=[pp[c]])
            S.op("S", lambda s: s.activation(out=gq_s[0:1, :], in_=PP[0:1, :],
                                             func=AF.Copy),
                 reads=[pp[0], pp[1]], writes=[gqsb])
            for c in range(2):
                S.op("T", lambda te, c=c: te.matmul(
                    PA[:, c * 512:(c + 1) * 512], ones1[:, :],
                    gq_s[0:1, c * 512:(c + 1) * 512], start=True, stop=True),
                    reads=[gqsb, constb], writes=[pa[c]])
            S.op("V", lambda v: v.tensor_copy(gqb_s[:, :], PA[:, :]),
                 reads=[pa[0], pa[1]], writes=[gqbb])
            dma_aw(aWv, 0)

            for n in range(NT):
                pax = (pa, pb)[n % 2]
                PX = PX2[n % 2]
                slot = n % 2
                for c in range(2):
                    S.group("T", [
                        (lambda te, kt=kt, n=n, c=c, PX=PX: te.matmul(
                            PX[:, c * 512:(c + 1) * 512],
                            xnT[:, kt, n * 128:(n + 1) * 128],
                            aw[:, 1, kt, c * 512:(c + 1) * 512],
                            start=(kt == 0), stop=(kt == KT - 1)))
                        for kt in range(KT)],
                        reads=[xnTb[n], awb[1]], writes=[pax[c]])
                S.op("V", lambda v, slot=slot, PX=PX: v.tensor_tensor(
                    psb[:, slot, :], PX[:, :], gqb_s[:, :], ALU.mult),
                    reads=[pax[0], pax[1], gqbb], writes=[psbb[slot]])
                S.op("S", lambda s, n=n, slot=slot: s.activation(
                    out=pexp[:, slot, :], in_=psb[:, slot, :], func=AF.Exp,
                    scale=SC_A, accum_out=stats[:, n, 1:2]),
                    reads=[psbb[slot]], writes=[pexpb[slot], statsb[n][1]])
                S.op("V", lambda v, n=n, slot=slot: v.reciprocal(
                    rcp[:, slot, :], stats[:, n, 1:2]),
                    reads=[statsb[n][1]], writes=[rcpb[slot]])
                S.op("V", lambda v, slot=slot: v.scalar_tensor_tensor(
                    out=qa[:, slot, :], in0=pexp[:, slot, :], scalar=rcp[:, slot, 0:1],
                    in1=psb[:, slot, :], op0=ALU.mult, op1=ALU.mult),
                    reads=[pexpb[slot], rcpb[slot], psbb[slot]], writes=[qab[slot]])
                for m_ in ([n - 1] if n > 0 else []) + ([n] if n == NT - 1 else []):
                    msl = m_ % 2
                    for c in range(2):
                        S.op("T", lambda te, m_=m_, c=c, msl=msl: te.matmul(
                            PP[0:1, c * 512:(c + 1) * 512], onesb[:, :],
                            qa[:, msl, c * 512:(c + 1) * 512],
                            start=(m_ == 0), stop=(m_ == NT - 1),
                            skip_group_check=True),
                            reads=[qab[msl], constb], writes=[pp[c]])
            S.op("S", lambda s: s.activation(out=gq_s[0:1, :], in_=PP[0:1, :],
                                             func=AF.Copy),
                 reads=[pp[0], pp[1]], writes=[gqsb])
            for kt in range(KT):
                S.op("T", lambda te, kt=kt: te.transpose(
                    PA[:, kt:kt + 1], gq_s[0:1, kt * 128:(kt + 1) * 128],
                    identf[:, :]), reads=[gqsb, constb], writes=[pa[0]])
            S.op("V", lambda v: v.tensor_copy(gkT_s[:, :], PA[:, 0:KT]),
                 reads=[pa[0]], writes=[gkTb])
            dma_aw(aWr, 1)

            for ft in range(KT):
                for ch in range(4):
                    idx = ft * 4 + ch
                    bank = ((pa, pb)[(idx // 2) % 2])[idx % 2]
                    PX = PX2[(idx // 2) % 2]
                    S.group("T", [
                        (lambda te, kt=kt, ft=ft, ch=ch, idx=idx, PX=PX: te.matmul(
                            PX[:, (idx % 2) * 512:(idx % 2) * 512 + 512],
                            aw[:, 0, kt, ft * 128:(ft + 1) * 128],
                            xnT[:, kt, ch * 512:(ch + 1) * 512],
                            start=(kt == 0), stop=(kt == KT - 1)))
                        for kt in range(KT)],
                        reads=[awb[0]] + [xnTb[4 * ch + q] for q in range(4)],
                        writes=[bank])
                    S.op("V", lambda v, ft=ft, ch=ch, idx=idx, PX=PX: v.tensor_scalar_mul(
                        uT[:, ft, ch * 512:(ch + 1) * 512],
                        PX[:, (idx % 2) * 512:(idx % 2) * 512 + 512],
                        gkT_s[:, ft:ft + 1]),
                        reads=[bank, gkTb], writes=[uTb])

            ln_st = None
            if ln_next is not None:
                ln_st = emit_ln_stats(ln_next)

            for n in range(NT):
                slot = n % 2
                S.op("Y", lambda y, n=n, slot=slot: y.dma_start(
                    out=q16[:, slot, :], in_=qsp_t[:, n, :]),
                    reads=[qspb[n]], writes=[q16b[slot]], dma="dy")
                pax = (pa, pb)[n % 2]
                PX = PX2[n % 2]
                for c in range(2):
                    S.group("T", [
                        (lambda te, kt=kt, n=n, c=c, PX=PX: te.matmul(
                            PX[:, c * 512:(c + 1) * 512],
                            uT[:, kt, n * 128:(n + 1) * 128],
                            aw[:, 1, kt, c * 512:(c + 1) * 512],
                            start=(kt == 0), stop=(kt == KT - 1)))
                        for kt in range(KT)],
                        reads=[uTb, awb[1]], writes=[pax[c]])
                tt = b * NT + n
                S.op("V", lambda v, slot=slot, PX=PX: v.scalar_tensor_tensor(
                    out=psb[:, slot, :], in0=PX[:, :], scalar=1.0,
                    in1=q16[:, slot, :], op0=ALU.mult, op1=ALU.add),
                    reads=[pax[0], pax[1], q16b[slot]], writes=[psbb[slot]])
                S.op("V", lambda v, tt=tt, slot=slot: v.tensor_tensor(
                    xsb[:, tt, :], psb[:, slot, :], xsb[:, tt, :], ALU.add),
                    reads=[psbb[slot], xb[tt]], writes=[xb[tt]])
            if ln_st is not None:
                emit_ln_apply(ln_st)
            return ln_st

        phases = [(c, b) for c in range(min(3, STOP_AFTER)) for b in range(BL)]
        carry = None
        for idx, (c, b) in enumerate(phases):
            nxt = phases[idx + 1][1] if idx + 1 < len(phases) else None
            if c == 1:
                carry = emit_attn_sub(b, ln_st0=carry, ln_next=nxt)
            else:
                carry = emit_ffn(b, c, ln_st0=carry, ln_next=nxt)

        if STOP_AFTER < 3:
            for i in range(4):
                S.op("G", lambda g, i=i: g.dma_start(
                    out=y_t[:, i * 8:(i + 1) * 8, :],
                    in_=xsb[:, i * 8:(i + 1) * 8, :]),
                    reads=[xb[j] for j in range(i * 8, (i + 1) * 8)], dma="dw")

        n_ops = {k: len(v) for k, v in S.streams.items()}
        n_ops["sem"] = dict(S.cnt)

        def replay(eng_key):
            def run(e):
                seen = {}
                for (fn, waits, edge) in S.streams[eng_key]:
                    for (sname, val) in waits:
                        if seen.get(sname, 0) < val:
                            e.wait_ge(sems[sname], val)
                            seen[sname] = val
                    ins = fn(e)
                    if edge is not None:
                        ins.then_inc(sems[edge[0]], edge[2])
                if eng_key == "Y" and S.cnt["dy"] > 0:
                    e.wait_ge(sems["dy"], S.cnt["dy"])
                if eng_key == "G" and STOP_AFTER < 3:
                    e.wait_ge(sems["dw"], S.cnt["dw"])
            return run

        with nc.allow_low_precision(reason="bf16/f16 stores of fp32-computed values"):
            block.tensor(replay("T"))
            block.vector(replay("V"))
            block.scalar(replay("S"))
            block.gpsimd(replay("G"))
            block.sync(replay("Y"))

    nc._n_ops = n_ops
    return nc


def kernel(L0_attn_x, L0_attn_Wq, L0_attn_Wk, L0_attn_Wv, L0_attn_Wr,
           L0_ffn_Wq, L0_ffn_Wkv, L0_ffn_Ws, L0_ffn_Wd, L0_ffn_Wo):
    if "nc" not in _CACHE:
        _CACHE["nc"] = build_nc()
    nc = _CACHE["nc"]
    x = np.ascontiguousarray(L0_attn_x, dtype=np.float32)
    shards = x.reshape(NC, BL, T, D)
    w = {
        "awq": np.ascontiguousarray(L0_attn_Wq, dtype=np.float32),
        "awk": np.ascontiguousarray(L0_attn_Wk, dtype=np.float32),
        "awv": np.ascontiguousarray(L0_attn_Wv, dtype=np.float32),
        "awr": np.ascontiguousarray(L0_attn_Wr, dtype=np.float32),
        "fwq": np.ascontiguousarray(L0_ffn_Wq, dtype=np.float32),
        "fwkv": np.ascontiguousarray(L0_ffn_Wkv, dtype=np.float32),
        "fws": np.ascontiguousarray(L0_ffn_Ws, dtype=np.float32),
        "fwd": np.ascontiguousarray(L0_ffn_Wd, dtype=np.float32),
        "fwo": np.ascontiguousarray(L0_ffn_Wo, dtype=np.float32),
    }
    in_maps = [dict(w, x=np.ascontiguousarray(shards[i])) for i in range(NC)]
    res = run_bass_kernel_spmd(nc, in_maps, core_ids=list(range(NC)))
    out = np.concatenate([r["y"] for r in res.results], axis=0)
    return out.reshape(B, T, D).astype(np.float32)
